# version 12
# baseline (speedup 1.0000x reference)
import sys
import os
sys.path.insert(0, "/opt/trn_rl_repo")
os.environ.setdefault("TRN_ALLOW_ICI_ALL_TO_ALL", "1")
import contextlib
import math
import numpy as np
import ml_dtypes
import concourse.bass as bass
import concourse.mybir as mybir
from concourse.bass_utils import run_bass_kernel_spmd

F32 = mybir.dt.float32
BF16 = mybir.dt.bfloat16
AF = mybir.ActivationFunctionType
ALU = mybir.AluOpType
AX = mybir.AxisListType
BF = ml_dtypes.bfloat16

D = 4096
S = 4096
B = 2
NTOK = 1024
TT = 8
FC = 32
CH = 512
LN_EPS = 1e-5
RMS_EPS = 1.1920929e-07
NCORES = 8

ENG_ORDER = ("sp", "pe", "dve", "act", "pool")
SECT = {"sp": "sync", "pe": "tensor", "dve": "vector", "act": "scalar", "pool": "gpsimd"}


def core_chunks(c):
    return [(0, c), (1, 7 - c)]


def core_rows_slices(c):
    return [(b * S + j * CH, CH) for (b, j) in core_chunks(c)]


def shard_rows(full, c):
    parts = [full[s : s + n] for (s, n) in core_rows_slices(c)]
    return np.ascontiguousarray(np.concatenate(parts, axis=0))


def unshard_rows(shards, out):
    for c, sh in enumerate(shards):
        o = 0
        for (s, n) in core_rows_slices(c):
            out[s : s + n] = sh[o : o + n]
            o += n
    return out


class Prog:

    def __init__(self, nc):
        self.nc = nc
        self.engs = {"pe": nc.tensor, "dve": nc.vector, "act": nc.scalar,
                     "pool": nc.gpsimd, "sp": nc.sync}
        self.sem = {}
        for s in ["pe", "dve", "act", "pool", "sp", "dsp", "dact", "dpool", "ddve"]:
            self.sem[s] = nc.alloc_semaphore("sem_" + s)
        self.cnt = {k: 0 for k in self.sem}
        self.plan = {e: [] for e in self.engs}
        self.wm = {e: {k: 0 for k in self.sem} for e in self.engs}
        self._cpool = nc.alloc_sbuf_tensor("const_pool", [128, 32], F32)
        self._consts = {}
        self._pool_auto = 0
        self._phase_auto = 0

    def const_col(self, value):
        v = float(value)
        if v not in self._consts:
            idx = len(self._consts)
            assert idx < 32
            tok = self.op("pool", "memset", self._cpool[:, idx : idx + 1], v)
            self._consts[v] = (self._cpool[:, idx : idx + 1], tok)
        return self._consts[v]

    def _waits(self, eng, deps):
        best = {}
        for (s, v) in deps:
            if v > best.get(s, 0):
                best[s] = v
        waits = []
        wm = self.wm[eng]
        for s, v in best.items():
            if v > wm[s]:
                waits.append((s, v))
                wm[s] = v
        return waits

    def op(self, eng, method, *args, deps=(), **kwargs):
        if eng == "pe":
            deps = [d for d in deps if d[0] != "pe"]
        waits = self._waits(eng, deps)
        self.cnt[eng] += 1
        self.plan[eng].append((method, args, kwargs, waits, [(eng, 1)]))
        return (eng, self.cnt[eng])

    def dma(self, eng, out, in_, deps=(), transpose=False, stream=None, **kwargs):
        assert eng in ("sp", "act", "pool")
        waits = self._waits(eng, deps)
        if stream is None:
            stream = f"_a{eng}{self._phase_auto}"
            self._phase_auto += 1
        d = "dma_" + stream
        if d not in self.sem:
            self.sem[d] = self.nc.alloc_semaphore("sem_" + d)
            self.cnt[d] = 0
            for e in self.wm:
                self.wm[e][d] = 0
        self.cnt[d] += 16
        meth = "dma_start_transpose" if transpose else "dma_start"
        self.plan[eng].append((meth, (out, in_), kwargs, waits, [(d, 16)]))
        return (d, self.cnt[d])

    def cc(self, kind, op, groups, ins, outs, deps=()):
        waits = self._waits("pool", deps)
        d = f"cc_{self._pool_auto}"
        self._pool_auto += 1
        self.sem[d] = self.nc.alloc_semaphore("sem_" + d)
        self.cnt[d] = 16
        for e in self.wm:
            self.wm[e][d] = 0
        self.plan["pool"].append((
            "collective_compute", (kind, op),
            {"replica_groups": groups, "ins": ins, "outs": outs},
            waits, [(d, 16)]))
        return (d, 16)

    def barrier(self):
        self._phase_auto = 0
        snapshot = [(s, self.cnt[s]) for s in self.cnt if self.cnt[s] > 0]
        for e in self.engs:
            waits = []
            wm = self.wm[e]
            for (s, v) in snapshot:
                if s != e and v > wm[s]:
                    waits.append((s, v))
                    wm[s] = v
            if waits:
                self.plan[e].append((None, (), {}, waits, []))

    def finish(self, final_deps):
        waits = self._waits("sp", final_deps)
        if waits:
            self.plan["sp"].append((None, (), {}, waits, []))

    def emit(self):
        nc = self.nc
        with nc.Block() as blk:
            for ename in ENG_ORDER:
                ops = self.plan[ename]
                if not ops:
                    continue
                sect = getattr(blk, SECT[ename])

                def make_body(body_ops):
                    def body(e):
                        for (method, args, kwargs, waits, incs) in body_ops:
                            for (s, v) in waits:
                                e.wait_ge(self.sem[s], v)
                            if method is None:
                                continue
                            instr = getattr(e, method)(*args, **kwargs)
                            for (s, amt) in incs:
                                instr.then_inc(self.sem[s], amt)
                    return body

                sect(make_body(ops))



def emit_ln(p, nc, x_ap, out_sb, n_tiles, sb_prefix, eps=LN_EPS, rms=False,
            dma_eng="sp", x_deps=(), asb=None):
    asb = asb or nc.alloc_sbuf_tensor
    """LayerNorm (or RMS-stat) over last dim 4096 of token rows.

    x_ap: DRAM [n_tiles*128, 4096] f32. out_sb: SBUF [128, n_tiles*4096] BF16.
    Returns (apply_tokens per tile, mv_sb, stat tokens).
    If rms=True computes 1/sqrt(mean(x^2)+eps) instead and applies ONLY the
    stats (returns rinv column tokens), still writing (x-m)*r for rms=False.
    """
    xt = asb(sb_prefix + "_x", [128, n_tiles * D], F32)
    st = asb(sb_prefix + "_st", [128, 48], F32)
    mv = asb(sb_prefix + "_mv", [128, 2 * n_tiles], F32)
    rc = asb(sb_prefix + "_rc", [128, 2 * n_tiles], F32)
    toks = []
    t_ld = p.dma(dma_eng, xt[:].rearrange("p (t d) -> p t d", t=n_tiles),
                 x_ap.rearrange("(t p) d -> p t d", p=128), deps=x_deps)
    prev_aggr = None
    for t in range(n_tiles):
        stats = []
        for j in range(8):
            dep = [t_ld]
            if prev_aggr is not None and t >= 1 and j == 0:
                pass
            stats.append(p.op("dve", "bn_stats", st[:, 6 * j : 6 * j + 6],
                              xt[:, t * D + 512 * j : t * D + 512 * (j + 1)],
                              deps=[t_ld] + ([prev_aggr] if prev_aggr else [])))
        aggr = p.op("dve", "bn_aggr", mv[:, 2 * t : 2 * t + 2], st[:], deps=stats)
        prev_aggr = aggr
        if rms:
            t1 = p.op("dve", "tensor_tensor", rc[:, 2 * t : 2 * t + 1],
                      mv[:, 2 * t : 2 * t + 1], mv[:, 2 * t : 2 * t + 1],
                      op=ALU.mult, deps=[aggr])
            t2 = p.op("dve", "tensor_tensor", rc[:, 2 * t : 2 * t + 1],
                      rc[:, 2 * t : 2 * t + 1], mv[:, 2 * t + 1 : 2 * t + 2],
                      op=ALU.add, deps=[aggr, t1])
            eap, etok = p.const_col(eps)
            t3 = p.op("act", "activation", rc[:, 2 * t + 1 : 2 * t + 2],
                      rc[:, 2 * t : 2 * t + 1], AF.Ln, bias=eap, scale=1.0,
                      deps=[t2, etok])
            t4 = p.op("act", "activation", rc[:, 2 * t : 2 * t + 1],
                      rc[:, 2 * t + 1 : 2 * t + 2], AF.Exp, scale=-0.5,
                      deps=[t3])
            toks.append((t4, rc[:, 2 * t : 2 * t + 1]))
        else:
            eap, etok = p.const_col(eps)
            t3 = p.op("act", "activation", rc[:, 2 * t + 1 : 2 * t + 2],
                      mv[:, 2 * t + 1 : 2 * t + 2], AF.Ln, bias=eap, scale=1.0,
                      deps=[aggr, etok])
            t4 = p.op("act", "activation", rc[:, 2 * t : 2 * t + 1],
                      rc[:, 2 * t + 1 : 2 * t + 2], AF.Exp, scale=-0.5,
                      deps=[t3])
            ap = p.op("dve", "tensor_scalar", out_sb[:, t * D : (t + 1) * D],
                      xt[:, t * D : (t + 1) * D],
                      mv[:, 2 * t : 2 * t + 1], rc[:, 2 * t : 2 * t + 1],
                      op0=ALU.subtract, op1=ALU.mult, deps=[aggr, t4])
            toks.append(ap)
    return toks, xt, t_ld


def build_ln_launch(nc):
    p = Prog(nc)
    x = nc.dram_tensor("x_in", [NTOK, D], F32, kind="ExternalInput").ap()
    xh = nc.dram_tensor("xh_out", [NTOK, D], BF16, kind="ExternalOutput").ap()
    xh_sb = nc.alloc_sbuf_tensor("xh_sb", [128, TT * D], BF16)
    toks, _, _ = emit_ln(p, nc, x, xh_sb, TT, "ln")
    st = p.dma("sp", xh.rearrange("(t p) d -> p t d", p=128),
               xh_sb[:].rearrange("p (t d) -> p t d", t=TT), deps=toks)
    p.finish([st])
    p.emit()
    return nc


def emit_ident(p, nc, asb=None, name="ident", dtype=BF16):
    asb = asb or nc.alloc_sbuf_tensor
    ident = asb(name, [128, 128], dtype)
    t0 = p.op("pool", "memset", ident[:], 0.0)
    t1 = p.op("pool", "affine_select", out=ident[:], in_=ident[:],
              compare_op=ALU.not_equal, fill=1.0, base=0,
              pattern=[[-1, 128]], channel_multiplier=1, deps=[t0])
    return ident, t1


def emit_ones(p, nc, asb, name, shape, dtype=BF16):
    t = asb(name, shape, dtype)
    tok = p.op("pool", "memset", t[:], 1.0)
    return t, tok


def emit_l0proj(p, nc, A, pfx, asb, aps_):
    xhT = A["xhT"]
    WqdT = A["WqdT"]
    Wqk = A["Wqk"]
    bqd = A["bqd"]
    bqk = A["bqk"]
    Wkvd = A["Wkvd"]
    bkv = A["bkv"]
    qnat = A["qnat"]
    ckv = A["ckv"]

    xhT_sb = asb(pfx + "xhT_sb", [128, FC * NTOK], BF16)
    wqdT_sb = asb(pfx + "wqdT_sb", [12, D], BF16)
    wqk_sb = asb(pfx + "wqk_sb", [12, 128], BF16)
    bqd_sb = asb(pfx + "bqd_sb", [12, 1], BF16)
    bqk_sb = asb(pfx + "bqk_sb", [1, 128], F32)
    wkvd_sb = asb(pfx + "wkvd_sb", [128, FC * 4], BF16)
    bkv_sb = asb(pfx + "bkv_sb", [1, 4], BF16)
    wqc_sb = asb(pfx + "wqc_sb", [128, FC * 128], BF16)
    bqc_sb = asb(pfx + "bqc_sb", [1, 128], F32)
    bqcb_sb = asb(pfx + "bqcb_sb", [1, 128], BF16)
    qnat_sb = asb(pfx + "qnat_sb", [128, TT * 128], BF16)
    ckv_sb = asb(pfx + "ckv_sb", [128, TT * 4], F32)

    ps0 = aps_(pfx + "ps0", [128, 512], F32)
    ps1 = aps_(pfx + "ps1", [128, 512], F32)
    ps2 = aps_(pfx + "ps2", [128, 512], F32)

    t_xh = p.dma("sp", xhT_sb[:].rearrange("p (c t) -> p c t", c=FC),
                 xhT.rearrange("(c p) t -> p c t", p=128))
    t_wqd = p.dma("pool", wqdT_sb[:], WqdT)
    t_wqk = p.dma("pool", wqk_sb[:], Wqk)
    t_bqd = p.dma("pool", bqd_sb[:], bqd)
    t_bqk = p.dma("sp", bqk_sb[:], bqk)
    t_wkv = p.dma("pool", wkvd_sb[:].rearrange("p (c n) -> p c n", c=FC),
                  Wkvd.rearrange("(c p) n -> p c n", p=128))
    t_bkv = p.dma("pool", bkv_sb[:], bkv)
    ones_row, t_ones = emit_ones(p, nc, asb, pfx + "ones_row", [1, 128])

    wqc_toks = []
    ps_tok = [None, None]
    for c in range(FC):
        ps = (ps0, ps1)[c % 2]
        deps = [t_wqd, t_wqk]
        if ps_tok[c % 2] is not None:
            deps.append(ps_tok[c % 2])
        mm = p.op("pe", "matmul", ps[0:128, 0:128], wqdT_sb[:, c * 128 : (c + 1) * 128],
                  wqk_sb[:], start=True, stop=True, deps=deps)
        ev = p.op("dve", "tensor_copy", wqc_sb[:, c * 128 : (c + 1) * 128],
                  ps[0:128, 0:128], deps=[mm])
        ps_tok[c % 2] = ev
        wqc_toks.append(ev)
    mm_b = p.op("pe", "matmul", ps2[0:1, 0:128], bqd_sb[:], wqk_sb[:],
                start=True, stop=True, deps=[t_bqd, t_wqk])
    ev_b = p.op("dve", "tensor_copy", bqc_sb[:], ps2[0:1, 0:128], deps=[mm_b, t_bqk])
    ad_b = p.op("dve", "tensor_tensor", bqc_sb[:], bqc_sb[:], bqk_sb[:],
                op=ALU.add, deps=[ev_b])
    cv_b = p.op("dve", "tensor_copy", bqcb_sb[:], bqc_sb[:], deps=[ad_b])

    q_toks, k_toks = [], []
    for tt in range(TT):
        psq = (ps0, ps1)[tt % 2]
        deps = [t_xh, wqc_toks[-1]]
        if tt >= 2:
            deps.append(q_toks[tt - 2][0])
        last = None
        for c in range(FC):
            dd = list(deps) if last is None else [last]
            last = p.op("pe", "matmul", psq[0:128, 0:128],
                        xhT_sb[:, c * NTOK + tt * 128 : c * NTOK + (tt + 1) * 128],
                        wqc_sb[:, c * 128 : (c + 1) * 128],
                        start=(c == 0), stop=False, deps=dd)
        last = p.op("pe", "matmul", psq[0:128, 0:128], ones_row[:],
                    bqcb_sb[:], start=False, stop=True, deps=[last, t_ones, cv_b])
        ev = p.op("dve", "tensor_copy", qnat_sb[:, tt * 128 : (tt + 1) * 128],
                  psq[0:128, 0:128], deps=[last])
        q_toks.append((ev, last))

        deps2 = [t_xh, t_wkv, ev_b if tt == 0 else k_toks[tt - 1][0]]
        last2 = None
        for c in range(FC):
            dd = list(deps2) if last2 is None else [last2]
            last2 = p.op("pe", "matmul", ps2[0:128, 0:4],
                         xhT_sb[:, c * NTOK + tt * 128 : c * NTOK + (tt + 1) * 128],
                         wkvd_sb[:, c * 4 : (c + 1) * 4],
                         start=(c == 0), stop=False, deps=dd)
        last2 = p.op("pe", "matmul", ps2[0:128, 0:4], ones_row[:], bkv_sb[:],
                     start=False, stop=True, deps=[last2, t_ones, t_bkv])
        ev2 = p.op("dve", "tensor_copy", ckv_sb[:, tt * 4 : (tt + 1) * 4],
                   ps2[0:128, 0:4], deps=[last2])
        k_toks.append((ev2, last2))

    s1 = p.dma("sp", qnat.rearrange("(t p) n -> p t n", p=128),
               qnat_sb[:].rearrange("p (t n) -> p t n", t=TT),
               deps=[t[0] for t in q_toks])
    s2 = p.dma("sp", ckv.rearrange("(t p) n -> p t n", p=128),
               ckv_sb[:].rearrange("p (t n) -> p t n", t=TT),
               deps=[t[0] for t in k_toks])
    return [s1, s2]


def build_l0proj(nc):
    p = Prog(nc)
    A = {}
    A["xhT"] = nc.dram_tensor("xhT", [D, NTOK], BF16, kind="ExternalInput").ap()
    A["WqdT"] = nc.dram_tensor("WqdT", [12, D], F32, kind="ExternalInput").ap()
    A["Wqk"] = nc.dram_tensor("Wqk", [12, 128], F32, kind="ExternalInput").ap()
    A["bqd"] = nc.dram_tensor("bqd", [12, 1], F32, kind="ExternalInput").ap()
    A["bqk"] = nc.dram_tensor("bqk", [1, 128], F32, kind="ExternalInput").ap()
    A["Wkvd"] = nc.dram_tensor("Wkvd", [D, 4], F32, kind="ExternalInput").ap()
    A["bkv"] = nc.dram_tensor("bkv", [1, 4], F32, kind="ExternalInput").ap()
    A["qnat"] = nc.dram_tensor("qnat", [NTOK, 128], BF16, kind="ExternalOutput").ap()
    A["ckv"] = nc.dram_tensor("ckv", [NTOK, 4], F32, kind="ExternalOutput").ap()
    toks = emit_l0proj(p, nc, A, "", nc.alloc_sbuf_tensor, nc.alloc_psum_tensor)
    p.finish(toks)
    p.emit()
    return nc


L0_SHIFT = 20.0


def emit_l0core(p, nc, A, pfx, asb, aps_):
    qT = A["qT"]
    ckvT32 = A.get("ckvT32")
    ckv5 = A.get("ckv5")
    Wo = A["Wo"]
    WvuT = A["WvuT"]
    bvu = A["bvu"]
    bo = A["bo"]
    xres = A["xres"]
    xout = A["xout"]
    wcomb_h = A["wcomb_scratch"]

    nkb2 = 64 if "ckv_g" in A else 32
    ckvT_sb = asb(pfx + "ckvT_sb", [32, nkb2 * 128], BF16)
    ckv5_sb = asb(pfx + "ckv5_sb", [128, nkb2 * 5], BF16)
    wvuT_sb = asb(pfx + "wvuT_sb", [128, 128], BF16)
    bvu_sb = asb(pfx + "bvu_sb", [128, FC], BF16)
    bo_sb = asb(pfx + "bo_sb", [1, D], F32)
    wo_sb = [asb(pfx + f"wo_sb{i}", [128, 32 * 512], BF16) for i in range(2)]
    wcs_sb = asb(pfx + "wcs_sb", [4, 512], BF16)
    bcomb_sb = asb(pfx + "bcomb_sb", [1, D], F32)
    bcombb_sb = asb(pfx + "bcombb_sb", [1, D], BF16)
    wcomb_sb = asb(pfx + "wcomb_sb", [128, D], BF16)
    q4h_sb = [asb(pfx + f"q4h_sb{i}", [32, NTOK], BF16) for i in range(2)]
    attnT_sb = asb(pfx + "attnT_sb", [128, 2048], BF16)
    o5_sb = asb(pfx + "o5_sb", [5, 512], BF16)
    ot_sb = asb(pfx + "ot_sb", [128, 20], F32)
    rcol_sb = asb(pfx + "rcol_sb", [128, 2], F32)
    oall_sb = asb(pfx + "oall_sb", [128, TT * 128], BF16)
    otT_sb = asb(pfx + "otT_sb", [128, TT * 128], BF16)
    xr_sb = [asb(pfx + f"xr_sb{i}", [128, 512], F32) for i in range(2)]
    xo_sb = [asb(pfx + f"xo_sb{i}", [128, 512], F32) for i in range(2)]

    psA = aps_(pfx + "psA", [128, 2048], F32)
    psO = aps_(pfx + "psO", [128, 512], F32)
    psOut = [aps_(pfx + f"psOut{i}", [128, 512], F32) for i in range(2)]
    psW = psOut[0]
    psB = psOut[1]
    ptps = aps_(pfx + "ptps", [128, 1024], BF16)

    ident, t_id = emit_ident(p, nc, asb, pfx + "ident")
    ones_row, t_ones = emit_ones(p, nc, asb, pfx + "ones_row", [1, 128])
    negB, t_negB = p.const_col(-L0_SHIFT)

    ckv_ev = None
    if "ckv_g" in A:
        ckv_g = A["ckv_g"]
        ckv_q = asb(pfx + "ckv_q", [128, 64 * 4], BF16)
        m5 = p.op("pool", "memset", ckv5_sb[:], 1.0)
        t5toks, tqtoks = [], []
        for b2 in range(2):
            for j in range(8):
                srcr = j if b2 == 0 else 7 - j
                r0 = srcr * 1024 + b2 * 512
                k0 = b2 * 32 + j * 4
                t5toks.append(p.dma(
                    "pool",
                    ckv5_sb[:].rearrange("p (k c) -> p k c", k=64)[:, k0 : k0 + 4, 0:4],
                    ckv_g[r0 : r0 + 512, :].rearrange("(k p) c -> p k c", p=128),
                    deps=[m5], stream="ckv5l"))
                tqtoks.append(p.dma(
                    "pool",
                    ckv_q[:].rearrange("p (k c) -> p k c", k=64)[:, k0 : k0 + 4, :],
                    ckv_g[r0 : r0 + 512, :].rearrange("(k p) c -> p k c", p=128),
                    stream="ckvql"))
        t_ckv5 = t5toks[-1]
        for tok in t5toks:
            pass
        mT = p.op("pool", "memset", ckvT_sb[:], 0.0)
        for kb in range(64):
            dtr = tqtoks + [t_id] + ([ckv_ev] if ckv_ev else [])
            tr = p.op("pe", "transpose", ptps[0:4, 0:128],
                      ckv_q[:, kb * 4 : (kb + 1) * 4], ident[:], deps=dtr)
            ckv_ev = p.op("dve", "tensor_copy",
                          ckvT_sb[0:4, kb * 128 : (kb + 1) * 128],
                          ptps[0:4, 0:128], deps=[tr, mT])
        t_ckvT = ckv_ev
        t_ckv5 = t5toks[-1]
    else:
        t_ckvT = p.dma("pool", ckvT_sb[:], ckvT32)
        t_ckv5 = p.dma("pool", ckv5_sb[:].rearrange("p (k c) -> p k c", k=32),
                       ckv5.rearrange("(k p) c -> p k c", p=128))
    t_wvuT = p.dma("pool", wvuT_sb[:], WvuT)
    t_bvu = p.dma("pool", bvu_sb[:], bvu)
    t_bo = p.dma("sp", bo_sb[:], bo)

    wc_state = {"dma": [], "psB_free": None, "wc_ev": None, "wcs_free": None,
                "wo_tok": [None, None], "bdone": None}

    def emit_wcomb_cb(cb):
        wi = cb % 2
        deps_w = [] if wc_state["wo_tok"][wi] is None else [wc_state["wo_tok"][wi]]
        t_wo = p.dma("pool", wo_sb[wi][:].rearrange("p (c n) -> p c n", c=32),
                     Wo[:, cb * 512 : (cb + 1) * 512].rearrange(
                         "(c p) n -> p c n", p=128),
                     deps=deps_w, stream=f"wo{wi}")
        mmb_last = None
        for h in range(32):
            dw = [t_wo, t_wvuT]
            if wc_state["wc_ev"] is not None:
                dw.append(wc_state["wc_ev"])
            mmw = p.op("pe", "matmul", psW[0:4, 0:512],
                       wvuT_sb[:, 4 * h : 4 * h + 4],
                       wo_sb[wi][:, h * 512 : (h + 1) * 512],
                       start=True, stop=True, deps=dw)
            de = [mmw]
            if wc_state["wcs_free"] is not None:
                de.append(wc_state["wcs_free"])
            wc_ev = p.op("dve", "tensor_copy", wcs_sb[:], psW[0:4, 0:512], deps=de)
            wc_state["wc_ev"] = wc_ev
            t_st = p.dma("sp", wcomb_h[4 * h : 4 * h + 4, cb * 512 : (cb + 1) * 512],
                         wcs_sb[:], deps=[wc_ev], stream="wcs")
            wc_state["wcs_free"] = t_st
            wc_state["dma"].append(t_st)
            db = [t_bvu]
            if mmb_last is not None:
                db.append(mmb_last)
            if h == 0 and wc_state["psB_free"] is not None:
                db.append(wc_state["psB_free"])
            mmb_last = p.op("pe", "matmul", psB[0:1, 0:512],
                            bvu_sb[:, h : h + 1],
                            wo_sb[wi][:, h * 512 : (h + 1) * 512],
                            start=(h == 0), stop=(h == 31), deps=db)
        wc_state["wo_tok"][wi] = mmb_last
        evb = p.op("dve", "tensor_copy", bcomb_sb[:, cb * 512 : (cb + 1) * 512],
                   psB[0:1, 0:512], deps=[mmb_last, t_bo])
        wc_state["psB_free"] = evb
        adb = p.op("dve", "tensor_tensor", bcomb_sb[:, cb * 512 : (cb + 1) * 512],
                   bcomb_sb[:, cb * 512 : (cb + 1) * 512],
                   bo_sb[:, cb * 512 : (cb + 1) * 512], op=ALU.add, deps=[evb])
        wc_state["bdone"] = p.op(
            "dve", "tensor_copy", bcombb_sb[:, cb * 512 : (cb + 1) * 512],
            bcomb_sb[:, cb * 512 : (cb + 1) * 512], deps=[adb])

    q4h_init = [p.op("pool", "memset", q4h_sb[i][:], 0.0) for i in range(2)]
    q4h_user = [q4h_init[0], q4h_init[1]]
    att_user = [None] * 4
    psA_user = [None] * 4
    psO_free = None
    o5_free = None
    ev_ot = ckv_ev
    nrm = None
    oall_toks = []
    for h in range(32):
        bi = h % 2
        t_q = p.dma("sp", q4h_sb[bi][0:4, :], qT[4 * h : 4 * h + 4, :],
                    deps=[q4h_user[bi]], stream=f"q4h{bi}")
        last_mm_s = None
        for qc in range(2):
            mm_av_last = None
            mm_s_l = {}
            ex_l = {}
            for step in range(35):
                if step < 32:
                    kb = step
                    si = kb % 4
                    da = [t_q, t_ckvT]
                    if psA_user[si] is not None:
                        da.append(psA_user[si])
                    kb2 = (qc * 32 + kb) if nkb2 == 64 else kb
                    mm_s_l[kb] = p.op("pe", "matmul",
                                      psA[0:128, si * 512 : (si + 1) * 512],
                                      ckvT_sb[0:32, kb2 * 128 : (kb2 + 1) * 128],
                                      q4h_sb[bi][:, qc * 512 : (qc + 1) * 512],
                                      start=True, stop=True, deps=da)
                    last_mm_s = mm_s_l[kb]
                if step >= 2 and (step - 2) % 2 == 0 and step - 2 < 32:
                    kb = step - 2
                    si = kb % 4
                    de = [mm_s_l[kb], mm_s_l[kb + 1], t_negB]
                    if att_user[si] is not None:
                        de.append(att_user[si])
                    if att_user[si + 1] is not None:
                        de.append(att_user[si + 1])
                    ex = p.op("act", "activation",
                              attnT_sb[:, si * 512 : (si + 2) * 512],
                              psA[0:128, si * 512 : (si + 2) * 512],
                              AF.Exp, scale=0.5, bias=negB, deps=de)
                    ex_l[kb] = ex
                    ex_l[kb + 1] = ex
                    psA_user[si] = ex
                    psA_user[si + 1] = ex
                if step >= 3:
                    kb = step - 3
                    si = kb % 4
                    kb2 = (qc * 32 + kb) if nkb2 == 64 else kb
                    dv = [ex_l[kb], t_ckv5]
                    if mm_av_last is not None:
                        dv.append(mm_av_last)
                    elif psO_free is not None:
                        dv.append(psO_free)
                    mm_av_last = p.op("pe", "matmul", psO[0:5, 0:512],
                                      ckv5_sb[:, kb2 * 5 : kb2 * 5 + 5],
                                      attnT_sb[:, si * 512 : (si + 1) * 512],
                                      start=(kb == 0), stop=(kb == 31), deps=dv)
                    att_user[si] = mm_av_last
                    del ex_l[kb]
            do5 = [mm_av_last]
            if o5_free is not None:
                do5.append(o5_free)
            ev_o5 = p.op("dve", "tensor_copy", o5_sb[:], psO[0:5, 0:512], deps=do5)
            psO_free = ev_o5
            tr_toks = []
            for i in range(4):
                tr = p.op("pe", "transpose", ptps[0:128, 8 * i : 8 * i + 5],
                          o5_sb[0:5, 128 * i : 128 * (i + 1)], ident[0:5, 0:5],
                          deps=[ev_o5, t_id] + ([ev_ot] if ev_ot is not None else []))
                tr_toks.append(tr)
            o5_free = tr_toks[-1]
            for i in range(4):
                ev_ot = p.op("dve", "tensor_copy", ot_sb[:, 5 * i : 5 * i + 5],
                             ptps[0:128, 8 * i : 8 * i + 5],
                             deps=tr_toks if i == 0 else [tr_toks[-1], ev_ot])
            for i in range(4):
                drc = [ev_ot]
                if nrm is not None:
                    drc.append(nrm)
                rc0 = p.op("act", "activation", rcol_sb[:, 1:2],
                           ot_sb[:, 5 * i + 4 : 5 * i + 5], AF.Ln, deps=drc)
                rc = p.op("act", "activation", rcol_sb[:, 0:1],
                          rcol_sb[:, 1:2], AF.Exp, scale=-1.0, deps=[rc0])
                qtile = qc * 4 + i
                nrm = p.op("dve", "tensor_scalar",
                           oall_sb[:, qtile * 128 + 4 * h : qtile * 128 + 4 * h + 4],
                           ot_sb[:, 5 * i : 5 * i + 4], rcol_sb[:, 0:1], None,
                           op0=ALU.mult, deps=[rc])
                oall_toks.append(nrm)
        q4h_user[bi] = last_mm_s
        if h % 4 == 3:
            emit_wcomb_cb(h // 4)

    t_wcomb_rd = p.dma("sp", wcomb_sb[:], wcomb_h[:, :], deps=wc_state["dma"])
    t_bcomb_done = wc_state["bdone"]

    ot_toks = []
    ev_t = ev_ot
    for tt in range(TT):
        tr = p.op("pe", "transpose", ptps[0:128, 0:128],
                  oall_sb[:, tt * 128 : (tt + 1) * 128], ident[:],
                  deps=[ev_t, oall_toks[-1]])
        ev_t = p.op("dve", "tensor_copy", otT_sb[:, tt * 128 : (tt + 1) * 128],
                    ptps[0:128, 0:128], deps=[tr])
        ot_toks.append(ev_t)

    xr_tok = [None, None]
    xo_tok = [None, None]
    psOut_tok = [None, None]
    final_toks = []
    for tt in range(TT):
        for cb in range(8):
            i = (tt * 8 + cb) % 2
            dx = [] if xr_tok[i] is None else [xr_tok[i]]
            t_xr = p.dma("act", xr_sb[i][:],
                         xres[tt * 128 : (tt + 1) * 128, cb * 512 : (cb + 1) * 512],
                         deps=dx, stream=f"xr{i}")
            dm = [ot_toks[tt], t_wcomb_rd]
            if psOut_tok[i] is not None:
                dm.append(psOut_tok[i])
            mm1 = p.op("pe", "matmul", psOut[i][0:128, 0:512],
                       otT_sb[:, tt * 128 : (tt + 1) * 128],
                       wcomb_sb[:, cb * 512 : (cb + 1) * 512],
                       start=True, stop=False, deps=dm)
            mm2 = p.op("pe", "matmul", psOut[i][0:128, 0:512],
                       ones_row[:], bcombb_sb[:, cb * 512 : (cb + 1) * 512],
                       start=False, stop=True, deps=[mm1, t_ones, t_bcomb_done])
            dadd = [mm2, t_xr]
            if xo_tok[i] is not None:
                dadd.append(xo_tok[i])
            add = p.op("dve", "tensor_tensor", xo_sb[i][:], psOut[i][0:128, 0:512],
                       xr_sb[i][:], op=ALU.add, deps=dadd)
            psOut_tok[i] = add
            xr_tok[i] = add
            st = p.dma("sp", xout[tt * 128 : (tt + 1) * 128, cb * 512 : (cb + 1) * 512],
                       xo_sb[i][:], deps=[add], stream=f"xo{i}")
            xo_tok[i] = st
            final_toks.append(st)
    return final_toks


def build_l0core(nc):
    p = Prog(nc)
    A = {}
    A["qT"] = nc.dram_tensor("qT", [128, NTOK], BF16, kind="ExternalInput").ap()
    A["ckvT32"] = nc.dram_tensor("ckvT32", [32, S], F32, kind="ExternalInput").ap()
    A["ckv5"] = nc.dram_tensor("ckv5", [S, 5], F32, kind="ExternalInput").ap()
    A["Wo"] = nc.dram_tensor("Wo", [D, D], F32, kind="ExternalInput").ap()
    A["WvuT"] = nc.dram_tensor("WvuT", [128, 128], F32, kind="ExternalInput").ap()
    A["bvu"] = nc.dram_tensor("bvu", [128, FC], F32, kind="ExternalInput").ap()
    A["bo"] = nc.dram_tensor("bo", [1, D], F32, kind="ExternalInput").ap()
    A["xres"] = nc.dram_tensor("xres", [NTOK, D], F32, kind="ExternalInput").ap()
    A["xout"] = nc.dram_tensor("xout", [NTOK, D], F32, kind="ExternalOutput").ap()
    A["wcomb_scratch"] = nc.dram_tensor("wcomb_scratch", [128, D], BF16, kind="Internal").ap()
    toks = emit_l0core(p, nc, A, "", nc.alloc_sbuf_tensor, nc.alloc_psum_tensor)
    p.finish(toks)
    p.emit()
    return nc


SINK_ITERS = 50
SINK_TAU = 0.1


def emit_l0ffn(p, nc, A, pfx, asb, aps_):
    xh2 = A["xh2"]
    xh2T = A["xh2T"]
    WstkT = A["WstkT"]
    bases = A["bases_bc"]
    gate = A["gate"]
    Wbr = A["Wbr"]
    xres = A["xres"]
    xout = A["xout"]

    xh_sb = asb(pfx + "xh_sb", [128, TT * D], BF16)
    xhT_sb = asb(pfx + "xhT_sb", [128, FC * NTOK], BF16)
    wstk_sb = asb(pfx + "wstk_sb", [128, FC * 24], BF16)
    bases_sb = asb(pfx + "bases_sb", [128, 24], F32)
    gate_sb = asb(pfx + "gate_sb", [1, 4], F32)
    st_sb = asb(pfx + "st_sb", [128, 48], F32)
    mv_sb = asb(pfx + "mv_sb", [128, 2], F32)
    rin_sb = asb(pfx + "rin_sb", [128, 2 * TT], F32)
    lg_sb = asb(pfx + "lg_sb", [128, TT * 24], F32)
    sm_sb = asb(pfx + "sm_sb", [128, 8], F32)
    hpre_sb = asb(pfx + "hpre_sb", [128, TT * 4], F32)
    hpost_sb = asb(pfx + "hpost_sb", [128, TT * 4], F32)
    Z_sb = asb(pfx + "Z_sb", [128, TT * 16], F32)
    Zs_sb = asb(pfx + "Zs_sb", [128, TT * 16], F32)
    T_sb = asb(pfx + "T_sb", [128, TT * 16], F32)
    E_sb = asb(pfx + "E_sb", [128, TT * 16], F32)
    S_sb = asb(pfx + "S_sb", [128, TT * 4], F32)
    U_sb = asb(pfx + "U_sb", [128, TT * 4], F32)
    V_sb = asb(pfx + "V_sb", [128, TT * 4], F32)
    H_sb = asb(pfx + "H_sb", [128, TT * 16], F32)
    gbc_sb = asb(pfx + "gbc_sb", [128, 2], F32)
    onesf_sb = asb(pfx + "onesf_sb", [1, 128], F32)
    xpre_sb = [asb(pfx + f"xpre_sb{i}", [128, 1024], BF16) for i in range(2)]
    xpreT_sb = asb(pfx + "xpreT_sb", [128, 8 * NTOK], BF16)
    wbr_sb = asb(pfx + "wbr_sb", [128, 8 * 1024], BF16)
    y2_sb = [asb(pfx + f"y2_sb{i}", [128, 1024], BF16) for i in range(2)]
    acc_sb = [asb(pfx + f"acc_sb{i}", [128, 1024], F32) for i in range(2)]
    xr_sb = [asb(pfx + f"xr_sb{i}", [128, 1024], F32) for i in range(2)]
    xo_sb = [asb(pfx + f"xo_sb{i}", [128, 1024], F32) for i in range(2)]

    ps0 = aps_(pfx + "ps0", [128, 512], F32)
    ps1 = aps_(pfx + "ps1", [128, 512], F32)
    ptps = aps_(pfx + "ptps", [128, 1024], BF16)

    ident, t_id = emit_ident(p, nc, asb, pfx + "ident")

    t_xh = p.dma("sp", xh_sb[:].rearrange("p (t d) -> p t d", t=TT),
                 xh2.rearrange("(t p) d -> p t d", p=128))
    t_xhT = p.dma("sp", xhT_sb[:].rearrange("p (c t) -> p c t", c=FC),
                  xh2T.rearrange("(c p) t -> p c t", p=128))
    t_ws = p.dma("pool", wstk_sb[:].rearrange("p (c n) -> p c n", c=FC),
                 WstkT.rearrange("(c p) n -> p c n", p=128))
    t_bs = p.dma("sp", bases_sb[:], bases)
    t_g = p.dma("sp", gate_sb[0:1, 0:1], gate)
    t_of = p.op("pool", "memset", onesf_sb[:], 1.0)
    t_wbr = p.dma("pool", wbr_sb[:].rearrange("p (c n) -> p c n", c=8),
                  Wbr.rearrange("(c p) n -> p c n", p=128), stream="wbr")

    rin_toks = []
    st_free = None
    mv_free = None
    for t in range(TT):
        stats = []
        for j in range(8):
            dd = [t_xh]
            if st_free is not None:
                dd.append(st_free)
            stats.append(p.op("dve", "bn_stats", st_sb[:, 6 * j : 6 * j + 6],
                              xh_sb[:, t * D + 512 * j : t * D + 512 * (j + 1)],
                              deps=dd))
        dd = stats + ([mv_free] if mv_free else [])
        aggr = p.op("dve", "bn_aggr", mv_sb[:], st_sb[:], deps=dd)
        st_free = aggr
        t1 = p.op("dve", "tensor_tensor", rin_sb[:, 2 * t : 2 * t + 1],
                  mv_sb[:, 0:1], mv_sb[:, 0:1], op=ALU.mult, deps=[aggr])
        t2 = p.op("dve", "tensor_tensor", rin_sb[:, 2 * t : 2 * t + 1],
                  rin_sb[:, 2 * t : 2 * t + 1], mv_sb[:, 1:2], op=ALU.add,
                  deps=[t1])
        mv_free = t2
        eap, etok = p.const_col(RMS_EPS)
        t3 = p.op("act", "activation", rin_sb[:, 2 * t + 1 : 2 * t + 2],
                  rin_sb[:, 2 * t : 2 * t + 1], AF.Ln, bias=eap, scale=1.0,
                  deps=[t2, etok])
        t4 = p.op("act", "activation", rin_sb[:, 2 * t : 2 * t + 1],
                  rin_sb[:, 2 * t + 1 : 2 * t + 2], AF.Exp, scale=-0.5,
                  deps=[t3])
        rin_toks.append(t4)

    lg_toks = []
    ps_free = [None, None]
    for t in range(TT):
        ps = (ps0, ps1)[t % 2]
        last = None
        for c in range(FC):
            dd = [t_xhT, t_ws] if last is None else [last]
            if last is None and ps_free[t % 2] is not None:
                dd.append(ps_free[t % 2])
            last = p.op("pe", "matmul", ps[0:128, 0:24],
                        xhT_sb[:, c * NTOK + t * 128 : c * NTOK + (t + 1) * 128],
                        wstk_sb[:, c * 24 : (c + 1) * 24],
                        start=(c == 0), stop=(c == FC - 1), deps=dd)
        ts1 = p.op("dve", "tensor_scalar", lg_sb[:, t * 24 : (t + 1) * 24],
                   ps[0:128, 0:24], rin_sb[:, 2 * t : 2 * t + 1], None,
                   op0=ALU.mult, deps=[last, rin_toks[t]])
        ps_free[t % 2] = ts1
        ad = p.op("dve", "tensor_tensor", lg_sb[:, t * 24 : (t + 1) * 24],
                  lg_sb[:, t * 24 : (t + 1) * 24], bases_sb[:], op=ALU.add,
                  deps=[ts1, t_bs])
        lg_toks.append(ad)

    hpre_toks, hpost_toks = [], []
    sm_free = None
    for t in range(TT):
        for (off, dst, toks) in ((16, hpre_sb, hpre_toks), (20, hpost_sb, hpost_toks)):
            sl = lg_sb[:, t * 24 + off : t * 24 + off + 4]
            dd = [lg_toks[t]]
            if sm_free is not None:
                dd.append(sm_free)
            m1 = p.op("dve", "tensor_reduce", sm_sb[:, 0:1], sl, axis=AX.X,
                      op=ALU.max, negate=True, deps=dd)
            e = p.op("act", "activation", sm_sb[:, 4:8], sl, AF.Exp,
                     scale=1.0, bias=sm_sb[:, 0:1], deps=[m1])
            s = p.op("dve", "reduce_sum", sm_sb[:, 1:2], sm_sb[:, 4:8],
                     axis=AX.X, deps=[e])
            r0 = p.op("act", "activation", sm_sb[:, 3:4], sm_sb[:, 1:2],
                      AF.Ln, deps=[s])
            r = p.op("act", "activation", sm_sb[:, 2:3], sm_sb[:, 3:4],
                     AF.Exp, scale=-1.0, deps=[r0])
            n = p.op("dve", "tensor_scalar", dst[:, t * 4 : (t + 1) * 4],
                     sm_sb[:, 4:8], sm_sb[:, 2:3], None, op0=ALU.mult, deps=[r])
            sm_free = n
            toks.append(n)

    xpre_toks = []
    xpre_free = [None, None]
    xpreT_toks = []
    pt_free = t_id
    for t in range(TT):
        xi = t % 2
        xp = xpre_sb[xi]
        dd = [t_xh, hpre_toks[t]]
        if xpre_free[xi] is not None:
            dd.append(xpre_free[xi])
        a = p.op("dve", "tensor_scalar", xp[:],
                 xh_sb[:, t * D : t * D + 1024], hpre_sb[:, t * 4 : t * 4 + 1],
                 None, op0=ALU.mult, deps=dd)
        for n in range(1, 4):
            a = p.op("dve", "scalar_tensor_tensor", xp[:],
                     xh_sb[:, t * D + n * 1024 : t * D + (n + 1) * 1024],
                     hpre_sb[:, t * 4 + n : t * 4 + n + 1], xp[:],
                     op0=ALU.mult, op1=ALU.add, deps=[a])
        xpre_toks.append(a)
        trs = []
        for cc in range(8):
            tr = p.op("pe", "transpose", ptps[0:128, cc * 128 : (cc + 1) * 128],
                      xp[:, cc * 128 : (cc + 1) * 128],
                      ident[:], deps=[a, t_id, pt_free])
            trs.append(tr)
        xpre_free[xi] = trs[-1]
        ev = p.op("dve", "tensor_copy",
                  xpreT_sb[:].rearrange("p (c t2) -> p c t2", c=8)[:, :, t * 128 : (t + 1) * 128],
                  ptps[0:128, 0:1024].rearrange("p (c x) -> p c x", c=8),
                  deps=trs)
        pt_free = ev
        xpreT_toks.append(ev)
    Z3 = Z_sb[:].rearrange("p (t i j) -> p t i j", t=TT, i=4)
    Zs3 = Zs_sb[:].rearrange("p (t j i) -> p t j i", t=TT, j=4)
    T3 = T_sb[:].rearrange("p (t a b) -> p t a b", t=TT, a=4)
    E3 = E_sb[:].rearrange("p (t a b) -> p t a b", t=TT, a=4)
    S3 = S_sb[:].rearrange("p (t a) -> p t a", t=TT)
    U3 = U_sb[:].rearrange("p (t a) -> p t a", t=TT)
    V3 = V_sb[:].rearrange("p (t a) -> p t a", t=TT)
    tz = p.op("dve", "tensor_scalar", Z_sb[:].rearrange("p (t x) -> p t x", t=TT),
              lg_sb[:].rearrange("p (t x) -> p t x", t=TT)[:, :, 0:16],
              1.0 / SINK_TAU, None, op0=ALU.mult, deps=[lg_toks[-1]])
    tzs = p.op("dve", "tensor_copy",
               Zs_sb[:].rearrange("p (t j i) -> p t j i", t=TT, j=4),
               Z_sb[:].rearrange("p (t i j) -> p t j i", t=TT, i=4),
               deps=[tz])
    tu = p.op("pool", "memset", U_sb[:], 0.0)
    tv = p.op("pool", "memset", V_sb[:], 0.0)
    u_tok, v_tok = tu, tv
    T_free = None
    E_free = None
    S_free = None
    for it in range(SINK_ITERS):
        adds = []
        for i in range(4):
            dd = [tz, v_tok]
            if T_free is not None:
                dd.append(T_free)
            adds.append(p.op("dve", "tensor_tensor", T3[:, :, i : i + 1, :], Z3[:, :, i : i + 1, :],
                             V3[:], op=ALU.add, deps=dd))
        dd = adds + ([E_free] if E_free else [])
        ex = p.op("act", "activation", E_sb[:], T_sb[:], AF.Exp, deps=dd)
        T_free = ex
        dd = [ex] + ([S_free] if S_free else [])
        s1 = p.op("dve", "tensor_tensor", S3[:], E3[:, :, :, 0:1], E3[:, :, :, 1:2],
                  op=ALU.add, deps=dd)
        s2 = p.op("dve", "tensor_tensor", S3[:], S3[:], E3[:, :, :, 2:3],
                  op=ALU.add, deps=[s1])
        s3 = p.op("dve", "tensor_tensor", S3[:], S3[:], E3[:, :, :, 3:4],
                  op=ALU.add, deps=[s2])
        E_free = s3
        ln = p.op("act", "activation", S_sb[:], S_sb[:], AF.Ln, deps=[s3])
        u_tok = p.op("dve", "tensor_scalar", U_sb[:], S_sb[:], -1.0, None,
                     op0=ALU.mult, deps=[ln, u_tok])
        S_free = u_tok
        adds = []
        for j in range(4):
            adds.append(p.op("dve", "tensor_tensor", T3[:, :, j : j + 1, :], Zs3[:, :, j : j + 1, :],
                             U3[:], op=ALU.add, deps=[tzs, u_tok, T_free]))
        ex = p.op("act", "activation", E_sb[:], T_sb[:], AF.Exp, deps=adds + [E_free])
        T_free = ex
        s1 = p.op("dve", "tensor_tensor", S3[:], E3[:, :, :, 0:1], E3[:, :, :, 1:2],
                  op=ALU.add, deps=[ex, S_free])
        s2 = p.op("dve", "tensor_tensor", S3[:], S3[:], E3[:, :, :, 2:3],
                  op=ALU.add, deps=[s1])
        s3 = p.op("dve", "tensor_tensor", S3[:], S3[:], E3[:, :, :, 3:4],
                  op=ALU.add, deps=[s2])
        E_free = s3
        ln = p.op("act", "activation", S_sb[:], S_sb[:], AF.Ln, deps=[s3])
        v_tok = p.op("dve", "tensor_scalar", V_sb[:], S_sb[:], -1.0, None,
                     op0=ALU.mult, deps=[ln, v_tok])
        S_free = v_tok
    adds = []
    for i in range(4):
        adds.append(p.op("dve", "tensor_tensor", T3[:, :, i : i + 1, :], Z3[:, :, i : i + 1, :],
                         V3[:], op=ALU.add, deps=[tz, v_tok, T_free]))
    sw = p.op("dve", "tensor_copy",
              E_sb[:].rearrange("p (t j i) -> p t j i", t=TT, j=4),
              T_sb[:].rearrange("p (t i j) -> p t j i", t=TT, i=4),
              deps=adds + [E_free])
    adds2 = []
    for j in range(4):
        adds2.append(p.op("dve", "tensor_tensor", E3[:, :, j : j + 1, :], E3[:, :, j : j + 1, :],
                          U3[:], op=ALU.add, deps=[sw, u_tok]))
    hx = p.op("act", "activation", H_sb[:], E_sb[:], AF.Exp, deps=adds2)
    sg = p.op("act", "activation", gate_sb[0:1, 1:2], gate_sb[0:1, 0:1],
              AF.Sigmoid, deps=[t_g])
    bcg = p.op("pe", "matmul", ps0[0:128, 0:1], onesf_sb[:], gate_sb[0:1, 1:2],
               start=True, stop=True, deps=[sg, t_of, ps_free[0]])
    evg = p.op("dve", "tensor_copy", gbc_sb[:, 0:1], ps0[0:128, 0:1], deps=[bcg])
    onemg = p.op("dve", "tensor_scalar", gbc_sb[:, 1:2], gbc_sb[:, 0:1],
                 -1.0, 1.0, op0=ALU.mult, op1=ALU.add, deps=[evg])
    hg = p.op("dve", "tensor_scalar", H_sb[:], H_sb[:], gbc_sb[:, 0:1], None,
              op0=ALU.mult, deps=[hx, evg])


    psy_free = [evg, ps_free[1]]
    y_free = [None, None]
    xr_tok = [None, None]
    final_toks = []
    acc_tok = [None, None]
    xo_tok = [None, None]
    y_done = []
    for t in range(TT):
        ps = (ps0, ps1)[t % 2]
        yb = y2_sb[t % 2]
        last = None
        for cc in range(8):
            dd = [t_wbr, xpreT_toks[t]] if last is None else [last]
            if last is None and psy_free[t % 2] is not None:
                dd.append(psy_free[t % 2])
            if last is None and y_free[t % 2] is not None:
                dd.append(y_free[t % 2])
            last = p.op("pe", "matmul", ps[0:128, 0:512],
                        xpreT_sb[:, cc * NTOK + t * 128 : cc * NTOK + (t + 1) * 128],
                        wbr_sb[:, cc * 1024 : cc * 1024 + 512],
                        start=(cc == 0), stop=(cc == 7), deps=dd)
        ev1 = p.op("act", "copy", yb[:, 0:512], ps[0:128, 0:512],
                   deps=[last] + ([y_free[t % 2]] if y_free[t % 2] else []))
        last = None
        for cc in range(8):
            dd = [t_wbr, xpreT_toks[t], ev1] if last is None else [last]
            last = p.op("pe", "matmul", ps[0:128, 0:512],
                        xpreT_sb[:, cc * NTOK + t * 128 : cc * NTOK + (t + 1) * 128],
                        wbr_sb[:, cc * 1024 + 512 : (cc + 1) * 1024],
                        start=(cc == 0), stop=(cc == 7), deps=dd)
        ev2 = p.op("act", "copy", yb[:, 512:1024], ps[0:128, 0:512], deps=[last])
        psy_free[t % 2] = ev2
        y_done.append(ev2)
        last_add = None
        for i in range(4):
            ri = (t * 4 + i) % 2
            dd = [] if xr_tok[ri] is None else [xr_tok[ri]]
            t_xr = p.dma("act", xr_sb[ri][:],
                         xres[t * 128 : (t + 1) * 128, i * 1024 : (i + 1) * 1024],
                         deps=dd, stream=f"xr{ri}")
            ai = i % 2
            dd = [t_xr, t_xh, onemg, hg, ev1, ev2, hpost_toks[t]]
            if acc_tok[ai] is not None:
                dd.append(acc_tok[ai])
            a = p.op("dve", "scalar_tensor_tensor", acc_sb[ai][:],
                     xh_sb[:, t * D + i * 1024 : t * D + (i + 1) * 1024],
                     gbc_sb[:, 1:2], xr_sb[ri][:],
                     op0=ALU.mult, op1=ALU.add, deps=dd)
            for j in range(4):
                a = p.op("dve", "scalar_tensor_tensor", acc_sb[ai][:],
                         xh_sb[:, t * D + j * 1024 : t * D + (j + 1) * 1024],
                         H_sb[:, t * 16 + 4 * j + i : t * 16 + 4 * j + i + 1],
                         acc_sb[ai][:], op0=ALU.mult, op1=ALU.add, deps=[a])
            oi = i % 2
            do = [a]
            if xo_tok[oi] is not None:
                do.append(xo_tok[oi])
            fin = p.op("dve", "scalar_tensor_tensor", xo_sb[oi][:],
                       yb[:], hpost_sb[:, t * 4 + i : t * 4 + i + 1],
                       acc_sb[ai][:], op0=ALU.mult, op1=ALU.add, deps=do)
            acc_tok[ai] = fin
            last_add = fin
            xr_tok[ri] = a
            y_free[t % 2] = fin
            st = p.dma("sp", xout[t * 128 : (t + 1) * 128, i * 1024 : (i + 1) * 1024],
                       xo_sb[oi][:], deps=[fin], stream=f"xo{oi}")
            xo_tok[oi] = st
            final_toks.append(st)
    return final_toks + y_done


def build_l0ffn(nc):
    p = Prog(nc)
    A = {}
    A["xh2"] = nc.dram_tensor("xh2", [NTOK, D], BF16, kind="ExternalInput").ap()
    A["xh2T"] = nc.dram_tensor("xh2T", [D, NTOK], BF16, kind="ExternalInput").ap()
    A["WstkT"] = nc.dram_tensor("WstkT", [D, 24], F32, kind="ExternalInput").ap()
    A["bases_bc"] = nc.dram_tensor("bases_bc", [128, 24], F32, kind="ExternalInput").ap()
    A["gate"] = nc.dram_tensor("gate", [1, 1], F32, kind="ExternalInput").ap()
    A["Wbr"] = nc.dram_tensor("Wbr", [1024, 1024], F32, kind="ExternalInput").ap()
    A["xres"] = nc.dram_tensor("xres", [NTOK, D], F32, kind="ExternalInput").ap()
    A["xout"] = nc.dram_tensor("xout", [NTOK, D], F32, kind="ExternalOutput").ap()
    toks = emit_l0ffn(p, nc, A, "", nc.alloc_sbuf_tensor, nc.alloc_psum_tensor)
    p.finish(toks)
    p.emit()
    return nc


def emit_l1qkv(p, nc, A, pfx, asb, aps_):
    xhT = A["xhT"]
    wqkv = A["wqkv"]
    qkvT = A["qkvT"]

    xhT_sb = asb(pfx + "xhT_sb", [128, FC * NTOK], BF16)
    wc_sb = [asb(pfx + f"wc_sb{i}", [128, FC * 128], BF16) for i in range(2)]
    o_sb = [asb(pfx + f"o_sb{i}", [128, 512], BF16) for i in range(2)]
    ps = [aps_(pfx + f"ps{i}", [128, 512], F32) for i in range(2)]

    t_xh = p.dma("sp", xhT_sb[:].rearrange("p (c t) -> p c t", c=FC),
                 xhT.rearrange("(c p) t -> p c t", p=128))
    w_tok = [None, None]
    ps_tok = [None, None]
    o_tok = [None, None]
    st_toks = []
    for colb in range(48):
        wi = colb % 2
        dd = [] if w_tok[wi] is None else [w_tok[wi]]
        t_w = p.dma("pool", wc_sb[wi][:].rearrange("p (c n) -> p c n", c=FC),
                    wqkv[:, colb * 128 : (colb + 1) * 128].rearrange(
                        "(c p) n -> p c n", p=128),
                    deps=dd, stream=f"w{wi}")
        last_any = None
        for tc in range(2):
            i = (colb * 2 + tc) % 2
            last = None
            for c in range(FC):
                dd = [t_w, t_xh] if last is None else [last]
                if last is None and ps_tok[i] is not None:
                    dd.append(ps_tok[i])
                last = p.op("pe", "matmul", ps[i][0:128, 0:512],
                            wc_sb[wi][:, c * 128 : (c + 1) * 128],
                            xhT_sb[:, c * NTOK + tc * 512 : c * NTOK + (tc + 1) * 512],
                            start=(c == 0), stop=(c == FC - 1), deps=dd)
            dd = [last]
            if o_tok[i] is not None:
                dd.append(o_tok[i])
            ev = p.op("dve", "tensor_copy", o_sb[i][:], ps[i][0:128, 0:512], deps=dd)
            ps_tok[i] = ev
            st = p.dma("sp", qkvT[colb * 128 : (colb + 1) * 128, tc * 512 : (tc + 1) * 512],
                       o_sb[i][:], deps=[ev], stream=f"o{i}")
            o_tok[i] = st
            st_toks.append(st)
            last_any = last
        w_tok[wi] = last_any
    return st_toks


def build_l1qkv(nc):
    p = Prog(nc)
    A = {}
    A["xhT"] = nc.dram_tensor("xhT", [D, NTOK], BF16, kind="ExternalInput").ap()
    A["wqkv"] = nc.dram_tensor("wqkv", [D, 6144], F32, kind="ExternalInput").ap()
    A["qkvT"] = nc.dram_tensor("qkvT", [6144, NTOK], BF16, kind="ExternalOutput").ap()
    toks = emit_l1qkv(p, nc, A, "", nc.alloc_sbuf_tensor, nc.alloc_psum_tensor)
    p.finish(toks)
    p.emit()
    return nc


L1_SHIFT = 12.0
NKB_OF_QC = [4 * qc + 4 for qc in range(8)]


A2A_CHUNKS = [[(0, r), (1, 7 - r)] for r in range(4)]


def emit_l1att(p, nc, A, pfx, asb, aps_):
    a2a = "qA2A" in A
    """GQA attention with sqrt-alibi causal bias, head-parallel.

    Per core: 8 (h,g) units of its batch, full 4096-token causal q range.
    in: QTs [1024,4096] bf16 (8 units x 128 d), KTc [256,4096] bf16 (2 kv
    heads), Vc [4096,256] bf16, SDrep [128,4608] f32, scales_bc [128,16] f32
    (col 2u: 1/(slope*sqrt(128)), col 2u+1: slope).
    out: OUTN [1024, 4096] bf16 (normalized attention out^T, unit-major rows).
    """
    SDrep = A["SDrep"]
    scales = A["scales_bc"]
    OUTN = A["OUTN"]
    if not a2a:
        QTs, KTc, Vc = A["QTs"], A["KTc"], A["Vc"]

    sd_sb = asb(pfx + "sd_sb", [128, 4608], F32)
    scl_sb = asb(pfx + "scl_sb", [128, 16], F32)
    kt_sb = [asb(pfx + f"kt_sb{i}", [128, S], BF16) for i in range(2)]
    v_sb = [asb(pfx + f"v_sb{i}", [128, 32 * 128], BF16) for i in range(2)]
    q_sb = [asb(pfx + f"q_sb{i}", [128, S], BF16) for i in range(2)]
    qs_sb = [asb(pfx + f"qs_sb{i}", [128, S], BF16) for i in range(2)]
    t_sb = [asb(pfx + f"t_sb{i}", [128, 512], F32) for i in range(4)]
    at_sb = [asb(pfx + f"at_sb{i}", [128, 512], BF16) for i in range(4)]
    rb_sb = asb(pfx + "rb_sb", [1, 512], BF16)
    rbf_sb = asb(pfx + "rbf_sb", [1, 512], F32)
    rbc_sb = asb(pfx + "rbc_sb", [128, 512], F32)
    on_sb = [asb(pfx + f"on_sb{i}", [128, 512], BF16) for i in range(2)]
    onesc_sb = asb(pfx + "onesc_sb", [128, 1], BF16)
    onesr_sb = asb(pfx + "onesr_sb", [1, 128], BF16)

    psS = [aps_(pfx + f"psS{i}", [128, 512], F32) for i in range(4)]
    psO = aps_(pfx + "psO", [128, 512], F32)
    psR = aps_(pfx + "psR", [128, 512], F32)
    psB = aps_(pfx + "psB", [128, 512], F32)

    negB, t_negB = p.const_col(-L1_SHIFT)
    t_sd = p.dma("sp", sd_sb[:], SDrep)
    t_scl = p.dma("sp", scl_sb[:], scales)
    t_oc = p.op("pool", "memset", onesc_sb[:], 1.0)
    t_or = p.op("pool", "memset", onesr_sb[:], 1.0)

    kv_tok = [[None, None], [None, None]]
    q_tok = [None, None]
    qs_tok = [None, None]
    psS_tok = [None] * 4
    at_tok = [None] * 4
    t_tok = [None] * 4
    psO_free = None
    psR_free = None
    rb_free = None
    rbc_free = None
    on_tok = [None, None]
    st_toks = []
    kt_cur = [None, None]
    v_cur = [None, None]
    for u in range(8):
        bb = u // 4
        hl = bb
        g = u % 4
        if g == 0:
            hb = hl % 2
            dk = [] if kv_tok[0][hb] is None else [kv_tok[0][hb]]
            dv_ = [] if kv_tok[1][hb] is None else [kv_tok[1][hb]]
            if a2a:
                for j in range(8):
                    srcr = j if bb == 0 else 7 - j
                    kt_cur[hb] = p.dma(
                        "sp", kt_sb[hb][:, j * 512 : (j + 1) * 512],
                        A["kA2A"][srcr * 128 : (srcr + 1) * 128,
                                  bb * 512 : (bb + 1) * 512],
                        deps=dk, stream=f"kt{hb}")
                    for sub in range(4):
                        kb = j * 4 + sub
                        v_cur[hb] = p.dma(
                            "act", v_sb[hb][:, kb * 128 : (kb + 1) * 128],
                            A["vA2A"][srcr * 128 : (srcr + 1) * 128,
                                      bb * 512 + sub * 128 : bb * 512 + (sub + 1) * 128],
                            deps=dv_, transpose=True, stream=f"v{hb}")
            else:
                kt_cur[hb] = p.dma("sp", kt_sb[hb][:], KTc[hl * 128 : (hl + 1) * 128, :],
                                   deps=dk, stream=f"kt{hb}")
                v_cur[hb] = p.dma("act", v_sb[hb][:].rearrange("p (k n) -> p k n", k=32),
                                  Vc[:, hl * 128 : (hl + 1) * 128].rearrange(
                                      "(k p) n -> p k n", p=128),
                                  deps=dv_, stream=f"v{hb}")
        hb = hl % 2
        qi = u % 2
        dq = [] if q_tok[qi] is None else [q_tok[qi]]
        if a2a:
            for j in range(8):
                srcr = j if bb == 0 else 7 - j
                t_q = p.dma("sp", q_sb[qi][:, j * 512 : (j + 1) * 512],
                            A["qA2A"][srcr * 512 + g * 128 : srcr * 512 + (g + 1) * 128,
                                      bb * 512 : (bb + 1) * 512],
                            deps=dq, stream=f"q{qi}")
        else:
            t_q = p.dma("sp", q_sb[qi][:], QTs[u * 128 : (u + 1) * 128, :],
                        deps=dq, stream=f"q{qi}")
        dqs = [t_q, t_scl]
        if qs_tok[qi] is not None:
            dqs.append(qs_tok[qi])
        t_qs = p.op("dve", "tensor_scalar", qs_sb[qi][:], q_sb[qi][:],
                    scl_sb[:, 2 * u : 2 * u + 1], None, op0=ALU.mult, deps=dqs)
        q_tok[qi] = t_qs
        last_mm_s = None
        for qc in range(8):
            Q0 = 512 * qc
            nkb = NKB_OF_QC[qc]
            mm_av = None
            mm_sum = None
            mm_s_l = {}
            sub_l = {}
            ex_l = {}
            NB = 4
            for step in range(nkb + 3):
                if step < nkb:
                    kb = step
                    si = kb % NB
                    da = [t_qs, kt_cur[hb]]
                    if psS_tok[si] is not None:
                        da.append(psS_tok[si])
                    mm_s_l[kb] = p.op("pe", "matmul", psS[si][0:128, 0:512],
                                      kt_sb[hb][:, kb * 128 : (kb + 1) * 128],
                                      qs_sb[qi][:, Q0 : Q0 + 512],
                                      start=True, stop=True, deps=da)
                    last_mm_s = mm_s_l[kb]
                if step >= 1 and step - 1 < nkb:
                    kb = step - 1
                    si = kb % NB
                    off = Q0 - 128 * kb + 512
                    dsub = [mm_s_l[kb], t_sd]
                    if t_tok[si] is not None:
                        dsub.append(t_tok[si])
                    sub_l[kb] = p.op("dve", "tensor_tensor", t_sb[si][:],
                                     psS[si][0:128, 0:512], sd_sb[:, off : off + 512],
                                     op=ALU.subtract, deps=dsub)
                    psS_tok[si] = sub_l[kb]
                if step >= 2 and step - 2 < nkb:
                    kb = step - 2
                    si = kb % NB
                    dex = [sub_l[kb], t_negB, t_scl]
                    if at_tok[si] is not None:
                        dex.append(at_tok[si])
                    ex_l[kb] = p.op("act", "activation", at_sb[si][:], t_sb[si][:],
                                    AF.Exp, scale=scl_sb[:, 2 * u + 1 : 2 * u + 2],
                                    bias=negB, deps=dex)
                    t_tok[si] = ex_l[kb]
                if step >= 3:
                    kb = step - 3
                    si = kb % NB
                    dav = [ex_l[kb], v_cur[hb]]
                    if mm_av is not None:
                        dav.append(mm_av)
                    elif psO_free is not None:
                        dav.append(psO_free)
                    mm_av = p.op("pe", "matmul", psO[0:128, 0:512],
                                 v_sb[hb][:, kb * 128 : (kb + 1) * 128],
                                 at_sb[si][:], start=(kb == 0), stop=(kb == nkb - 1),
                                 deps=dav)
                    dsm = [ex_l[kb], t_oc]
                    if mm_sum is not None:
                        dsm.append(mm_sum)
                    elif psR_free is not None:
                        dsm.append(psR_free)
                    mm_sum = p.op("pe", "matmul", psR[0:1, 0:512], onesc_sb[:],
                                  at_sb[si][:], start=(kb == 0), stop=(kb == nkb - 1),
                                  deps=dsm)
                    at_tok[si] = mm_sum
                    del ex_l[kb]
            drb = [mm_sum]
            if rb_free is not None:
                drb.append(rb_free)
            rec0 = p.op("act", "activation", rbf_sb[:], psR[0:1, 0:512],
                        AF.Ln, deps=drb)
            psR_free = rec0
            rec = p.op("act", "activation", rb_sb[:], rbf_sb[:], AF.Exp,
                       scale=-1.0, deps=[rec0])
            dbc = [rec, t_or]
            if rbc_free is not None:
                dbc.append(rbc_free)
            bc = p.op("pe", "matmul", psB[0:128, 0:512], onesr_sb[:], rb_sb[:],
                      start=True, stop=True, deps=dbc)
            rb_free = bc
            evbc = p.op("act", "copy", rbc_sb[:], psB[0:128, 0:512], deps=[bc])
            rbc_free = evbc
            oi = (u * 8 + qc) % 2
            dn = [mm_av, evbc]
            if on_tok[oi] is not None:
                dn.append(on_tok[oi])
            nm = p.op("dve", "tensor_tensor", on_sb[oi][:], psO[0:128, 0:512],
                      rbc_sb[:], op=ALU.mult, deps=dn)
            psO_free = nm
            if a2a:
                jdst = qc if bb == 0 else 7 - qc
                st = p.dma("sp",
                           OUTN[jdst * 512 + g * 128 : jdst * 512 + (g + 1) * 128,
                                bb * 512 : (bb + 1) * 512],
                           on_sb[oi][:], deps=[nm], stream=f"on{oi}")
            else:
                st = p.dma("sp", OUTN[u * 128 : (u + 1) * 128, Q0 : Q0 + 512],
                           on_sb[oi][:], deps=[nm], stream=f"on{oi}")
            on_tok[oi] = st
            st_toks.append(st)
        kv_tok[0][hb] = last_mm_s
        kv_tok[1][hb] = mm_av
        q_tok[qi] = last_mm_s
        qs_tok[qi] = last_mm_s
    return st_toks


def build_l1att(nc):
    p = Prog(nc)
    A = {}
    A["QTs"] = nc.dram_tensor("QTs", [NTOK, S], BF16, kind="ExternalInput").ap()
    A["KTc"] = nc.dram_tensor("KTc", [256, S], BF16, kind="ExternalInput").ap()
    A["Vc"] = nc.dram_tensor("Vc", [S, 256], BF16, kind="ExternalInput").ap()
    A["SDrep"] = nc.dram_tensor("SDrep", [128, 4608], F32, kind="ExternalInput").ap()
    A["scales_bc"] = nc.dram_tensor("scales_bc", [128, 16], F32, kind="ExternalInput").ap()
    A["OUTN"] = nc.dram_tensor("OUTN", [NTOK, S], BF16, kind="ExternalOutput").ap()
    toks = emit_l1att(p, nc, A, "", nc.alloc_sbuf_tensor, nc.alloc_psum_tensor)
    p.finish(toks)
    p.emit()
    return nc


def emit_l1wo(p, nc, A, pfx, asb, aps_):
    OT = A["OT"]
    wo = A["wo"]
    xres = A["xres"]
    xout = A["xout"]

    ot_sb = asb(pfx + "ot_sb", [128, TT * FC * 128], BF16)
    wc_sb = [asb(pfx + f"wc_sb{i}", [128, FC * 512], BF16) for i in range(2)]
    xr_sb = [asb(pfx + f"xr_sb{i}", [128, 512], F32) for i in range(2)]
    xo_sb = [asb(pfx + f"xo_sb{i}", [128, 512], F32) for i in range(2)]
    ps = [aps_(pfx + f"ps{i}", [128, 512], F32) for i in range(2)]

    ot_toks = []
    for tt in range(TT):
        ot_toks.append(p.dma(
            "sp",
            ot_sb[:].rearrange("p (t c n) -> p t c n", t=TT, c=FC)[:, tt : tt + 1, :, :],
            OT[:, tt * 128 : (tt + 1) * 128].rearrange("(c p) n -> p c n", p=128)))
    w_tok = [None, None]
    ps_tok = [None, None]
    xr_tok = [None, None]
    xo_tok = [None, None]
    st_toks = []
    for cb in range(8):
        wi = cb % 2
        dd = [] if w_tok[wi] is None else [w_tok[wi]]
        t_w = p.dma("pool", wc_sb[wi][:].rearrange("p (c n) -> p c n", c=FC),
                    wo[:, cb * 512 : (cb + 1) * 512].rearrange("(c p) n -> p c n", p=128),
                    deps=dd, stream=f"w{wi}")
        last_any = None
        for tt in range(TT):
            i = (cb * TT + tt) % 2
            dx = [] if xr_tok[i] is None else [xr_tok[i]]
            t_xr = p.dma("act", xr_sb[i][:],
                         xres[tt * 128 : (tt + 1) * 128, cb * 512 : (cb + 1) * 512],
                         deps=dx, stream=f"xr{i}")
            last = None
            for c in range(FC):
                dd = [t_w, ot_toks[tt]] if last is None else [last]
                if last is None and ps_tok[i] is not None:
                    dd.append(ps_tok[i])
                last = p.op("pe", "matmul", ps[i][0:128, 0:512],
                            ot_sb[:, tt * FC * 128 + c * 128 : tt * FC * 128 + (c + 1) * 128],
                            wc_sb[wi][:, c * 512 : (c + 1) * 512],
                            start=(c == 0), stop=(c == FC - 1), deps=dd)
            last_any = last
            dadd = [last, t_xr]
            if xo_tok[i] is not None:
                dadd.append(xo_tok[i])
            add = p.op("dve", "tensor_tensor", xo_sb[i][:], ps[i][0:128, 0:512],
                       xr_sb[i][:], op=ALU.add, deps=dadd)
            ps_tok[i] = add
            xr_tok[i] = add
            st = p.dma("sp", xout[tt * 128 : (tt + 1) * 128, cb * 512 : (cb + 1) * 512],
                       xo_sb[i][:], deps=[add], stream=f"xo{i}")
            xo_tok[i] = st
            st_toks.append(st)
        w_tok[wi] = last_any
    return st_toks


def build_l1wo(nc):
    p = Prog(nc)
    A = {}
    A["OT"] = nc.dram_tensor("OT", [D, NTOK], BF16, kind="ExternalInput").ap()
    A["wo"] = nc.dram_tensor("wo", [D, D], F32, kind="ExternalInput").ap()
    A["xres"] = nc.dram_tensor("xres", [NTOK, D], F32, kind="ExternalInput").ap()
    A["xout"] = nc.dram_tensor("xout", [NTOK, D], F32, kind="ExternalOutput").ap()
    toks = emit_l1wo(p, nc, A, "", nc.alloc_sbuf_tensor, nc.alloc_psum_tensor)
    p.finish(toks)
    p.emit()
    return nc


def emit_l1ffa(p, nc, A, pfx, asb, aps_):
    xhT = A["xhT"]
    gw1T = A["gw1T"]
    gb1 = A["gb1"]
    gamma = A["gamma_bc"]
    beta = A["beta_bc"]
    gw2T = A["gw2T"]
    gb2 = A["gb2"]
    oh1 = A["oh1"]
    oh2 = A["oh2"]
    g1o = A["g1"]
    pco = A["pc"]

    xhT_sb = asb(pfx + "xhT_sb", [128, FC * NTOK], BF16)
    gw1_sb = asb(pfx + "gw1_sb", [128, FC * 32], BF16)
    gb1_sb = asb(pfx + "gb1_sb", [1, 32], BF16)
    gam_sb = asb(pfx + "gam_sb", [128, 32], F32)
    bet_sb = asb(pfx + "bet_sb", [128, 32], F32)
    gw2_sb = asb(pfx + "gw2_sb", [32, 8], BF16)
    gb2_sb = asb(pfx + "gb2_sb", [1, 8], BF16)
    h_sb = asb(pfx + "h_sb", [128, TT * 32], F32)
    hr_sb = asb(pfx + "hr_sb", [128, TT * 32], BF16)
    hT_sb = asb(pfx + "hT_sb", [32, TT * 128], BF16)
    st_sb = asb(pfx + "st_sb", [128, 6], F32)
    mv_sb = asb(pfx + "mv_sb", [128, 4], F32)
    g_sb = asb(pfx + "g_sb", [128, TT * 8], F32)
    m_sb = asb(pfx + "m_sb", [128, 8], F32)
    oh1_sb = asb(pfx + "oh1_sb", [128, TT * 8], BF16)
    oh2_sb = asb(pfx + "oh2_sb", [128, TT * 8], BF16)
    msk_sb = asb(pfx + "msk_sb", [128, 8], F32)
    g1_sb = asb(pfx + "g1_sb", [128, TT], F32)
    pc_sb = asb(pfx + "pc_sb", [1, 16], F32)
    onesr_sb = asb(pfx + "onesr_sb", [1, 128], BF16)
    onesc_sb = asb(pfx + "onesc_sb", [128, 1], BF16)

    ps0 = aps_(pfx + "ps0", [128, 512], F32)
    ps1 = aps_(pfx + "ps1", [128, 512], F32)
    ptps = aps_(pfx + "ptps", [128, 1024], BF16)
    psc = aps_(pfx + "psc", [128, 512], F32)
    psd = aps_(pfx + "psd", [128, 512], F32)
    psg = aps_(pfx + "psg", [128, 512], F32)

    ident, t_id = emit_ident(p, nc, asb, pfx + "ident")
    t_xh = p.dma("sp", xhT_sb[:].rearrange("p (c t) -> p c t", c=FC),
                 xhT.rearrange("(c p) t -> p c t", p=128))
    t_w1 = p.dma("pool", gw1_sb[:].rearrange("p (c n) -> p c n", c=FC),
                 gw1T.rearrange("(c p) n -> p c n", p=128))
    t_b1 = p.dma("pool", gb1_sb[:], gb1)
    t_gm = p.dma("sp", gam_sb[:], gamma)
    t_bt = p.dma("sp", bet_sb[:], beta)
    t_w2 = p.dma("pool", gw2_sb[:], gw2T)
    t_b2 = p.dma("pool", gb2_sb[:], gb2)
    t_or = p.op("pool", "memset", onesr_sb[:], 1.0)
    t_oc = p.op("pool", "memset", onesc_sb[:], 1.0)
    eap, etok = p.const_col(LN_EPS)

    ps_tok = [None, None]
    st_free = None
    mv_free = None
    h_toks = []
    for tt in range(TT):
        psi = (ps0, ps1)[tt % 2]
        last = None
        for c in range(FC):
            dd = [t_xh, t_w1] if last is None else [last]
            if last is None and ps_tok[tt % 2] is not None:
                dd.append(ps_tok[tt % 2])
            last = p.op("pe", "matmul", psi[0:128, 0:32],
                        xhT_sb[:, c * NTOK + tt * 128 : c * NTOK + (tt + 1) * 128],
                        gw1_sb[:, c * 32 : (c + 1) * 32],
                        start=(c == 0), stop=False, deps=dd)
        last = p.op("pe", "matmul", psi[0:128, 0:32], onesr_sb[:], gb1_sb[:],
                    start=False, stop=True, deps=[last, t_or, t_b1])
        ev = p.op("dve", "tensor_copy", h_sb[:, tt * 32 : (tt + 1) * 32],
                  psi[0:128, 0:32], deps=[last])
        ps_tok[tt % 2] = ev
        dd = [ev] + ([st_free] if st_free else [])
        stt = p.op("dve", "bn_stats", st_sb[:], h_sb[:, tt * 32 : (tt + 1) * 32],
                   deps=dd)
        dd = [stt] + ([mv_free] if mv_free else [])
        ag = p.op("dve", "bn_aggr", mv_sb[:, 0:2], st_sb[:], deps=dd)
        st_free = ag
        t3 = p.op("act", "activation", mv_sb[:, 2:3], mv_sb[:, 1:2], AF.Ln,
                  bias=eap, scale=1.0, deps=[ag, etok])
        t4 = p.op("act", "activation", mv_sb[:, 3:4], mv_sb[:, 2:3], AF.Exp,
                  scale=-0.5, deps=[t3])
        mv_free = t4
        ap_ = p.op("dve", "tensor_scalar", h_sb[:, tt * 32 : (tt + 1) * 32],
                   h_sb[:, tt * 32 : (tt + 1) * 32], mv_sb[:, 0:1], mv_sb[:, 3:4],
                   op0=ALU.subtract, op1=ALU.mult, deps=[t4])
        gm = p.op("dve", "tensor_tensor", h_sb[:, tt * 32 : (tt + 1) * 32],
                  h_sb[:, tt * 32 : (tt + 1) * 32], gam_sb[:], op=ALU.mult,
                  deps=[ap_, t_gm])
        bt = p.op("dve", "tensor_tensor", h_sb[:, tt * 32 : (tt + 1) * 32],
                  h_sb[:, tt * 32 : (tt + 1) * 32], bet_sb[:], op=ALU.add,
                  deps=[gm, t_bt])
        rl = p.op("dve", "tensor_scalar", hr_sb[:, tt * 32 : (tt + 1) * 32],
                  h_sb[:, tt * 32 : (tt + 1) * 32], 0.0, None, op0=ALU.max,
                  deps=[bt])
        h_toks.append(rl)
    pt_free = None
    hT_toks = []
    for tt in range(TT):
        dd = [h_toks[tt], t_id] + ([pt_free] if pt_free else [])
        tr = p.op("pe", "transpose", ptps[0:32, 0:128],
                  hr_sb[:, tt * 32 : (tt + 1) * 32], ident[:], deps=dd)
        ev = p.op("dve", "tensor_copy", hT_sb[0:32, tt * 128 : (tt + 1) * 128],
                  ptps[0:32, 0:128], deps=[tr])
        pt_free = ev
        hT_toks.append(ev)
    pc1_mm = None
    pc2_mm = None
    g_free = None
    m_free = None
    msk_free = None
    oh_toks = []
    g1_toks = []
    for tt in range(TT):
        dd = [hT_toks[tt], t_w2]
        if g_free is not None:
            dd.append(g_free)
        mm = p.op("pe", "matmul", psg[0:128, 0:8],
                  hT_sb[0:32, tt * 128 : (tt + 1) * 128], gw2_sb[:],
                  start=True, stop=False, deps=dd)
        mm = p.op("pe", "matmul", psg[0:128, 0:8], onesr_sb[:], gb2_sb[:],
                  start=False, stop=True, deps=[mm, t_or, t_b2])
        ev = p.op("dve", "tensor_copy", g_sb[:, tt * 8 : (tt + 1) * 8],
                  psg[0:128, 0:8], deps=[mm])
        g_free = ev
        dd = [ev] + ([m_free] if m_free else [])
        m1 = p.op("dve", "tensor_reduce", m_sb[:, 0:1],
                  g_sb[:, tt * 8 : (tt + 1) * 8], axis=AX.X, op=ALU.max, deps=dd)
        o1 = p.op("dve", "tensor_scalar", oh1_sb[:, tt * 8 : (tt + 1) * 8],
                  g_sb[:, tt * 8 : (tt + 1) * 8], m_sb[:, 0:1], None,
                  op0=ALU.is_equal, deps=[m1])
        dd = [o1] + ([msk_free] if msk_free else [])
        mk = p.op("dve", "scalar_tensor_tensor", msk_sb[:],
                  oh1_sb[:, tt * 8 : (tt + 1) * 8], -1e30,
                  g_sb[:, tt * 8 : (tt + 1) * 8], op0=ALU.mult, op1=ALU.add,
                  deps=dd)
        m2 = p.op("dve", "tensor_reduce", m_sb[:, 1:2], msk_sb[:], axis=AX.X,
                  op=ALU.max, deps=[mk])
        o2 = p.op("dve", "tensor_scalar", oh2_sb[:, tt * 8 : (tt + 1) * 8],
                  msk_sb[:], m_sb[:, 1:2], None, op0=ALU.is_equal, deps=[m2])
        msk_free = o2
        df = p.op("dve", "tensor_tensor", m_sb[:, 2:3], m_sb[:, 0:1],
                  m_sb[:, 1:2], op=ALU.subtract,
                  deps=[m2] + ([g1_toks[-1]] if g1_toks else []))
        m_free = df
        sg = p.op("act", "activation", g1_sb[:, tt : tt + 1], m_sb[:, 2:3],
                  AF.Sigmoid, deps=[df])
        g1_toks.append(sg)
        oh_toks.append((o1, o2))
        dd = [o1, t_oc]
        if pc1_mm is not None:
            dd.append(pc1_mm)
        pc1_mm = p.op("pe", "matmul", psc[0:1, 0:8], onesc_sb[:],
                      oh1_sb[:, tt * 8 : (tt + 1) * 8],
                      start=(tt == 0), stop=(tt == TT - 1), deps=dd)
        dd = [o2, t_oc]
        if pc2_mm is not None:
            dd.append(pc2_mm)
        pc2_mm = p.op("pe", "matmul", psd[0:1, 0:8], onesc_sb[:],
                      oh2_sb[:, tt * 8 : (tt + 1) * 8],
                      start=(tt == 0), stop=(tt == TT - 1), deps=dd)
    e1 = p.op("dve", "tensor_copy", pc_sb[:, 0:8], psc[0:1, 0:8],
              deps=[pc1_mm])
    e2 = p.op("dve", "tensor_copy", pc_sb[:, 8:16], psd[0:1, 0:8],
              deps=[pc2_mm, e1])
    s1 = p.dma("sp", oh1.rearrange("(t p) n -> p t n", p=128),
               oh1_sb[:].rearrange("p (t n) -> p t n", t=TT),
               deps=[t[0] for t in oh_toks])
    s2 = p.dma("sp", oh2.rearrange("(t p) n -> p t n", p=128),
               oh2_sb[:].rearrange("p (t n) -> p t n", t=TT),
               deps=[t[1] for t in oh_toks])
    s3 = p.dma("sp", g1o, g1_sb[:], deps=g1_toks)
    s4 = p.dma("sp", pco, pc_sb[:], deps=[e2])
    return [s1, s2, s3, s4]


def build_l1ffa(nc):
    p = Prog(nc)
    A = {}
    A["xhT"] = nc.dram_tensor("xhT", [D, NTOK], BF16, kind="ExternalInput").ap()
    A["gw1T"] = nc.dram_tensor("gw1T", [D, 32], F32, kind="ExternalInput").ap()
    A["gb1"] = nc.dram_tensor("gb1", [1, 32], F32, kind="ExternalInput").ap()
    A["gamma_bc"] = nc.dram_tensor("gamma_bc", [128, 32], F32, kind="ExternalInput").ap()
    A["beta_bc"] = nc.dram_tensor("beta_bc", [128, 32], F32, kind="ExternalInput").ap()
    A["gw2T"] = nc.dram_tensor("gw2T", [32, 8], F32, kind="ExternalInput").ap()
    A["gb2"] = nc.dram_tensor("gb2", [1, 8], F32, kind="ExternalInput").ap()
    A["oh1"] = nc.dram_tensor("oh1", [NTOK, 8], BF16, kind="ExternalOutput").ap()
    A["oh2"] = nc.dram_tensor("oh2", [NTOK, 8], BF16, kind="ExternalOutput").ap()
    A["g1"] = nc.dram_tensor("g1", [128, TT], F32, kind="ExternalOutput").ap()
    A["pc"] = nc.dram_tensor("pc", [1, 16], F32, kind="ExternalOutput").ap()
    toks = emit_l1ffa(p, nc, A, "", nc.alloc_sbuf_tensor, nc.alloc_psum_tensor)
    p.finish(toks)
    p.emit()
    return nc


CAPACITY = 3072.0
ALPHA = 1.0 / 16.0


def emit_l1ffb(p, nc, A, pfx, asb, aps_):
    xhT = A["xhT"]
    pcall = A.get("pcall")
    oh1 = A["oh1"]
    oh2 = A["oh2"]
    g1 = A["g1"]
    wT = A["wT"]
    lA = A["lA"]
    lB = A["lB"]
    xres = A["xres"]
    xout = A["xout"]

    xhT_sb = asb(pfx + "xhT_sb", [128, FC * NTOK], BF16)
    pc_sb = asb(pfx + "pc_sb", [8, 16], F32)
    cnt_sb = asb(pfx + "cnt_sb", [1, 16], F32)
    alw_sb = asb(pfx + "alw_sb", [1, 16], F32)
    abc_sb = asb(pfx + "abc_sb", [128, 16], F32)
    oh1_sb = asb(pfx + "oh1_sb", [128, TT * 8], BF16)
    oh2_sb = asb(pfx + "oh2_sb", [128, TT * 8], BF16)
    g1_sb = asb(pfx + "g1_sb", [128, TT], F32)
    g2_sb = asb(pfx + "g2_sb", [128, TT], F32)
    wg_sb = asb(pfx + "wg_sb", [128, TT * 8], F32)
    a_sb = asb(pfx + "a_sb", [128, 8], F32)
    lA_sb = asb(pfx + "lA_sb", [128, FC * 128], BF16)
    lB_sb = asb(pfx + "lB_sb", [128, FC * 128], BF16)
    tmp_sb = asb(pfx + "tmp_sb", [128, TT * 128], BF16)
    tmpT_sb = asb(pfx + "tmpT_sb", [128, TT * 128], BF16)
    wc_sb = [asb(pfx + f"wc_sb{i}", [128, FC * 512], BF16) for i in range(2)]
    xr_sb = [asb(pfx + f"xr_sb{i}", [128, 512], F32) for i in range(2)]
    xo_sb = [asb(pfx + f"xo_sb{i}", [128, 512], F32) for i in range(2)]
    onesf_sb = asb(pfx + "onesf_sb", [1, 128], F32)
    ones8_sb = asb(pfx + "ones8_sb", [8, 1], F32)

    ps0 = aps_(pfx + "ps0", [128, 512], F32)
    ps1 = aps_(pfx + "ps1", [128, 512], F32)
    ptps = aps_(pfx + "ptps", [128, 1024], BF16)

    ident, t_id = emit_ident(p, nc, asb, pfx + "ident")
    t_xh = p.dma("sp", xhT_sb[:].rearrange("p (c t) -> p c t", c=FC),
                 xhT.rearrange("(c p) t -> p c t", p=128))
    t_pc = p.dma("sp", pc_sb[:], pcall) if pcall is not None else None
    t_o1 = p.dma("sp", oh1_sb[:].rearrange("p (t n) -> p t n", t=TT),
                 oh1.rearrange("(t p) n -> p t n", p=128))
    t_o2 = p.dma("sp", oh2_sb[:].rearrange("p (t n) -> p t n", t=TT),
                 oh2.rearrange("(t p) n -> p t n", p=128))
    t_g1 = p.dma("sp", g1_sb[:], g1)
    t_lA = p.dma("pool", lA_sb[:].rearrange("p (c n) -> p c n", c=FC),
                 lA.rearrange("(c p) n -> p c n", p=128))
    t_lB = p.dma("pool", lB_sb[:], lB)
    t_of = p.op("pool", "memset", onesf_sb[:], 1.0)
    t_o8 = p.op("pool", "memset", ones8_sb[:], 1.0)

    if "pcr" in A:
        ev_c = p.dma("sp", cnt_sb[:], A["pcr"])
    else:
        mm_c = p.op("pe", "matmul", ps0[0:1, 0:16], ones8_sb[:], pc_sb[:],
                    start=True, stop=True, deps=[t_pc, t_o8])
        ev_c = p.op("dve", "tensor_copy", cnt_sb[:], ps0[0:1, 0:16], deps=[mm_c])
    alw = p.op("dve", "tensor_scalar", alw_sb[:], cnt_sb[:], CAPACITY, None,
               op0=ALU.is_le, deps=[ev_c])
    mm_b = p.op("pe", "matmul", ps0[0:128, 16:32], onesf_sb[:], alw_sb[:],
                start=True, stop=True, deps=[alw, t_of, ev_c])
    ev_b = p.op("dve", "tensor_copy", abc_sb[:], ps0[0:128, 16:32], deps=[mm_b])
    wg_toks = []
    a_free = None
    for tt in range(TT):
        g2 = p.op("dve", "tensor_scalar", g2_sb[:, tt : tt + 1],
                  g1_sb[:, tt : tt + 1], -1.0, 1.0, op0=ALU.mult, op1=ALU.add,
                  deps=[t_g1] + ([wg_toks[-1]] if wg_toks else []))
        dd = [t_o1, g2, ev_b]
        if a_free is not None:
            dd.append(a_free)
        a1 = p.op("dve", "tensor_scalar", a_sb[:], oh1_sb[:, tt * 8 : (tt + 1) * 8],
                  g1_sb[:, tt : tt + 1], None, op0=ALU.mult, deps=dd)
        a1b = p.op("dve", "tensor_tensor", wg_sb[:, tt * 8 : (tt + 1) * 8],
                   a_sb[:], abc_sb[:, 0:8], op=ALU.mult, deps=[a1])
        a2 = p.op("dve", "tensor_scalar", a_sb[:], oh2_sb[:, tt * 8 : (tt + 1) * 8],
                  g2_sb[:, tt : tt + 1], None, op0=ALU.mult, deps=[a1b, t_o2])
        a2b = p.op("dve", "tensor_tensor", a_sb[:], a_sb[:], abc_sb[:, 8:16],
                   op=ALU.mult, deps=[a2])
        a_free = a2b
        w = p.op("dve", "tensor_tensor", wg_sb[:, tt * 8 : (tt + 1) * 8],
                 wg_sb[:, tt * 8 : (tt + 1) * 8], a_sb[:], op=ALU.add,
                 deps=[a2b])
        wg_toks.append(w)
    tmp_toks = []
    ps_tok2 = None
    for tt in range(TT):
        last = None
        for c in range(FC):
            dd = [t_xh, t_lA] if last is None else [last]
            if last is None and ps_tok2 is not None:
                dd.append(ps_tok2)
            last = p.op("pe", "matmul", ps1[0:128, 0:128],
                        xhT_sb[:, c * NTOK + tt * 128 : c * NTOK + (tt + 1) * 128],
                        lA_sb[:, c * 128 : (c + 1) * 128],
                        start=(c == 0), stop=(c == FC - 1), deps=dd)
        sc_toks = []
        for e in range(8):
            sc = p.op("dve", "tensor_scalar",
                      tmp_sb[:, tt * 128 + e * 16 : tt * 128 + (e + 1) * 16],
                      ps1[0:128, e * 16 : (e + 1) * 16],
                      wg_sb[:, tt * 8 + e : tt * 8 + e + 1], ALPHA,
                      op0=ALU.mult, op1=ALU.mult, deps=[last, wg_toks[tt]])
            sc_toks.append(sc)
        ps_tok2 = sc_toks[-1]
        tmp_toks.append(sc_toks)
    pt_free = None
    tmpT_toks = []
    for tt in range(TT):
        dd = tmp_toks[tt] + [t_id] + ([pt_free] if pt_free else [])
        tr = p.op("pe", "transpose", ptps[0:128, 0:128],
                  tmp_sb[:, tt * 128 : (tt + 1) * 128], ident[:], deps=dd)
        ev = p.op("dve", "tensor_copy", tmpT_sb[:, tt * 128 : (tt + 1) * 128],
                  ptps[0:128, 0:128], deps=[tr])
        pt_free = ev
        tmpT_toks.append(ev)
    w_tok = [None, None]
    ps_tok = [ev_b, ps_tok2]
    xr_tok = [None, None]
    xo_tok = [None, None]
    st_toks = []
    for cb in range(8):
        wi = cb % 2
        dd = [] if w_tok[wi] is None else [w_tok[wi]]
        t_w = p.dma("pool", wc_sb[wi][:].rearrange("p (c n) -> p c n", c=FC),
                    wT[:, cb * 512 : (cb + 1) * 512].rearrange("(c p) n -> p c n", p=128),
                    deps=dd, stream=f"w{wi}")
        last_any = None
        for tt in range(TT):
            i = (cb * TT + tt) % 2
            dx = [] if xr_tok[i] is None else [xr_tok[i]]
            t_xr = p.dma("act", xr_sb[i][:],
                         xres[tt * 128 : (tt + 1) * 128, cb * 512 : (cb + 1) * 512],
                         deps=dx, stream=f"xr{i}")
            last = None
            for c in range(FC):
                dd = [t_w, t_xh] if last is None else [last]
                if last is None and ps_tok[i] is not None:
                    dd.append(ps_tok[i])
                last = p.op("pe", "matmul", (ps0, ps1)[i][0:128, 0:512],
                            xhT_sb[:, c * NTOK + tt * 128 : c * NTOK + (tt + 1) * 128],
                            wc_sb[wi][:, c * 512 : (c + 1) * 512],
                            start=(c == 0), stop=False, deps=dd)
            last = p.op("pe", "matmul", (ps0, ps1)[i][0:128, 0:512],
                        tmpT_sb[:, tt * 128 : (tt + 1) * 128],
                        lB_sb[:, cb * 512 : (cb + 1) * 512],
                        start=False, stop=True, deps=[last, tmpT_toks[tt], t_lB])
            last_any = last
            dadd = [last, t_xr]
            if xo_tok[i] is not None:
                dadd.append(xo_tok[i])
            add = p.op("dve", "tensor_tensor", xo_sb[i][:],
                       (ps0, ps1)[i][0:128, 0:512], xr_sb[i][:], op=ALU.add,
                       deps=dadd)
            ps_tok[i] = add
            xr_tok[i] = add
            st = p.dma("sp", xout[tt * 128 : (tt + 1) * 128, cb * 512 : (cb + 1) * 512],
                       xo_sb[i][:], deps=[add], stream=f"xo{i}")
            xo_tok[i] = st
            st_toks.append(st)
        w_tok[wi] = last_any
    return st_toks


def build_l1ffb(nc):
    p = Prog(nc)
    A = {}
    A["xhT"] = nc.dram_tensor("xhT", [D, NTOK], BF16, kind="ExternalInput").ap()
    A["pcall"] = nc.dram_tensor("pcall", [8, 16], F32, kind="ExternalInput").ap()
    A["oh1"] = nc.dram_tensor("oh1", [NTOK, 8], BF16, kind="ExternalInput").ap()
    A["oh2"] = nc.dram_tensor("oh2", [NTOK, 8], BF16, kind="ExternalInput").ap()
    A["g1"] = nc.dram_tensor("g1", [128, TT], F32, kind="ExternalInput").ap()
    A["wT"] = nc.dram_tensor("wT", [D, D], F32, kind="ExternalInput").ap()
    A["lA"] = nc.dram_tensor("lA", [D, 128], F32, kind="ExternalInput").ap()
    A["lB"] = nc.dram_tensor("lB", [128, D], F32, kind="ExternalInput").ap()
    A["xres"] = nc.dram_tensor("xres", [NTOK, D], F32, kind="ExternalInput").ap()
    A["xout"] = nc.dram_tensor("xout", [NTOK, D], F32, kind="ExternalOutput").ap()
    toks = emit_l1ffb(p, nc, A, "", nc.alloc_sbuf_tensor, nc.alloc_psum_tensor)
    p.finish(toks)
    p.emit()
    return nc



FUSED = False


def _run(build_fn, in_maps, out_names):
    nc = bass.Bass("TRN2", target_bir_lowering=False)
    build_fn(nc)
    nc.finalize()
    ins = set()
    for alloc in nc.m.functions[0].allocations:
        if isinstance(alloc, mybir.MemoryLocationSet) and alloc.kind == "ExternalInput":
            ins.add(alloc.memorylocations[0].name)
    in_maps = [{k: v for k, v in m.items() if k in ins} for m in in_maps]
    res = run_bass_kernel_spmd(nc, in_maps, core_ids=list(range(NCORES)))
    return [{k: r[k] for k in out_names} for r in res.results]




def emit_lnT(p, nc, asb, x_ap, xh_h, xhT_h, pfx):
    xh_sb = asb(pfx + "xh_sb", [128, TT * D], BF16)
    trb = [asb(pfx + f"trb{i}", [128, NTOK], BF16) for i in range(2)]
    toks, _, _ = emit_ln(p, nc, x_ap, xh_sb, TT, pfx + "ln", asb=asb)
    st = p.dma("sp", xh_h.rearrange("(t p) d -> p t d", p=128),
               xh_sb[:].rearrange("p (t d) -> p t d", t=TT), deps=toks,
               stream="st")
    tr_tok = [None, None]
    sts = []
    for j in range(FC):
        i = j % 2
        dd = [st] + ([tr_tok[i]] if tr_tok[i] else [])
        t_tr = p.dma("sp", trb[i][:], xh_h[:, j * 128 : (j + 1) * 128],
                     deps=dd, transpose=True, stream=f"tr{i}")
        t_st = p.dma("act", xhT_h[j * 128 : (j + 1) * 128, :], trb[i][:],
                     deps=[t_tr], stream=f"ts{i}")
        tr_tok[i] = t_st
        sts.append(t_st)
    return sts


def build_fused(nc):
    p = Prog(nc)
    EI = lambda n, shp, dt: nc.dram_tensor(n, shp, dt, kind="ExternalInput").ap()
    IN = lambda n, shp, dt: nc.dram_tensor(n, shp, dt, kind="Internal").ap()
    x = EI("x", [NTOK, D], F32)
    xout = nc.dram_tensor("xout", [NTOK, D], F32, kind="ExternalOutput").ap()
    W = {}
    for n, shp, dt in [
        ("WqdT", [12, D], F32), ("Wqk", [12, 128], F32), ("bqd", [12, 1], F32),
        ("bqk", [1, 128], F32), ("Wkvd", [D, 4], F32), ("bkv", [1, 4], F32),
        ("Wo", [D, D], F32), ("WvuT", [128, 128], F32), ("bvu", [128, FC], F32),
        ("bo", [1, D], F32), ("WstkT", [D, 24], F32), ("bases_bc", [128, 24], F32),
        ("gate", [1, 1], F32), ("Wbr", [1024, 1024], F32),
        ("wqkv", [D, 6144], F32), ("SDrep", [128, 4608], F32),
        ("scales_bc", [128, 16], F32), ("wo", [D, D], F32),
        ("gw1T", [D, 32], F32), ("gb1", [1, 32], F32),
        ("gamma_bc", [128, 32], F32), ("beta_bc", [128, 32], F32),
        ("gw2T", [32, 8], F32), ("gb2", [1, 8], F32),
        ("wT", [D, D], F32), ("lA", [D, 128], F32), ("lB", [128, D], F32),
    ]:
        W[n] = EI(n, shp, dt)
    xh_h = IN("xh_h", [NTOK, D], BF16)
    xhT_h = IN("xhT_h", [D, NTOK], BF16)
    x1_h = IN("x1_h", [NTOK, D], F32)
    x2_h = IN("x2_h", [NTOK, D], F32)
    x4_h = IN("x4_h", [NTOK, D], F32)
    qnat_h = IN("qnat_h", [NTOK, 128], BF16)
    qT_h = IN("qT_h", [128, NTOK], BF16)
    ckv_h = IN("ckv_h", [NTOK, 4], F32)
    ckv_g = IN("ckv_g", [B * S, 4], F32)
    wcomb_h = IN("wcomb_h", [128, D], BF16)
    qkvT_h = IN("qkvT_h", [6144, NTOK], BF16)
    qA2A = IN("qA2A", [S, NTOK], BF16)
    kA2A = IN("kA2A", [1024, NTOK], BF16)
    vA2A = IN("vA2A", [1024, NTOK], BF16)
    outn_s = IN("outn_s", [S, NTOK], BF16)
    ot_h = IN("ot_h", [D, NTOK], BF16)
    oh1_h = IN("oh1_h", [NTOK, 8], BF16)
    oh2_h = IN("oh2_h", [NTOK, 8], BF16)
    g1_h = IN("g1_h", [128, TT], F32)
    pc_h = IN("pc_h", [1, 16], F32)
    pcr_h = IN("pcr_h", [1, 16], F32)
    G4 = [[0, 1, 2, 3], [4, 5, 6, 7]]
    G8 = [list(range(8))]

    stack = [None]

    def asb(name, shape, dtype):
        return stack[0].enter_context(nc.sbuf_tensor(name, shape, dtype))

    def aps_(name, shape, dtype):
        return stack[0].enter_context(nc.psum_tensor(name, shape, dtype))

    def begin():
        stack[0] = contextlib.ExitStack()

    def end():
        p.barrier()
        stack[0].close()

    begin()
    emit_lnT(p, nc, asb, x, xh_h, xhT_h, "p1_")
    end()
    begin()
    A = dict(W)
    A.update(xhT=xhT_h, qnat=qnat_h, ckv=ckv_h)
    emit_l0proj(p, nc, A, "p2_", asb, aps_)
    p.barrier()
    qtb = asb("p2_qtb", [128, NTOK], BF16)
    t1 = p.dma("sp", qtb[:], qnat_h, transpose=True, stream="p2_qtr")
    p.dma("sp", qT_h[:, :], qtb[:], deps=[t1], stream="p2_qst")
    p.cc("AllGather", ALU.bypass, G8, [ckv_h], [ckv_g])
    end()
    begin()
    A = dict(W)
    A.update(qT=qT_h, ckv_g=ckv_g, xres=x, xout=x1_h, wcomb_scratch=wcomb_h)
    emit_l0core(p, nc, A, "p3_", asb, aps_)
    end()
    begin()
    emit_lnT(p, nc, asb, x1_h, xh_h, xhT_h, "p4_")
    end()
    begin()
    A = dict(W)
    A.update(xh2=xh_h, xh2T=xhT_h, xres=x1_h, xout=x2_h)
    emit_l0ffn(p, nc, A, "p5_", asb, aps_)
    end()
    begin()
    emit_lnT(p, nc, asb, x2_h, xh_h, xhT_h, "p6_")
    end()
    begin()
    A = dict(W)
    A.update(xhT=xhT_h, qkvT=qkvT_h)
    emit_l1qkv(p, nc, A, "p7_", asb, aps_)
    p.barrier()
    p.cc("AllToAll", ALU.bypass, G8, [qkvT_h[0:4096, :]], [qA2A])
    p.cc("AllToAll", ALU.bypass, G8, [qkvT_h[4096:5120, :]], [kA2A])
    p.cc("AllToAll", ALU.bypass, G8, [qkvT_h[5120:6144, :]], [vA2A])
    end()
    begin()
    A = dict(W)
    A.update(qA2A=qA2A, kA2A=kA2A, vA2A=vA2A, OUTN=outn_s)
    emit_l1att(p, nc, A, "p8_", asb, aps_)
    p.barrier()
    p.cc("AllToAll", ALU.bypass, G8, [outn_s], [ot_h])
    end()
    begin()
    A = dict(W)
    A.update(OT=ot_h, xres=x2_h, xout=x4_h)
    emit_l1wo(p, nc, A, "p9_", asb, aps_)
    end()
    begin()
    emit_lnT(p, nc, asb, x4_h, xh_h, xhT_h, "pg_")
    end()
    begin()
    A = dict(W)
    A.update(xhT=xhT_h, oh1=oh1_h, oh2=oh2_h, g1=g1_h, pc=pc_h)
    emit_l1ffa(p, nc, A, "pa_", asb, aps_)
    p.barrier()
    p.cc("AllReduce", ALU.add, G8, [pc_h], [pcr_h])
    end()
    begin()
    A = dict(W)
    A.update(xhT=xhT_h, pcr=pcr_h, oh1=oh1_h, oh2=oh2_h, g1=g1_h,
             xres=x4_h, xout=xout)
    toks = emit_l1ffb(p, nc, A, "pb_", asb, aps_)
    p.finish(toks)
    stack[0].close()
    p.emit()
    return nc




def _mk_phase_tools(nc, p):
    stack = [None]

    def asb(name, shape, dtype):
        return stack[0].enter_context(nc.sbuf_tensor(name, shape, dtype))

    def aps_(name, shape, dtype):
        return stack[0].enter_context(nc.psum_tensor(name, shape, dtype))

    def begin():
        stack[0] = contextlib.ExitStack()

    def end():
        p.barrier()
        stack[0].close()

    return asb, aps_, begin, end, stack


def _ext_weights(nc, names):
    shapes = {
        "WqdT": ([12, D], F32), "Wqk": ([12, 128], F32), "bqd": ([12, 1], F32),
        "bqk": ([1, 128], F32), "Wkvd": ([D, 4], F32), "bkv": ([1, 4], F32),
        "Wo": ([D, D], F32), "WvuT": ([128, 128], F32), "bvu": ([128, FC], F32),
        "bo": ([1, D], F32), "WstkT": ([D, 24], F32), "bases_bc": ([128, 24], F32),
        "gate": ([1, 1], F32), "Wbr": ([1024, 1024], F32),
        "wqkv": ([D, 6144], F32), "SDrep": ([128, 4608], F32),
        "scales_bc": ([128, 16], F32), "wo": ([D, D], F32),
        "gw1T": ([D, 32], F32), "gb1": ([1, 32], F32),
        "gamma_bc": ([128, 32], F32), "beta_bc": ([128, 32], F32),
        "gw2T": ([32, 8], F32), "gb2": ([1, 8], F32),
        "wT": ([D, D], F32), "lA": ([D, 128], F32), "lB": ([128, D], F32),
    }
    return {n: nc.dram_tensor(n, *shapes[n], kind="ExternalInput").ap() for n in names}


def build_seg1(nc):
    p = Prog(nc)
    x = nc.dram_tensor("x", [NTOK, D], F32, kind="ExternalInput").ap()
    qT = nc.dram_tensor("qT", [128, NTOK], BF16, kind="ExternalOutput").ap()
    ckv = nc.dram_tensor("ckv", [NTOK, 4], F32, kind="ExternalOutput").ap()
    qnat_h = nc.dram_tensor("qnat_h", [NTOK, 128], BF16, kind="Internal").ap()
    xh_h = nc.dram_tensor("xh_h", [NTOK, D], BF16, kind="Internal").ap()
    xhT_h = nc.dram_tensor("xhT_h", [D, NTOK], BF16, kind="Internal").ap()
    W = _ext_weights(nc, ["WqdT", "Wqk", "bqd", "bqk", "Wkvd", "bkv"])
    asb, aps_, begin, end, stack = _mk_phase_tools(nc, p)
    begin()
    emit_lnT(p, nc, asb, x, xh_h, xhT_h, "p1_")
    end()
    begin()
    A = dict(W)
    A.update(xhT=xhT_h, qnat=qnat_h, ckv=ckv)
    toks = emit_l0proj(p, nc, A, "p2_", asb, aps_)
    p.barrier()
    qtb = asb("p2_qtb", [128, NTOK], BF16)
    t1 = p.dma("sp", qtb[:], qnat_h, transpose=True, stream="p2qtr")
    t2 = p.dma("sp", qT[:, :], qtb[:], deps=[t1], stream="p2qst")
    p.finish(list(toks) + [t2])
    stack[0].close()
    p.emit()
    return nc


def build_seg2(nc):
    p = Prog(nc)
    x = nc.dram_tensor("x", [NTOK, D], F32, kind="ExternalInput").ap()
    qT_h = nc.dram_tensor("qT", [128, NTOK], BF16, kind="ExternalInput").ap()
    ckv_g = nc.dram_tensor("ckv_g", [B * S, 4], F32, kind="ExternalInput").ap()
    qkvT = nc.dram_tensor("qkvT", [6144, NTOK], BF16, kind="ExternalOutput").ap()
    x2o = nc.dram_tensor("x2o", [NTOK, D], F32, kind="ExternalOutput").ap()
    x1_h = nc.dram_tensor("x1_h", [NTOK, D], F32, kind="Internal").ap()
    xh_h = nc.dram_tensor("xh_h", [NTOK, D], BF16, kind="Internal").ap()
    xhT_h = nc.dram_tensor("xhT_h", [D, NTOK], BF16, kind="Internal").ap()
    wcomb_h = nc.dram_tensor("wcomb_h", [128, D], BF16, kind="Internal").ap()
    W = _ext_weights(nc, ["Wo", "WvuT", "bvu", "bo", "WstkT", "bases_bc",
                          "gate", "Wbr", "wqkv"])
    asb, aps_, begin, end, stack = _mk_phase_tools(nc, p)
    begin()
    A = dict(W)
    A.update(qT=qT_h, ckv_g=ckv_g, xres=x, xout=x1_h, wcomb_scratch=wcomb_h)
    emit_l0core(p, nc, A, "p3_", asb, aps_)
    end()
    begin()
    emit_lnT(p, nc, asb, x1_h, xh_h, xhT_h, "p4_")
    end()
    begin()
    A = dict(W)
    A.update(xh2=xh_h, xh2T=xhT_h, xres=x1_h, xout=x2o)
    t5 = emit_l0ffn(p, nc, A, "p5_", asb, aps_)
    end()
    begin()
    emit_lnT(p, nc, asb, x2o, xh_h, xhT_h, "p6_")
    end()
    begin()
    A = dict(W)
    A.update(xhT=xhT_h, qkvT=qkvT)
    t7 = emit_l1qkv(p, nc, A, "p7_", asb, aps_)
    p.finish(list(t5) + list(t7))
    stack[0].close()
    p.emit()
    return nc


def build_seg3(nc):
    p = Prog(nc)
    qA2A = nc.dram_tensor("qA2A", [S, NTOK], BF16, kind="ExternalInput").ap()
    kA2A = nc.dram_tensor("kA2A", [1024, NTOK], BF16, kind="ExternalInput").ap()
    vA2A = nc.dram_tensor("vA2A", [1024, NTOK], BF16, kind="ExternalInput").ap()
    outn = nc.dram_tensor("outn", [S, NTOK], BF16, kind="ExternalOutput").ap()
    W = _ext_weights(nc, ["SDrep", "scales_bc"])
    asb, aps_, begin, end, stack = _mk_phase_tools(nc, p)
    begin()
    A = dict(W)
    A.update(qA2A=qA2A, kA2A=kA2A, vA2A=vA2A, OUTN=outn)
    toks = emit_l1att(p, nc, A, "p8_", asb, aps_)
    p.finish(toks)
    stack[0].close()
    p.emit()
    return nc


def build_seg4(nc):
    p = Prog(nc)
    ot = nc.dram_tensor("ot", [D, NTOK], BF16, kind="ExternalInput").ap()
    x2 = nc.dram_tensor("x2", [NTOK, D], F32, kind="ExternalInput").ap()
    x4o = nc.dram_tensor("x4o", [NTOK, D], F32, kind="ExternalOutput").ap()
    xh4T = nc.dram_tensor("xh4T", [D, NTOK], BF16, kind="ExternalOutput").ap()
    oh1 = nc.dram_tensor("oh1", [NTOK, 8], BF16, kind="ExternalOutput").ap()
    oh2 = nc.dram_tensor("oh2", [NTOK, 8], BF16, kind="ExternalOutput").ap()
    g1 = nc.dram_tensor("g1", [128, TT], F32, kind="ExternalOutput").ap()
    pc = nc.dram_tensor("pc", [1, 16], F32, kind="ExternalOutput").ap()
    xh_h = nc.dram_tensor("xh_h", [NTOK, D], BF16, kind="Internal").ap()
    W = _ext_weights(nc, ["wo", "gw1T", "gb1", "gamma_bc", "beta_bc",
                          "gw2T", "gb2"])
    asb, aps_, begin, end, stack = _mk_phase_tools(nc, p)
    begin()
    A = dict(W)
    A.update(OT=ot, xres=x2, xout=x4o)
    emit_l1wo(p, nc, A, "p9_", asb, aps_)
    end()
    begin()
    emit_lnT(p, nc, asb, x4o, xh_h, xh4T, "pg_")
    end()
    begin()
    A = dict(W)
    A.update(xhT=xh4T, oh1=oh1, oh2=oh2, g1=g1, pc=pc)
    toks = emit_l1ffa(p, nc, A, "pa_", asb, aps_)
    p.finish(toks)
    stack[0].close()
    p.emit()
    return nc


def build_seg5(nc):
    p = Prog(nc)
    xh4T = nc.dram_tensor("xh4T", [D, NTOK], BF16, kind="ExternalInput").ap()
    pcall = nc.dram_tensor("pcall", [8, 16], F32, kind="ExternalInput").ap()
    oh1 = nc.dram_tensor("oh1", [NTOK, 8], BF16, kind="ExternalInput").ap()
    oh2 = nc.dram_tensor("oh2", [NTOK, 8], BF16, kind="ExternalInput").ap()
    g1 = nc.dram_tensor("g1", [128, TT], F32, kind="ExternalInput").ap()
    x4 = nc.dram_tensor("x4", [NTOK, D], F32, kind="ExternalInput").ap()
    xout = nc.dram_tensor("xout", [NTOK, D], F32, kind="ExternalOutput").ap()
    W = _ext_weights(nc, ["wT", "lA", "lB"])
    asb, aps_, begin, end, stack = _mk_phase_tools(nc, p)
    begin()
    A = dict(W)
    A.update(xhT=xh4T, pcall=pcall, oh1=oh1, oh2=oh2, g1=g1, xres=x4, xout=xout)
    toks = emit_l1ffb(p, nc, A, "pb_", asb, aps_)
    p.finish(toks)
    stack[0].close()
    p.emit()
    return nc


def _alibi_slopes(n):
    cp2 = 2 ** math.floor(math.log2(n))
    base = 2.0 ** (-8.0 / cp2)
    slopes = base ** np.arange(1, cp2 + 1, dtype=np.float32)
    if cp2 != n:
        eb = 2.0 ** (-4.0 / cp2)
        extra = eb ** np.arange(1, 1 + 2 * (n - cp2), 2, dtype=np.float32)
        slopes = np.concatenate([slopes, extra])
    return slopes.astype(np.float32)


def _sd_rep():
    pp = np.arange(128)[:, None]
    cc = np.arange(4608)[None, :]
    r = cc - 512 - pp
    return np.where(r >= 0, np.sqrt(np.maximum(r, 0).astype(np.float64)),
                    1e30).astype(np.float32)


def _kernel_unfused_obsolete(**inputs):
    asc = np.ascontiguousarray
    NH, NKV, Dh = 32, 8, 128

    x = np.asarray(inputs["L0_attn_x"], np.float32).reshape(B * S, D)
    xc = [shard_rows(x, c) for c in range(NCORES)]

    def run_ln(xcur):
        r = _run(build_ln_launch, [{"x_in": xcur[c]} for c in range(NCORES)],
                 ["xh_out"])
        xh = [r[c]["xh_out"] for c in range(NCORES)]
        xhT = [asc(h.T) for h in xh]
        return xh, xhT

    _, xh1T = run_ln(xc)
    WqdT = asc(np.asarray(inputs["L0_attn_Wq_d"], np.float32).T)
    com = {
        "WqdT": WqdT,
        "Wqk": np.asarray(inputs["L0_attn_W_qk"], np.float32),
        "bqd": asc(np.asarray(inputs["L0_attn_bq_d"], np.float32)[:, None]),
        "bqk": asc(np.asarray(inputs["L0_attn_b_qk"], np.float32)[None, :]),
        "Wkvd": np.asarray(inputs["L0_attn_Wkv_d"], np.float32),
        "bkv": asc(np.asarray(inputs["L0_attn_bkv_d"], np.float32)[None, :]),
    }
    r = _run(build_l0proj, [dict(com, xhT=xh1T[c]) for c in range(NCORES)],
             ["qnat", "ckv"])
    ckv_full = np.zeros((B * S, 4), np.float32)
    unshard_rows([r[c]["ckv"] for c in range(NCORES)], ckv_full)
    Wv_u = np.asarray(inputs["L0_attn_Wv_u"], np.float32)
    WvuT = asc(np.concatenate(
        [Wv_u[:, h * 128 : (h + 1) * 128].T for h in range(32)], axis=1))
    in3 = []
    for c in range(NCORES):
        b = c // 4
        cf = ckv_full[b * S : (b + 1) * S]
        ckvT32 = np.zeros((32, S), np.float32)
        ckvT32[0:4] = cf.T
        ckv5 = asc(np.concatenate([cf, np.ones((S, 1), np.float32)], 1))
        in3.append({
            "qT": asc(r[c]["qnat"].T),
            "ckvT32": ckvT32,
            "ckv5": ckv5,
            "Wo": np.asarray(inputs["L0_attn_Wo"], np.float32),
            "WvuT": WvuT,
            "bvu": asc(np.asarray(inputs["L0_attn_bv_u"], np.float32).reshape(32, 128).T),
            "bo": asc(np.asarray(inputs["L0_attn_bo"], np.float32)[None, :]),
            "xres": xc[c],
        })
    r = _run(build_l0core, in3, ["xout"])
    x1c = [r[c]["xout"] for c in range(NCORES)]

    xh2, xh2T = run_ln(x1c)
    WstkT = asc(np.concatenate(
        [np.asarray(inputs["L0_ffn_W_res"], np.float32),
         np.asarray(inputs["L0_ffn_W_pre"], np.float32),
         np.asarray(inputs["L0_ffn_W_post"], np.float32)], 0).T)
    bases = np.concatenate(
        [np.asarray(inputs["L0_ffn_H_res_base"], np.float32).reshape(-1),
         np.asarray(inputs["L0_ffn_H_pre_base"], np.float32),
         np.asarray(inputs["L0_ffn_H_post_base"], np.float32)]).astype(np.float32)
    bases_bc = asc(np.broadcast_to(bases[None, :], (128, 24)))
    com = {
        "WstkT": WstkT, "bases_bc": bases_bc,
        "gate": np.asarray(inputs["L0_ffn_gate"], np.float32).reshape(1, 1),
        "Wbr": np.asarray(inputs["L0_ffn_W_branch"], np.float32),
    }
    r = _run(build_l0ffn,
             [dict(com, xh2=xh2[c], xh2T=xh2T[c], xres=x1c[c])
              for c in range(NCORES)], ["xout"])
    x2c = [r[c]["xout"] for c in range(NCORES)]

    _, xh3T = run_ln(x2c)
    w_qkv = np.asarray(inputs["L1_attn_w_qkv"], np.float32)
    r = _run(build_l1qkv, [{"xhT": xh3T[c], "wqkv": w_qkv} for c in range(NCORES)],
             ["qkvT"])
    qkvTc = [r[c]["qkvT"] for c in range(NCORES)]
    qkvT_b = [np.zeros((6144, S), BF) for _ in range(B)]
    for c in range(NCORES):
        b = c // 4
        o = 0
        for (st, n) in core_rows_slices(c):
            qkvT_b[b][:, st - b * S : st - b * S + n] = qkvTc[c][:, o : o + n]
            o += n
    slopes = _alibi_slopes(NH)
    SDrep = _sd_rep()
    in_att = []
    for c in range(NCORES):
        b, s_ = c // 4, c % 4
        units = [s_ * 8 + u for u in range(8)]
        QTs = asc(qkvT_b[b][units[0] * 128 : (units[-1] + 1) * 128, :])
        kvh = [units[0] // 4, units[4] // 4]
        KTc = np.concatenate(
            [qkvT_b[b][NH * Dh + hh * Dh : NH * Dh + (hh + 1) * Dh, :] for hh in kvh], 0)
        Vc = asc(np.concatenate(
            [qkvT_b[b][NH * Dh + NKV * Dh + hh * Dh : NH * Dh + NKV * Dh + (hh + 1) * Dh, :]
             for hh in kvh], 0).T)
        scales_bc = np.zeros((128, 16), np.float32)
        for i, uu in enumerate(units):
            scales_bc[:, 2 * i] = 1.0 / (slopes[uu] * np.sqrt(Dh))
            scales_bc[:, 2 * i + 1] = slopes[uu]
        in_att.append({"QTs": QTs, "KTc": asc(KTc), "Vc": Vc, "SDrep": SDrep,
                       "scales_bc": scales_bc})
    r = _run(build_l1att, in_att, ["OUTN"])
    OUTN_b = [np.zeros((D, S), BF) for _ in range(B)]
    for c in range(NCORES):
        b, s_ = c // 4, c % 4
        OUTN_b[b][s_ * 8 * 128 : (s_ * 8 + 8) * 128, :] = r[c]["OUTN"]
    in_wo = []
    w_o = np.asarray(inputs["L1_attn_w_o"], np.float32)
    for c in range(NCORES):
        b = c // 4
        cols = []
        for (st, n) in core_rows_slices(c):
            cols.append(OUTN_b[b][:, st - b * S : st - b * S + n])
        in_wo.append({"OT": asc(np.concatenate(cols, 1)), "wo": w_o,
                      "xres": x2c[c]})
    r = _run(build_l1wo, in_wo, ["xout"])
    x4c = [r[c]["xout"] for c in range(NCORES)]

    _, xh4T = run_ln(x4c)
    gw1 = np.asarray(inputs["L1_ffn_gw1"], np.float32)
    com = {
        "gw1T": asc(gw1.T),
        "gb1": asc(np.asarray(inputs["L1_ffn_gb1"], np.float32)[None, :]),
        "gamma_bc": asc(np.broadcast_to(
            np.asarray(inputs["L1_ffn_ln_gamma"], np.float32)[None, :], (128, 32))),
        "beta_bc": asc(np.broadcast_to(
            np.asarray(inputs["L1_ffn_ln_beta"], np.float32)[None, :], (128, 32))),
        "gw2T": asc(np.asarray(inputs["L1_ffn_gw2"], np.float32).T),
        "gb2": asc(np.asarray(inputs["L1_ffn_gb2"], np.float32)[None, :]),
    }
    r = _run(build_l1ffa, [dict(com, xhT=xh4T[c]) for c in range(NCORES)],
             ["oh1", "oh2", "g1", "pc"])
    pcall = asc(np.concatenate([r[c]["pc"] for c in range(NCORES)], 0))
    lora_A = np.asarray(inputs["L1_ffn_lora_A"], np.float32)
    lora_B = np.asarray(inputs["L1_ffn_lora_B"], np.float32)
    lA_cat = asc(np.concatenate([lora_A[e] for e in range(8)], axis=1))
    lB_cat = asc(np.concatenate([lora_B[e] for e in range(8)], axis=0))
    wT = asc(np.asarray(inputs["L1_ffn_weight"], np.float32).T)
    in_b = [{"xhT": xh4T[c], "pcall": pcall, "oh1": r[c]["oh1"],
             "oh2": r[c]["oh2"], "g1": r[c]["g1"], "wT": wT, "lA": lA_cat,
             "lB": lB_cat, "xres": x4c[c]} for c in range(NCORES)]
    r = _run(build_l1ffb, in_b, ["xout"])
    out = np.zeros((B * S, D), np.float32)
    unshard_rows([r[c]["xout"] for c in range(NCORES)], out)
    return out.reshape(B, S, D)


def kernel(**inputs):
    asc = np.ascontiguousarray
    NH, Dh = 32, 128
    x = np.asarray(inputs["L0_attn_x"], np.float32).reshape(B * S, D)
    xc = [shard_rows(x, c) for c in range(NCORES)]
    slopes = _alibi_slopes(NH)
    SDrep = _sd_rep()
    Wv_u = np.asarray(inputs["L0_attn_Wv_u"], np.float32)
    W = {
        "WqdT": asc(np.asarray(inputs["L0_attn_Wq_d"], np.float32).T),
        "Wqk": np.asarray(inputs["L0_attn_W_qk"], np.float32),
        "bqd": asc(np.asarray(inputs["L0_attn_bq_d"], np.float32)[:, None]),
        "bqk": asc(np.asarray(inputs["L0_attn_b_qk"], np.float32)[None, :]),
        "Wkvd": np.asarray(inputs["L0_attn_Wkv_d"], np.float32),
        "bkv": asc(np.asarray(inputs["L0_attn_bkv_d"], np.float32)[None, :]),
        "Wo": np.asarray(inputs["L0_attn_Wo"], np.float32),
        "WvuT": asc(np.concatenate(
            [Wv_u[:, h * 128 : (h + 1) * 128].T for h in range(32)], axis=1)),
        "bvu": asc(np.asarray(inputs["L0_attn_bv_u"], np.float32).reshape(32, 128).T),
        "bo": asc(np.asarray(inputs["L0_attn_bo"], np.float32)[None, :]),
        "WstkT": asc(np.concatenate(
            [np.asarray(inputs["L0_ffn_W_res"], np.float32),
             np.asarray(inputs["L0_ffn_W_pre"], np.float32),
             np.asarray(inputs["L0_ffn_W_post"], np.float32)], 0).T),
        "bases_bc": asc(np.broadcast_to(np.concatenate(
            [np.asarray(inputs["L0_ffn_H_res_base"], np.float32).reshape(-1),
             np.asarray(inputs["L0_ffn_H_pre_base"], np.float32),
             np.asarray(inputs["L0_ffn_H_post_base"], np.float32)]
        ).astype(np.float32)[None, :], (128, 24))),
        "gate": np.asarray(inputs["L0_ffn_gate"], np.float32).reshape(1, 1),
        "Wbr": np.asarray(inputs["L0_ffn_W_branch"], np.float32),
        "wqkv": np.asarray(inputs["L1_attn_w_qkv"], np.float32),
        "SDrep": SDrep,
        "wo": np.asarray(inputs["L1_attn_w_o"], np.float32),
        "gw1T": asc(np.asarray(inputs["L1_ffn_gw1"], np.float32).T),
        "gb1": asc(np.asarray(inputs["L1_ffn_gb1"], np.float32)[None, :]),
        "gamma_bc": asc(np.broadcast_to(
            np.asarray(inputs["L1_ffn_ln_gamma"], np.float32)[None, :], (128, 32))),
        "beta_bc": asc(np.broadcast_to(
            np.asarray(inputs["L1_ffn_ln_beta"], np.float32)[None, :], (128, 32))),
        "gw2T": asc(np.asarray(inputs["L1_ffn_gw2"], np.float32).T),
        "gb2": asc(np.asarray(inputs["L1_ffn_gb2"], np.float32)[None, :]),
        "wT": asc(np.asarray(inputs["L1_ffn_weight"], np.float32).T),
        "lA": asc(np.concatenate(
            [np.asarray(inputs["L1_ffn_lora_A"], np.float32)[e] for e in range(8)],
            axis=1)),
        "lB": asc(np.concatenate(
            [np.asarray(inputs["L1_ffn_lora_B"], np.float32)[e] for e in range(8)],
            axis=0)),
    }
    scs = []
    for c in range(NCORES):
        sc = np.zeros((128, 16), np.float32)
        for u in range(8):
            g = u % 4
            sl = slopes[c * 4 + g]
            sc[:, 2 * u] = 1.0 / (sl * np.sqrt(Dh))
            sc[:, 2 * u + 1] = sl
        scs.append(sc)
    if FUSED:
        in_maps = [dict(W, x=xc[c], scales_bc=scs[c]) for c in range(NCORES)]
        nc = bass.Bass("TRN2", target_bir_lowering=False)
        build_fused(nc)
        nc.finalize()
        res = run_bass_kernel_spmd(nc, in_maps, core_ids=list(range(NCORES)))
        out = np.zeros((B * S, D), np.float32)
        unshard_rows([res.results[c]["xout"] for c in range(NCORES)], out)
        return out.reshape(B, S, D)

    r1 = _run(build_seg1, [dict(W, x=xc[c]) for c in range(NCORES)],
              ["qT", "ckv"])
    ckv_g = asc(np.concatenate([r1[c]["ckv"] for c in range(NCORES)], 0))
    r2 = _run(build_seg2,
              [dict(W, x=xc[c], qT=r1[c]["qT"], ckv_g=ckv_g)
               for c in range(NCORES)],
              ["qkvT", "x2o"])
    qkvTs = [r2[c]["qkvT"] for c in range(NCORES)]
    in3 = []
    for c in range(NCORES):
        qA2A = asc(np.concatenate(
            [qkvTs[r][c * 512 : (c + 1) * 512, :] for r in range(NCORES)], 0))
        kA2A = asc(np.concatenate(
            [qkvTs[r][4096 + c * 128 : 4096 + (c + 1) * 128, :] for r in range(NCORES)], 0))
        vA2A = asc(np.concatenate(
            [qkvTs[r][5120 + c * 128 : 5120 + (c + 1) * 128, :] for r in range(NCORES)], 0))
        in3.append(dict(W, qA2A=qA2A, kA2A=kA2A, vA2A=vA2A, scales_bc=scs[c]))
    r3 = _run(build_seg3, in3, ["outn"])
    in4 = []
    for c in range(NCORES):
        ot = asc(np.concatenate(
            [r3[r]["outn"][c * 512 : (c + 1) * 512, :] for r in range(NCORES)], 0))
        in4.append(dict(W, ot=ot, x2=r2[c]["x2o"]))
    r4 = _run(build_seg4, in4, ["x4o", "xh4T", "oh1", "oh2", "g1", "pc"])
    pcall = asc(np.concatenate([r4[c]["pc"] for c in range(NCORES)], 0))
    in5 = [dict(W, xh4T=r4[c]["xh4T"], pcall=pcall, oh1=r4[c]["oh1"],
                oh2=r4[c]["oh2"], g1=r4[c]["g1"], x4=r4[c]["x4o"])
           for c in range(NCORES)]
    r5 = _run(build_seg5, in5, ["xout"])
    out = np.zeros((B * S, D), np.float32)
    unshard_rows([r5[c]["xout"] for c in range(NCORES)], out)
    return out.reshape(B, S, D)


# version 13
# speedup vs baseline: 1.1600x; 1.1600x over previous
import sys
import os
sys.path.insert(0, "/opt/trn_rl_repo")
os.environ.setdefault("TRN_ALLOW_ICI_ALL_TO_ALL", "1")
import contextlib
import math
import numpy as np
import ml_dtypes
import concourse.bass as bass
import concourse.mybir as mybir
from concourse.bass_utils import run_bass_kernel_spmd

F32 = mybir.dt.float32
BF16 = mybir.dt.bfloat16
AF = mybir.ActivationFunctionType
ALU = mybir.AluOpType
AX = mybir.AxisListType
BF = ml_dtypes.bfloat16

D = 4096
S = 4096
B = 2
NTOK = 1024
TT = 8
FC = 32
CH = 512
LN_EPS = 1e-5
RMS_EPS = 1.1920929e-07
NCORES = 8

ENG_ORDER = ("sp", "pe", "dve", "act", "pool")
SECT = {"sp": "sync", "pe": "tensor", "dve": "vector", "act": "scalar", "pool": "gpsimd"}


def core_chunks(c):
    return [(0, c), (1, 7 - c)]


def core_rows_slices(c):
    return [(b * S + j * CH, CH) for (b, j) in core_chunks(c)]


def shard_rows(full, c):
    parts = [full[s : s + n] for (s, n) in core_rows_slices(c)]
    return np.ascontiguousarray(np.concatenate(parts, axis=0))


def unshard_rows(shards, out):
    for c, sh in enumerate(shards):
        o = 0
        for (s, n) in core_rows_slices(c):
            out[s : s + n] = sh[o : o + n]
            o += n
    return out


class Prog:

    def __init__(self, nc):
        self.nc = nc
        self.engs = {"pe": nc.tensor, "dve": nc.vector, "act": nc.scalar,
                     "pool": nc.gpsimd, "sp": nc.sync}
        self.sem = {}
        for s in ["pe", "dve", "act", "pool", "sp", "dsp", "dact", "dpool", "ddve"]:
            self.sem[s] = nc.alloc_semaphore("sem_" + s)
        self.cnt = {k: 0 for k in self.sem}
        self.plan = {e: [] for e in self.engs}
        self.wm = {e: {k: 0 for k in self.sem} for e in self.engs}
        self._cpool = nc.alloc_sbuf_tensor("const_pool", [128, 32], F32)
        self._consts = {}
        self._pool_auto = 0
        self._phase_auto = 0

    def const_col(self, value):
        v = float(value)
        if v not in self._consts:
            idx = len(self._consts)
            assert idx < 32
            tok = self.op("pool", "memset", self._cpool[:, idx : idx + 1], v)
            self._consts[v] = (self._cpool[:, idx : idx + 1], tok)
        return self._consts[v]

    def _waits(self, eng, deps):
        best = {}
        for (s, v) in deps:
            if v > best.get(s, 0):
                best[s] = v
        waits = []
        wm = self.wm[eng]
        for s, v in best.items():
            if v > wm[s]:
                waits.append((s, v))
                wm[s] = v
        return waits

    def op(self, eng, method, *args, deps=(), **kwargs):
        if eng == "pe":
            deps = [d for d in deps if d[0] != "pe"]
        waits = self._waits(eng, deps)
        self.cnt[eng] += 1
        self.plan[eng].append((method, args, kwargs, waits, [(eng, 1)]))
        return (eng, self.cnt[eng])

    def dma(self, eng, out, in_, deps=(), transpose=False, stream=None, **kwargs):
        assert eng in ("sp", "act", "pool")
        waits = self._waits(eng, deps)
        if stream is None:
            stream = f"_a{eng}{self._phase_auto}"
            self._phase_auto += 1
        d = "dma_" + stream
        if d not in self.sem:
            self.sem[d] = self.nc.alloc_semaphore("sem_" + d)
            self.cnt[d] = 0
            for e in self.wm:
                self.wm[e][d] = 0
        self.cnt[d] += 16
        meth = "dma_start_transpose" if transpose else "dma_start"
        self.plan[eng].append((meth, (out, in_), kwargs, waits, [(d, 16)]))
        return (d, self.cnt[d])

    def cc(self, kind, op, groups, ins, outs, deps=()):
        waits = self._waits("pool", deps)
        d = f"cc_{self._pool_auto}"
        self._pool_auto += 1
        self.sem[d] = self.nc.alloc_semaphore("sem_" + d)
        self.cnt[d] = 16
        for e in self.wm:
            self.wm[e][d] = 0
        self.plan["pool"].append((
            "collective_compute", (kind, op),
            {"replica_groups": groups, "ins": ins, "outs": outs},
            waits, [(d, 16)]))
        return (d, 16)

    def barrier(self):
        self._phase_auto = 0
        snapshot = [(s, self.cnt[s]) for s in self.cnt if self.cnt[s] > 0]
        for e in self.engs:
            waits = []
            wm = self.wm[e]
            for (s, v) in snapshot:
                if s != e and v > wm[s]:
                    waits.append((s, v))
                    wm[s] = v
            if waits:
                self.plan[e].append((None, (), {}, waits, []))

    def finish(self, final_deps):
        waits = self._waits("sp", final_deps)
        if waits:
            self.plan["sp"].append((None, (), {}, waits, []))

    def emit(self):
        nc = self.nc
        with nc.Block() as blk:
            for ename in ENG_ORDER:
                ops = self.plan[ename]
                if not ops:
                    continue
                sect = getattr(blk, SECT[ename])

                def make_body(body_ops):
                    def body(e):
                        for (method, args, kwargs, waits, incs) in body_ops:
                            for (s, v) in waits:
                                e.wait_ge(self.sem[s], v)
                            if method is None:
                                continue
                            instr = getattr(e, method)(*args, **kwargs)
                            for (s, amt) in incs:
                                instr.then_inc(self.sem[s], amt)
                    return body

                sect(make_body(ops))



def emit_ln(p, nc, x_ap, out_sb, n_tiles, sb_prefix, eps=LN_EPS, rms=False,
            dma_eng="sp", x_deps=(), asb=None):
    asb = asb or nc.alloc_sbuf_tensor
    """LayerNorm (or RMS-stat) over last dim 4096 of token rows.

    x_ap: DRAM [n_tiles*128, 4096] f32. out_sb: SBUF [128, n_tiles*4096] BF16.
    Returns (apply_tokens per tile, mv_sb, stat tokens).
    If rms=True computes 1/sqrt(mean(x^2)+eps) instead and applies ONLY the
    stats (returns rinv column tokens), still writing (x-m)*r for rms=False.
    """
    xt = asb(sb_prefix + "_x", [128, n_tiles * D], F32)
    st = asb(sb_prefix + "_st", [128, 48], F32)
    mv = asb(sb_prefix + "_mv", [128, 2 * n_tiles], F32)
    rc = asb(sb_prefix + "_rc", [128, 2 * n_tiles], F32)
    toks = []
    t_ld = p.dma(dma_eng, xt[:].rearrange("p (t d) -> p t d", t=n_tiles),
                 x_ap.rearrange("(t p) d -> p t d", p=128), deps=x_deps)
    prev_aggr = None
    for t in range(n_tiles):
        stats = []
        for j in range(8):
            dep = [t_ld]
            if prev_aggr is not None and t >= 1 and j == 0:
                pass
            stats.append(p.op("dve", "bn_stats", st[:, 6 * j : 6 * j + 6],
                              xt[:, t * D + 512 * j : t * D + 512 * (j + 1)],
                              deps=[t_ld] + ([prev_aggr] if prev_aggr else [])))
        aggr = p.op("dve", "bn_aggr", mv[:, 2 * t : 2 * t + 2], st[:], deps=stats)
        prev_aggr = aggr
        if rms:
            t1 = p.op("dve", "tensor_tensor", rc[:, 2 * t : 2 * t + 1],
                      mv[:, 2 * t : 2 * t + 1], mv[:, 2 * t : 2 * t + 1],
                      op=ALU.mult, deps=[aggr])
            t2 = p.op("dve", "tensor_tensor", rc[:, 2 * t : 2 * t + 1],
                      rc[:, 2 * t : 2 * t + 1], mv[:, 2 * t + 1 : 2 * t + 2],
                      op=ALU.add, deps=[aggr, t1])
            eap, etok = p.const_col(eps)
            t3 = p.op("act", "activation", rc[:, 2 * t + 1 : 2 * t + 2],
                      rc[:, 2 * t : 2 * t + 1], AF.Ln, bias=eap, scale=1.0,
                      deps=[t2, etok])
            t4 = p.op("act", "activation", rc[:, 2 * t : 2 * t + 1],
                      rc[:, 2 * t + 1 : 2 * t + 2], AF.Exp, scale=-0.5,
                      deps=[t3])
            toks.append((t4, rc[:, 2 * t : 2 * t + 1]))
        else:
            eap, etok = p.const_col(eps)
            t3 = p.op("act", "activation", rc[:, 2 * t + 1 : 2 * t + 2],
                      mv[:, 2 * t + 1 : 2 * t + 2], AF.Ln, bias=eap, scale=1.0,
                      deps=[aggr, etok])
            t4 = p.op("act", "activation", rc[:, 2 * t : 2 * t + 1],
                      rc[:, 2 * t + 1 : 2 * t + 2], AF.Exp, scale=-0.5,
                      deps=[t3])
            ap = p.op("dve", "tensor_scalar", out_sb[:, t * D : (t + 1) * D],
                      xt[:, t * D : (t + 1) * D],
                      mv[:, 2 * t : 2 * t + 1], rc[:, 2 * t : 2 * t + 1],
                      op0=ALU.subtract, op1=ALU.mult, deps=[aggr, t4])
            toks.append(ap)
    return toks, xt, t_ld


def build_ln_launch(nc):
    p = Prog(nc)
    x = nc.dram_tensor("x_in", [NTOK, D], F32, kind="ExternalInput").ap()
    xh = nc.dram_tensor("xh_out", [NTOK, D], BF16, kind="ExternalOutput").ap()
    xh_sb = nc.alloc_sbuf_tensor("xh_sb", [128, TT * D], BF16)
    toks, _, _ = emit_ln(p, nc, x, xh_sb, TT, "ln")
    st = p.dma("sp", xh.rearrange("(t p) d -> p t d", p=128),
               xh_sb[:].rearrange("p (t d) -> p t d", t=TT), deps=toks)
    p.finish([st])
    p.emit()
    return nc


def emit_ident(p, nc, asb=None, name="ident", dtype=BF16):
    asb = asb or nc.alloc_sbuf_tensor
    ident = asb(name, [128, 128], dtype)
    t0 = p.op("pool", "memset", ident[:], 0.0)
    t1 = p.op("pool", "affine_select", out=ident[:], in_=ident[:],
              compare_op=ALU.not_equal, fill=1.0, base=0,
              pattern=[[-1, 128]], channel_multiplier=1, deps=[t0])
    return ident, t1


def emit_ones(p, nc, asb, name, shape, dtype=BF16):
    t = asb(name, shape, dtype)
    tok = p.op("pool", "memset", t[:], 1.0)
    return t, tok


def emit_l0proj(p, nc, A, pfx, asb, aps_):
    xhT = A["xhT"]
    WqdT = A["WqdT"]
    Wqk = A["Wqk"]
    bqd = A["bqd"]
    bqk = A["bqk"]
    Wkvd = A["Wkvd"]
    bkv = A["bkv"]
    qnat = A["qnat"]
    ckv = A["ckv"]

    xhT_sb = asb(pfx + "xhT_sb", [128, FC * NTOK], BF16)
    wqdT_sb = asb(pfx + "wqdT_sb", [12, D], BF16)
    wqk_sb = asb(pfx + "wqk_sb", [12, 128], BF16)
    bqd_sb = asb(pfx + "bqd_sb", [12, 1], BF16)
    bqk_sb = asb(pfx + "bqk_sb", [1, 128], F32)
    wkvd_sb = asb(pfx + "wkvd_sb", [128, FC * 4], BF16)
    bkv_sb = asb(pfx + "bkv_sb", [1, 4], BF16)
    wqc_sb = asb(pfx + "wqc_sb", [128, FC * 128], BF16)
    bqc_sb = asb(pfx + "bqc_sb", [1, 128], F32)
    bqcb_sb = asb(pfx + "bqcb_sb", [1, 128], BF16)
    qnat_sb = asb(pfx + "qnat_sb", [128, TT * 128], BF16)
    ckv_sb = asb(pfx + "ckv_sb", [128, TT * 4], F32)

    ps0 = aps_(pfx + "ps0", [128, 512], F32)
    ps1 = aps_(pfx + "ps1", [128, 512], F32)
    ps2 = aps_(pfx + "ps2", [128, 512], F32)

    t_xh = p.dma("sp", xhT_sb[:].rearrange("p (c t) -> p c t", c=FC),
                 xhT.rearrange("(c p) t -> p c t", p=128))
    t_wqd = p.dma("pool", wqdT_sb[:], WqdT)
    t_wqk = p.dma("pool", wqk_sb[:], Wqk)
    t_bqd = p.dma("pool", bqd_sb[:], bqd)
    t_bqk = p.dma("sp", bqk_sb[:], bqk)
    t_wkv = p.dma("pool", wkvd_sb[:].rearrange("p (c n) -> p c n", c=FC),
                  Wkvd.rearrange("(c p) n -> p c n", p=128))
    t_bkv = p.dma("pool", bkv_sb[:], bkv)
    ones_row, t_ones = emit_ones(p, nc, asb, pfx + "ones_row", [1, 128])

    wqc_toks = []
    ps_tok = [None, None]
    for c in range(FC):
        ps = (ps0, ps1)[c % 2]
        deps = [t_wqd, t_wqk]
        if ps_tok[c % 2] is not None:
            deps.append(ps_tok[c % 2])
        mm = p.op("pe", "matmul", ps[0:128, 0:128], wqdT_sb[:, c * 128 : (c + 1) * 128],
                  wqk_sb[:], start=True, stop=True, deps=deps)
        ev = p.op("dve", "tensor_copy", wqc_sb[:, c * 128 : (c + 1) * 128],
                  ps[0:128, 0:128], deps=[mm])
        ps_tok[c % 2] = ev
        wqc_toks.append(ev)
    mm_b = p.op("pe", "matmul", ps2[0:1, 0:128], bqd_sb[:], wqk_sb[:],
                start=True, stop=True, deps=[t_bqd, t_wqk])
    ev_b = p.op("dve", "tensor_copy", bqc_sb[:], ps2[0:1, 0:128], deps=[mm_b, t_bqk])
    ad_b = p.op("dve", "tensor_tensor", bqc_sb[:], bqc_sb[:], bqk_sb[:],
                op=ALU.add, deps=[ev_b])
    cv_b = p.op("dve", "tensor_copy", bqcb_sb[:], bqc_sb[:], deps=[ad_b])

    q_toks, k_toks = [], []
    for tt in range(TT):
        psq = (ps0, ps1)[tt % 2]
        deps = [t_xh, wqc_toks[-1]]
        if tt >= 2:
            deps.append(q_toks[tt - 2][0])
        last = None
        for c in range(FC):
            dd = list(deps) if last is None else [last]
            last = p.op("pe", "matmul", psq[0:128, 0:128],
                        xhT_sb[:, c * NTOK + tt * 128 : c * NTOK + (tt + 1) * 128],
                        wqc_sb[:, c * 128 : (c + 1) * 128],
                        start=(c == 0), stop=False, deps=dd)
        last = p.op("pe", "matmul", psq[0:128, 0:128], ones_row[:],
                    bqcb_sb[:], start=False, stop=True, deps=[last, t_ones, cv_b])
        ev = p.op("dve", "tensor_copy", qnat_sb[:, tt * 128 : (tt + 1) * 128],
                  psq[0:128, 0:128], deps=[last])
        q_toks.append((ev, last))

        deps2 = [t_xh, t_wkv, ev_b if tt == 0 else k_toks[tt - 1][0]]
        last2 = None
        for c in range(FC):
            dd = list(deps2) if last2 is None else [last2]
            last2 = p.op("pe", "matmul", ps2[0:128, 0:4],
                         xhT_sb[:, c * NTOK + tt * 128 : c * NTOK + (tt + 1) * 128],
                         wkvd_sb[:, c * 4 : (c + 1) * 4],
                         start=(c == 0), stop=False, deps=dd)
        last2 = p.op("pe", "matmul", ps2[0:128, 0:4], ones_row[:], bkv_sb[:],
                     start=False, stop=True, deps=[last2, t_ones, t_bkv])
        ev2 = p.op("dve", "tensor_copy", ckv_sb[:, tt * 4 : (tt + 1) * 4],
                   ps2[0:128, 0:4], deps=[last2])
        k_toks.append((ev2, last2))

    s1 = p.dma("sp", qnat.rearrange("(t p) n -> p t n", p=128),
               qnat_sb[:].rearrange("p (t n) -> p t n", t=TT),
               deps=[t[0] for t in q_toks])
    s2 = p.dma("sp", ckv.rearrange("(t p) n -> p t n", p=128),
               ckv_sb[:].rearrange("p (t n) -> p t n", t=TT),
               deps=[t[0] for t in k_toks])
    return [s1, s2]


def build_l0proj(nc):
    p = Prog(nc)
    A = {}
    A["xhT"] = nc.dram_tensor("xhT", [D, NTOK], BF16, kind="ExternalInput").ap()
    A["WqdT"] = nc.dram_tensor("WqdT", [12, D], F32, kind="ExternalInput").ap()
    A["Wqk"] = nc.dram_tensor("Wqk", [12, 128], F32, kind="ExternalInput").ap()
    A["bqd"] = nc.dram_tensor("bqd", [12, 1], F32, kind="ExternalInput").ap()
    A["bqk"] = nc.dram_tensor("bqk", [1, 128], F32, kind="ExternalInput").ap()
    A["Wkvd"] = nc.dram_tensor("Wkvd", [D, 4], F32, kind="ExternalInput").ap()
    A["bkv"] = nc.dram_tensor("bkv", [1, 4], F32, kind="ExternalInput").ap()
    A["qnat"] = nc.dram_tensor("qnat", [NTOK, 128], BF16, kind="ExternalOutput").ap()
    A["ckv"] = nc.dram_tensor("ckv", [NTOK, 4], F32, kind="ExternalOutput").ap()
    toks = emit_l0proj(p, nc, A, "", nc.alloc_sbuf_tensor, nc.alloc_psum_tensor)
    p.finish(toks)
    p.emit()
    return nc


L0_SHIFT = 20.0


def emit_l0core(p, nc, A, pfx, asb, aps_):
    qT = A["qT"]
    ckvT32 = A.get("ckvT32")
    ckv5 = A.get("ckv5")
    Wo = A["Wo"]
    WvuT = A["WvuT"]
    bvu = A["bvu"]
    bo = A["bo"]
    xres = A["xres"]
    xout = A["xout"]
    wcomb_h = A["wcomb_scratch"]

    nkb2 = 64 if "ckv_g" in A else 32
    ckvT_sb = asb(pfx + "ckvT_sb", [32, nkb2 * 128], BF16)
    ckv5_sb = asb(pfx + "ckv5_sb", [128, nkb2 * 5], BF16)
    wvuT_sb = asb(pfx + "wvuT_sb", [128, 128], BF16)
    bvu_sb = asb(pfx + "bvu_sb", [128, FC], BF16)
    bo_sb = asb(pfx + "bo_sb", [1, D], F32)
    wo_sb = [asb(pfx + f"wo_sb{i}", [128, 32 * 512], BF16) for i in range(2)]
    wcs_sb = asb(pfx + "wcs_sb", [4, 512], BF16)
    bcomb_sb = asb(pfx + "bcomb_sb", [1, D], F32)
    bcombb_sb = asb(pfx + "bcombb_sb", [1, D], BF16)
    wcomb_sb = asb(pfx + "wcomb_sb", [128, D], BF16)
    q4h_sb = [asb(pfx + f"q4h_sb{i}", [32, NTOK], BF16) for i in range(2)]
    attnT_sb = asb(pfx + "attnT_sb", [128, 2048], BF16)
    o5_sb = asb(pfx + "o5_sb", [5, 512], BF16)
    ot_sb = asb(pfx + "ot_sb", [128, 20], F32)
    rcol_sb = asb(pfx + "rcol_sb", [128, 2], F32)
    oall_sb = asb(pfx + "oall_sb", [128, TT * 128], BF16)
    otT_sb = asb(pfx + "otT_sb", [128, TT * 128], BF16)
    xr_sb = [asb(pfx + f"xr_sb{i}", [128, 512], F32) for i in range(2)]
    xo_sb = [asb(pfx + f"xo_sb{i}", [128, 512], F32) for i in range(2)]

    psA = aps_(pfx + "psA", [128, 2048], F32)
    psO = aps_(pfx + "psO", [128, 512], F32)
    psOut = [aps_(pfx + f"psOut{i}", [128, 512], F32) for i in range(2)]
    psW = psOut[0]
    psB = psOut[1]
    ptps = aps_(pfx + "ptps", [128, 1024], BF16)

    ident, t_id = emit_ident(p, nc, asb, pfx + "ident")
    ones_row, t_ones = emit_ones(p, nc, asb, pfx + "ones_row", [1, 128])
    negB, t_negB = p.const_col(-L0_SHIFT)

    ckv_ev = None
    if "ckv_g" in A:
        ckv_g = A["ckv_g"]
        ckv_q = asb(pfx + "ckv_q", [128, 64 * 4], BF16)
        m5 = p.op("pool", "memset", ckv5_sb[:], 1.0)
        t5toks, tqtoks = [], []
        for b2 in range(2):
            for j in range(8):
                srcr = j if b2 == 0 else 7 - j
                r0 = srcr * 1024 + b2 * 512
                k0 = b2 * 32 + j * 4
                t5toks.append(p.dma(
                    "pool",
                    ckv5_sb[:].rearrange("p (k c) -> p k c", k=64)[:, k0 : k0 + 4, 0:4],
                    ckv_g[r0 : r0 + 512, :].rearrange("(k p) c -> p k c", p=128),
                    deps=[m5], stream="ckv5l"))
                tqtoks.append(p.dma(
                    "pool",
                    ckv_q[:].rearrange("p (k c) -> p k c", k=64)[:, k0 : k0 + 4, :],
                    ckv_g[r0 : r0 + 512, :].rearrange("(k p) c -> p k c", p=128),
                    stream="ckvql"))
        t_ckv5 = t5toks[-1]
        for tok in t5toks:
            pass
        mT = p.op("pool", "memset", ckvT_sb[:], 0.0)
        for kb in range(64):
            dtr = tqtoks + [t_id] + ([ckv_ev] if ckv_ev else [])
            tr = p.op("pe", "transpose", ptps[0:4, 0:128],
                      ckv_q[:, kb * 4 : (kb + 1) * 4], ident[:], deps=dtr)
            ckv_ev = p.op("dve", "tensor_copy",
                          ckvT_sb[0:4, kb * 128 : (kb + 1) * 128],
                          ptps[0:4, 0:128], deps=[tr, mT])
        t_ckvT = ckv_ev
        t_ckv5 = t5toks[-1]
    else:
        t_ckvT = p.dma("pool", ckvT_sb[:], ckvT32)
        t_ckv5 = p.dma("pool", ckv5_sb[:].rearrange("p (k c) -> p k c", k=32),
                       ckv5.rearrange("(k p) c -> p k c", p=128))
    t_wvuT = p.dma("pool", wvuT_sb[:], WvuT)
    t_bvu = p.dma("pool", bvu_sb[:], bvu)
    t_bo = p.dma("sp", bo_sb[:], bo)

    wc_state = {"dma": [], "psB_free": None, "wc_ev": None, "wcs_free": None,
                "wo_tok": [None, None], "bdone": None}

    def emit_wcomb_cb(cb):
        wi = cb % 2
        deps_w = [] if wc_state["wo_tok"][wi] is None else [wc_state["wo_tok"][wi]]
        t_wo = p.dma("pool", wo_sb[wi][:].rearrange("p (c n) -> p c n", c=32),
                     Wo[:, cb * 512 : (cb + 1) * 512].rearrange(
                         "(c p) n -> p c n", p=128),
                     deps=deps_w, stream=f"wo{wi}")
        mmb_last = None
        for h in range(32):
            dw = [t_wo, t_wvuT]
            if wc_state["wc_ev"] is not None:
                dw.append(wc_state["wc_ev"])
            mmw = p.op("pe", "matmul", psW[0:4, 0:512],
                       wvuT_sb[:, 4 * h : 4 * h + 4],
                       wo_sb[wi][:, h * 512 : (h + 1) * 512],
                       start=True, stop=True, deps=dw)
            de = [mmw]
            if wc_state["wcs_free"] is not None:
                de.append(wc_state["wcs_free"])
            wc_ev = p.op("dve", "tensor_copy", wcs_sb[:], psW[0:4, 0:512], deps=de)
            wc_state["wc_ev"] = wc_ev
            t_st = p.dma("sp", wcomb_h[4 * h : 4 * h + 4, cb * 512 : (cb + 1) * 512],
                         wcs_sb[:], deps=[wc_ev], stream="wcs")
            wc_state["wcs_free"] = t_st
            wc_state["dma"].append(t_st)
            db = [t_bvu]
            if mmb_last is not None:
                db.append(mmb_last)
            if h == 0 and wc_state["psB_free"] is not None:
                db.append(wc_state["psB_free"])
            mmb_last = p.op("pe", "matmul", psB[0:1, 0:512],
                            bvu_sb[:, h : h + 1],
                            wo_sb[wi][:, h * 512 : (h + 1) * 512],
                            start=(h == 0), stop=(h == 31), deps=db)
        wc_state["wo_tok"][wi] = mmb_last
        evb = p.op("dve", "tensor_copy", bcomb_sb[:, cb * 512 : (cb + 1) * 512],
                   psB[0:1, 0:512], deps=[mmb_last, t_bo])
        wc_state["psB_free"] = evb
        adb = p.op("dve", "tensor_tensor", bcomb_sb[:, cb * 512 : (cb + 1) * 512],
                   bcomb_sb[:, cb * 512 : (cb + 1) * 512],
                   bo_sb[:, cb * 512 : (cb + 1) * 512], op=ALU.add, deps=[evb])
        wc_state["bdone"] = p.op(
            "dve", "tensor_copy", bcombb_sb[:, cb * 512 : (cb + 1) * 512],
            bcomb_sb[:, cb * 512 : (cb + 1) * 512], deps=[adb])

    q4h_init = [p.op("pool", "memset", q4h_sb[i][:], 0.0) for i in range(2)]
    q4h_user = [q4h_init[0], q4h_init[1]]
    att_user = [None] * 4
    psA_user = [None] * 4
    psO_free = None
    o5_free = None
    ev_ot = ckv_ev
    nrm = None
    oall_toks = []
    for h in range(32):
        bi = h % 2
        t_q = p.dma("sp", q4h_sb[bi][0:4, :], qT[4 * h : 4 * h + 4, :],
                    deps=[q4h_user[bi]], stream=f"q4h{bi}")
        last_mm_s = None
        for qc in range(2):
            mm_av_last = None
            mm_s_l = {}
            ex_l = {}
            for step in range(35):
                if step < 32:
                    kb = step
                    si = kb % 4
                    da = [t_q, t_ckvT]
                    if psA_user[si] is not None:
                        da.append(psA_user[si])
                    kb2 = (qc * 32 + kb) if nkb2 == 64 else kb
                    mm_s_l[kb] = p.op("pe", "matmul",
                                      psA[0:128, si * 512 : (si + 1) * 512],
                                      ckvT_sb[0:32, kb2 * 128 : (kb2 + 1) * 128],
                                      q4h_sb[bi][:, qc * 512 : (qc + 1) * 512],
                                      start=True, stop=True, deps=da)
                    last_mm_s = mm_s_l[kb]
                if step >= 2 and (step - 2) % 2 == 0 and step - 2 < 32:
                    kb = step - 2
                    si = kb % 4
                    de = [mm_s_l[kb], mm_s_l[kb + 1], t_negB]
                    if att_user[si] is not None:
                        de.append(att_user[si])
                    if att_user[si + 1] is not None:
                        de.append(att_user[si + 1])
                    ex = p.op("act", "activation",
                              attnT_sb[:, si * 512 : (si + 2) * 512],
                              psA[0:128, si * 512 : (si + 2) * 512],
                              AF.Exp, scale=0.5, bias=negB, deps=de)
                    ex_l[kb] = ex
                    ex_l[kb + 1] = ex
                    psA_user[si] = ex
                    psA_user[si + 1] = ex
                if step >= 3:
                    kb = step - 3
                    si = kb % 4
                    kb2 = (qc * 32 + kb) if nkb2 == 64 else kb
                    dv = [ex_l[kb], t_ckv5]
                    if mm_av_last is not None:
                        dv.append(mm_av_last)
                    elif psO_free is not None:
                        dv.append(psO_free)
                    mm_av_last = p.op("pe", "matmul", psO[0:5, 0:512],
                                      ckv5_sb[:, kb2 * 5 : kb2 * 5 + 5],
                                      attnT_sb[:, si * 512 : (si + 1) * 512],
                                      start=(kb == 0), stop=(kb == 31), deps=dv)
                    att_user[si] = mm_av_last
                    del ex_l[kb]
            do5 = [mm_av_last]
            if o5_free is not None:
                do5.append(o5_free)
            ev_o5 = p.op("dve", "tensor_copy", o5_sb[:], psO[0:5, 0:512], deps=do5)
            psO_free = ev_o5
            tr_toks = []
            for i in range(4):
                tr = p.op("pe", "transpose", ptps[0:128, 8 * i : 8 * i + 5],
                          o5_sb[0:5, 128 * i : 128 * (i + 1)], ident[0:5, 0:5],
                          deps=[ev_o5, t_id] + ([ev_ot] if ev_ot is not None else []))
                tr_toks.append(tr)
            o5_free = tr_toks[-1]
            for i in range(4):
                ev_ot = p.op("dve", "tensor_copy", ot_sb[:, 5 * i : 5 * i + 5],
                             ptps[0:128, 8 * i : 8 * i + 5],
                             deps=tr_toks if i == 0 else [tr_toks[-1], ev_ot])
            for i in range(4):
                drc = [ev_ot]
                if nrm is not None:
                    drc.append(nrm)
                rc0 = p.op("act", "activation", rcol_sb[:, 1:2],
                           ot_sb[:, 5 * i + 4 : 5 * i + 5], AF.Ln, deps=drc)
                rc = p.op("act", "activation", rcol_sb[:, 0:1],
                          rcol_sb[:, 1:2], AF.Exp, scale=-1.0, deps=[rc0])
                qtile = qc * 4 + i
                nrm = p.op("dve", "tensor_scalar",
                           oall_sb[:, qtile * 128 + 4 * h : qtile * 128 + 4 * h + 4],
                           ot_sb[:, 5 * i : 5 * i + 4], rcol_sb[:, 0:1], None,
                           op0=ALU.mult, deps=[rc])
                oall_toks.append(nrm)
        q4h_user[bi] = last_mm_s
        if h % 4 == 3:
            emit_wcomb_cb(h // 4)

    t_wcomb_rd = p.dma("sp", wcomb_sb[:], wcomb_h[:, :], deps=wc_state["dma"])
    t_bcomb_done = wc_state["bdone"]

    ot_toks = []
    ev_t = ev_ot
    for tt in range(TT):
        tr = p.op("pe", "transpose", ptps[0:128, 0:128],
                  oall_sb[:, tt * 128 : (tt + 1) * 128], ident[:],
                  deps=[ev_t, oall_toks[-1]])
        ev_t = p.op("dve", "tensor_copy", otT_sb[:, tt * 128 : (tt + 1) * 128],
                    ptps[0:128, 0:128], deps=[tr])
        ot_toks.append(ev_t)

    xr_tok = [None, None]
    xo_tok = [None, None]
    psOut_tok = [None, None]
    final_toks = []
    for tt in range(TT):
        for cb in range(8):
            i = (tt * 8 + cb) % 2
            dx = [] if xr_tok[i] is None else [xr_tok[i]]
            t_xr = p.dma("act", xr_sb[i][:],
                         xres[tt * 128 : (tt + 1) * 128, cb * 512 : (cb + 1) * 512],
                         deps=dx, stream=f"xr{i}")
            dm = [ot_toks[tt], t_wcomb_rd]
            if psOut_tok[i] is not None:
                dm.append(psOut_tok[i])
            mm1 = p.op("pe", "matmul", psOut[i][0:128, 0:512],
                       otT_sb[:, tt * 128 : (tt + 1) * 128],
                       wcomb_sb[:, cb * 512 : (cb + 1) * 512],
                       start=True, stop=False, deps=dm)
            mm2 = p.op("pe", "matmul", psOut[i][0:128, 0:512],
                       ones_row[:], bcombb_sb[:, cb * 512 : (cb + 1) * 512],
                       start=False, stop=True, deps=[mm1, t_ones, t_bcomb_done])
            dadd = [mm2, t_xr]
            if xo_tok[i] is not None:
                dadd.append(xo_tok[i])
            add = p.op("dve", "tensor_tensor", xo_sb[i][:], psOut[i][0:128, 0:512],
                       xr_sb[i][:], op=ALU.add, deps=dadd)
            psOut_tok[i] = add
            xr_tok[i] = add
            st = p.dma("sp", xout[tt * 128 : (tt + 1) * 128, cb * 512 : (cb + 1) * 512],
                       xo_sb[i][:], deps=[add], stream=f"xo{i}")
            xo_tok[i] = st
            final_toks.append(st)
    return final_toks


def build_l0core(nc):
    p = Prog(nc)
    A = {}
    A["qT"] = nc.dram_tensor("qT", [128, NTOK], BF16, kind="ExternalInput").ap()
    A["ckvT32"] = nc.dram_tensor("ckvT32", [32, S], F32, kind="ExternalInput").ap()
    A["ckv5"] = nc.dram_tensor("ckv5", [S, 5], F32, kind="ExternalInput").ap()
    A["Wo"] = nc.dram_tensor("Wo", [D, D], F32, kind="ExternalInput").ap()
    A["WvuT"] = nc.dram_tensor("WvuT", [128, 128], F32, kind="ExternalInput").ap()
    A["bvu"] = nc.dram_tensor("bvu", [128, FC], F32, kind="ExternalInput").ap()
    A["bo"] = nc.dram_tensor("bo", [1, D], F32, kind="ExternalInput").ap()
    A["xres"] = nc.dram_tensor("xres", [NTOK, D], F32, kind="ExternalInput").ap()
    A["xout"] = nc.dram_tensor("xout", [NTOK, D], F32, kind="ExternalOutput").ap()
    A["wcomb_scratch"] = nc.dram_tensor("wcomb_scratch", [128, D], BF16, kind="Internal").ap()
    toks = emit_l0core(p, nc, A, "", nc.alloc_sbuf_tensor, nc.alloc_psum_tensor)
    p.finish(toks)
    p.emit()
    return nc


SINK_ITERS = 50
SINK_TAU = 0.1


def emit_l0ffn(p, nc, A, pfx, asb, aps_):
    xh2 = A["xh2"]
    xh2T = A["xh2T"]
    WstkT = A["WstkT"]
    bases = A["bases_bc"]
    gate = A["gate"]
    Wbr = A["Wbr"]
    xres = A["xres"]
    xout = A["xout"]

    xh_sb = asb(pfx + "xh_sb", [128, TT * D], BF16)
    xhT_sb = asb(pfx + "xhT_sb", [128, FC * NTOK], BF16)
    wstk_sb = asb(pfx + "wstk_sb", [128, FC * 24], BF16)
    bases_sb = asb(pfx + "bases_sb", [128, 24], F32)
    gate_sb = asb(pfx + "gate_sb", [1, 4], F32)
    st_sb = asb(pfx + "st_sb", [128, 48], F32)
    mv_sb = asb(pfx + "mv_sb", [128, 2], F32)
    rin_sb = asb(pfx + "rin_sb", [128, 2 * TT], F32)
    lg_sb = asb(pfx + "lg_sb", [128, TT * 24], F32)
    sm_sb = asb(pfx + "sm_sb", [128, 8], F32)
    hpre_sb = asb(pfx + "hpre_sb", [128, TT * 4], F32)
    hpost_sb = asb(pfx + "hpost_sb", [128, TT * 4], F32)
    Z_sb = asb(pfx + "Z_sb", [128, TT * 16], F32)
    Zs_sb = asb(pfx + "Zs_sb", [128, TT * 16], F32)
    T_sb = asb(pfx + "T_sb", [128, TT * 16], F32)
    E_sb = asb(pfx + "E_sb", [128, TT * 16], F32)
    S_sb = asb(pfx + "S_sb", [128, TT * 4], F32)
    U_sb = asb(pfx + "U_sb", [128, TT * 4], F32)
    V_sb = asb(pfx + "V_sb", [128, TT * 4], F32)
    H_sb = asb(pfx + "H_sb", [128, TT * 16], F32)
    gbc_sb = asb(pfx + "gbc_sb", [128, 2], F32)
    onesf_sb = asb(pfx + "onesf_sb", [1, 128], F32)
    xpre_sb = [asb(pfx + f"xpre_sb{i}", [128, 1024], BF16) for i in range(2)]
    xpreT_sb = asb(pfx + "xpreT_sb", [128, 8 * NTOK], BF16)
    wbr_sb = asb(pfx + "wbr_sb", [128, 8 * 1024], BF16)
    y2_sb = [asb(pfx + f"y2_sb{i}", [128, 1024], BF16) for i in range(2)]
    acc_sb = [asb(pfx + f"acc_sb{i}", [128, 1024], F32) for i in range(2)]
    xr_sb = [asb(pfx + f"xr_sb{i}", [128, 1024], F32) for i in range(2)]
    xo_sb = [asb(pfx + f"xo_sb{i}", [128, 1024], F32) for i in range(2)]

    ps0 = aps_(pfx + "ps0", [128, 512], F32)
    ps1 = aps_(pfx + "ps1", [128, 512], F32)
    ptps = aps_(pfx + "ptps", [128, 1024], BF16)

    ident, t_id = emit_ident(p, nc, asb, pfx + "ident")

    t_xh = p.dma("sp", xh_sb[:].rearrange("p (t d) -> p t d", t=TT),
                 xh2.rearrange("(t p) d -> p t d", p=128))
    t_xhT = p.dma("sp", xhT_sb[:].rearrange("p (c t) -> p c t", c=FC),
                  xh2T.rearrange("(c p) t -> p c t", p=128))
    t_ws = p.dma("pool", wstk_sb[:].rearrange("p (c n) -> p c n", c=FC),
                 WstkT.rearrange("(c p) n -> p c n", p=128))
    t_bs = p.dma("sp", bases_sb[:], bases)
    t_g = p.dma("sp", gate_sb[0:1, 0:1], gate)
    t_of = p.op("pool", "memset", onesf_sb[:], 1.0)
    t_wbr = p.dma("pool", wbr_sb[:].rearrange("p (c n) -> p c n", c=8),
                  Wbr.rearrange("(c p) n -> p c n", p=128), stream="wbr")

    rin_toks = []
    st_free = None
    mv_free = None
    for t in range(TT):
        stats = []
        for j in range(8):
            dd = [t_xh]
            if st_free is not None:
                dd.append(st_free)
            stats.append(p.op("dve", "bn_stats", st_sb[:, 6 * j : 6 * j + 6],
                              xh_sb[:, t * D + 512 * j : t * D + 512 * (j + 1)],
                              deps=dd))
        dd = stats + ([mv_free] if mv_free else [])
        aggr = p.op("dve", "bn_aggr", mv_sb[:], st_sb[:], deps=dd)
        st_free = aggr
        t1 = p.op("dve", "tensor_tensor", rin_sb[:, 2 * t : 2 * t + 1],
                  mv_sb[:, 0:1], mv_sb[:, 0:1], op=ALU.mult, deps=[aggr])
        t2 = p.op("dve", "tensor_tensor", rin_sb[:, 2 * t : 2 * t + 1],
                  rin_sb[:, 2 * t : 2 * t + 1], mv_sb[:, 1:2], op=ALU.add,
                  deps=[t1])
        mv_free = t2
        eap, etok = p.const_col(RMS_EPS)
        t3 = p.op("act", "activation", rin_sb[:, 2 * t + 1 : 2 * t + 2],
                  rin_sb[:, 2 * t : 2 * t + 1], AF.Ln, bias=eap, scale=1.0,
                  deps=[t2, etok])
        t4 = p.op("act", "activation", rin_sb[:, 2 * t : 2 * t + 1],
                  rin_sb[:, 2 * t + 1 : 2 * t + 2], AF.Exp, scale=-0.5,
                  deps=[t3])
        rin_toks.append(t4)

    lg_toks = []
    ps_free = [None, None]
    for t in range(TT):
        ps = (ps0, ps1)[t % 2]
        last = None
        for c in range(FC):
            dd = [t_xhT, t_ws] if last is None else [last]
            if last is None and ps_free[t % 2] is not None:
                dd.append(ps_free[t % 2])
            last = p.op("pe", "matmul", ps[0:128, 0:24],
                        xhT_sb[:, c * NTOK + t * 128 : c * NTOK + (t + 1) * 128],
                        wstk_sb[:, c * 24 : (c + 1) * 24],
                        start=(c == 0), stop=(c == FC - 1), deps=dd)
        ts1 = p.op("dve", "tensor_scalar", lg_sb[:, t * 24 : (t + 1) * 24],
                   ps[0:128, 0:24], rin_sb[:, 2 * t : 2 * t + 1], None,
                   op0=ALU.mult, deps=[last, rin_toks[t]])
        ps_free[t % 2] = ts1
        ad = p.op("dve", "tensor_tensor", lg_sb[:, t * 24 : (t + 1) * 24],
                  lg_sb[:, t * 24 : (t + 1) * 24], bases_sb[:], op=ALU.add,
                  deps=[ts1, t_bs])
        lg_toks.append(ad)

    hpre_toks, hpost_toks = [], []
    sm_free = None
    for t in range(TT):
        for (off, dst, toks) in ((16, hpre_sb, hpre_toks), (20, hpost_sb, hpost_toks)):
            sl = lg_sb[:, t * 24 + off : t * 24 + off + 4]
            dd = [lg_toks[t]]
            if sm_free is not None:
                dd.append(sm_free)
            m1 = p.op("dve", "tensor_reduce", sm_sb[:, 0:1], sl, axis=AX.X,
                      op=ALU.max, negate=True, deps=dd)
            e = p.op("act", "activation", sm_sb[:, 4:8], sl, AF.Exp,
                     scale=1.0, bias=sm_sb[:, 0:1], deps=[m1])
            s = p.op("dve", "reduce_sum", sm_sb[:, 1:2], sm_sb[:, 4:8],
                     axis=AX.X, deps=[e])
            r0 = p.op("act", "activation", sm_sb[:, 3:4], sm_sb[:, 1:2],
                      AF.Ln, deps=[s])
            r = p.op("act", "activation", sm_sb[:, 2:3], sm_sb[:, 3:4],
                     AF.Exp, scale=-1.0, deps=[r0])
            n = p.op("dve", "tensor_scalar", dst[:, t * 4 : (t + 1) * 4],
                     sm_sb[:, 4:8], sm_sb[:, 2:3], None, op0=ALU.mult, deps=[r])
            sm_free = n
            toks.append(n)

    xpre_toks = []
    xpre_free = [None, None]
    xpreT_toks = []
    pt_free = t_id
    for t in range(TT):
        xi = t % 2
        xp = xpre_sb[xi]
        dd = [t_xh, hpre_toks[t]]
        if xpre_free[xi] is not None:
            dd.append(xpre_free[xi])
        a = p.op("dve", "tensor_scalar", xp[:],
                 xh_sb[:, t * D : t * D + 1024], hpre_sb[:, t * 4 : t * 4 + 1],
                 None, op0=ALU.mult, deps=dd)
        for n in range(1, 4):
            a = p.op("dve", "scalar_tensor_tensor", xp[:],
                     xh_sb[:, t * D + n * 1024 : t * D + (n + 1) * 1024],
                     hpre_sb[:, t * 4 + n : t * 4 + n + 1], xp[:],
                     op0=ALU.mult, op1=ALU.add, deps=[a])
        xpre_toks.append(a)
        trs = []
        for cc in range(8):
            tr = p.op("pe", "transpose", ptps[0:128, cc * 128 : (cc + 1) * 128],
                      xp[:, cc * 128 : (cc + 1) * 128],
                      ident[:], deps=[a, t_id, pt_free])
            trs.append(tr)
        xpre_free[xi] = trs[-1]
        ev = p.op("dve", "tensor_copy",
                  xpreT_sb[:].rearrange("p (c t2) -> p c t2", c=8)[:, :, t * 128 : (t + 1) * 128],
                  ptps[0:128, 0:1024].rearrange("p (c x) -> p c x", c=8),
                  deps=trs)
        pt_free = ev
        xpreT_toks.append(ev)
    Z3 = Z_sb[:].rearrange("p (t i j) -> p t i j", t=TT, i=4)
    Zs3 = Zs_sb[:].rearrange("p (t j i) -> p t j i", t=TT, j=4)
    T3 = T_sb[:].rearrange("p (t a b) -> p t a b", t=TT, a=4)
    E3 = E_sb[:].rearrange("p (t a b) -> p t a b", t=TT, a=4)
    S3 = S_sb[:].rearrange("p (t a) -> p t a", t=TT)
    U3 = U_sb[:].rearrange("p (t a) -> p t a", t=TT)
    V3 = V_sb[:].rearrange("p (t a) -> p t a", t=TT)
    tz = p.op("dve", "tensor_scalar", Z_sb[:].rearrange("p (t x) -> p t x", t=TT),
              lg_sb[:].rearrange("p (t x) -> p t x", t=TT)[:, :, 0:16],
              1.0 / SINK_TAU, 20.0, op0=ALU.mult, op1=ALU.subtract,
              deps=[lg_toks[-1]])
    tzs = p.op("dve", "tensor_copy",
               Zs_sb[:].rearrange("p (t j i) -> p t j i", t=TT, j=4),
               Z_sb[:].rearrange("p (t i j) -> p t j i", t=TT, i=4),
               deps=[tz])
    tu = p.op("pool", "memset", U_sb[:], 0.0)
    tv = p.op("pool", "memset", V_sb[:], 0.0)
    u_tok, v_tok = tu, tv
    T_free = None
    E_free = None
    S_free = None
    for it in range(SINK_ITERS):
        adds = []
        for i in range(4):
            dd = [tz, v_tok]
            if T_free is not None:
                dd.append(T_free)
            adds.append(p.op("dve", "tensor_tensor", T3[:, :, i : i + 1, :], Z3[:, :, i : i + 1, :],
                             V3[:], op=ALU.add, deps=dd))
        dd = adds + ([E_free] if E_free else [])
        ex = p.op("act", "activation", E_sb[:], T_sb[:], AF.Exp, deps=dd)
        T_free = ex
        dd = [ex] + ([S_free] if S_free else [])
        s1 = p.op("dve", "tensor_tensor", S3[:], E3[:, :, :, 0:1], E3[:, :, :, 1:2],
                  op=ALU.add, deps=dd)
        s2 = p.op("dve", "tensor_tensor", S3[:], S3[:], E3[:, :, :, 2:3],
                  op=ALU.add, deps=[s1])
        s3 = p.op("dve", "tensor_tensor", S3[:], S3[:], E3[:, :, :, 3:4],
                  op=ALU.add, deps=[s2])
        E_free = s3
        ln = p.op("act", "activation", S_sb[:], S_sb[:], AF.Ln, deps=[s3])
        u_tok = p.op("dve", "tensor_scalar", U_sb[:], S_sb[:], -1.0, None,
                     op0=ALU.mult, deps=[ln, u_tok])
        S_free = u_tok
        adds = []
        for j in range(4):
            adds.append(p.op("dve", "tensor_tensor", T3[:, :, j : j + 1, :], Zs3[:, :, j : j + 1, :],
                             U3[:], op=ALU.add, deps=[tzs, u_tok, T_free]))
        ex = p.op("act", "activation", E_sb[:], T_sb[:], AF.Exp, deps=adds + [E_free])
        T_free = ex
        s1 = p.op("dve", "tensor_tensor", S3[:], E3[:, :, :, 0:1], E3[:, :, :, 1:2],
                  op=ALU.add, deps=[ex, S_free])
        s2 = p.op("dve", "tensor_tensor", S3[:], S3[:], E3[:, :, :, 2:3],
                  op=ALU.add, deps=[s1])
        s3 = p.op("dve", "tensor_tensor", S3[:], S3[:], E3[:, :, :, 3:4],
                  op=ALU.add, deps=[s2])
        E_free = s3
        ln = p.op("act", "activation", S_sb[:], S_sb[:], AF.Ln, deps=[s3])
        v_tok = p.op("dve", "tensor_scalar", V_sb[:], S_sb[:], -1.0, None,
                     op0=ALU.mult, deps=[ln, v_tok])
        S_free = v_tok
    adds = []
    for i in range(4):
        adds.append(p.op("dve", "tensor_tensor", T3[:, :, i : i + 1, :], Z3[:, :, i : i + 1, :],
                         V3[:], op=ALU.add, deps=[tz, v_tok, T_free]))
    sw = p.op("dve", "tensor_copy",
              E_sb[:].rearrange("p (t j i) -> p t j i", t=TT, j=4),
              T_sb[:].rearrange("p (t i j) -> p t j i", t=TT, i=4),
              deps=adds + [E_free])
    adds2 = []
    for j in range(4):
        adds2.append(p.op("dve", "tensor_tensor", E3[:, :, j : j + 1, :], E3[:, :, j : j + 1, :],
                          U3[:], op=ALU.add, deps=[sw, u_tok]))
    hx = p.op("act", "activation", H_sb[:], E_sb[:], AF.Exp, deps=adds2)
    sg = p.op("act", "activation", gate_sb[0:1, 1:2], gate_sb[0:1, 0:1],
              AF.Sigmoid, deps=[t_g])
    bcg = p.op("pe", "matmul", ps0[0:128, 0:1], onesf_sb[:], gate_sb[0:1, 1:2],
               start=True, stop=True, deps=[sg, t_of, ps_free[0]])
    evg = p.op("dve", "tensor_copy", gbc_sb[:, 0:1], ps0[0:128, 0:1], deps=[bcg])
    onemg = p.op("dve", "tensor_scalar", gbc_sb[:, 1:2], gbc_sb[:, 0:1],
                 -1.0, 1.0, op0=ALU.mult, op1=ALU.add, deps=[evg])
    hg = p.op("dve", "tensor_scalar", H_sb[:], H_sb[:], gbc_sb[:, 0:1], None,
              op0=ALU.mult, deps=[hx, evg])


    psy_free = [evg, ps_free[1]]
    y_free = [None, None]
    xr_tok = [None, None]
    final_toks = []
    acc_tok = [None, None]
    xo_tok = [None, None]
    y_done = []
    for t in range(TT):
        ps = (ps0, ps1)[t % 2]
        yb = y2_sb[t % 2]
        last = None
        for cc in range(8):
            dd = [t_wbr, xpreT_toks[t]] if last is None else [last]
            if last is None and psy_free[t % 2] is not None:
                dd.append(psy_free[t % 2])
            if last is None and y_free[t % 2] is not None:
                dd.append(y_free[t % 2])
            last = p.op("pe", "matmul", ps[0:128, 0:512],
                        xpreT_sb[:, cc * NTOK + t * 128 : cc * NTOK + (t + 1) * 128],
                        wbr_sb[:, cc * 1024 : cc * 1024 + 512],
                        start=(cc == 0), stop=(cc == 7), deps=dd)
        ev1 = p.op("act", "copy", yb[:, 0:512], ps[0:128, 0:512],
                   deps=[last] + ([y_free[t % 2]] if y_free[t % 2] else []))
        last = None
        for cc in range(8):
            dd = [t_wbr, xpreT_toks[t], ev1] if last is None else [last]
            last = p.op("pe", "matmul", ps[0:128, 0:512],
                        xpreT_sb[:, cc * NTOK + t * 128 : cc * NTOK + (t + 1) * 128],
                        wbr_sb[:, cc * 1024 + 512 : (cc + 1) * 1024],
                        start=(cc == 0), stop=(cc == 7), deps=dd)
        ev2 = p.op("act", "copy", yb[:, 512:1024], ps[0:128, 0:512], deps=[last])
        psy_free[t % 2] = ev2
        y_done.append(ev2)
        last_add = None
        for i in range(4):
            ri = (t * 4 + i) % 2
            dd = [] if xr_tok[ri] is None else [xr_tok[ri]]
            t_xr = p.dma("act", xr_sb[ri][:],
                         xres[t * 128 : (t + 1) * 128, i * 1024 : (i + 1) * 1024],
                         deps=dd, stream=f"xr{ri}")
            ai = i % 2
            dd = [t_xr, t_xh, onemg, hg, ev1, ev2, hpost_toks[t]]
            if acc_tok[ai] is not None:
                dd.append(acc_tok[ai])
            a = p.op("dve", "scalar_tensor_tensor", acc_sb[ai][:],
                     xh_sb[:, t * D + i * 1024 : t * D + (i + 1) * 1024],
                     gbc_sb[:, 1:2], xr_sb[ri][:],
                     op0=ALU.mult, op1=ALU.add, deps=dd)
            for j in range(4):
                a = p.op("dve", "scalar_tensor_tensor", acc_sb[ai][:],
                         xh_sb[:, t * D + j * 1024 : t * D + (j + 1) * 1024],
                         H_sb[:, t * 16 + 4 * j + i : t * 16 + 4 * j + i + 1],
                         acc_sb[ai][:], op0=ALU.mult, op1=ALU.add, deps=[a])
            oi = i % 2
            do = [a]
            if xo_tok[oi] is not None:
                do.append(xo_tok[oi])
            fin = p.op("dve", "scalar_tensor_tensor", xo_sb[oi][:],
                       yb[:], hpost_sb[:, t * 4 + i : t * 4 + i + 1],
                       acc_sb[ai][:], op0=ALU.mult, op1=ALU.add, deps=do)
            acc_tok[ai] = fin
            last_add = fin
            xr_tok[ri] = a
            y_free[t % 2] = fin
            st = p.dma("sp", xout[t * 128 : (t + 1) * 128, i * 1024 : (i + 1) * 1024],
                       xo_sb[oi][:], deps=[fin], stream=f"xo{oi}")
            xo_tok[oi] = st
            final_toks.append(st)
    return final_toks + y_done


def build_l0ffn(nc):
    p = Prog(nc)
    A = {}
    A["xh2"] = nc.dram_tensor("xh2", [NTOK, D], BF16, kind="ExternalInput").ap()
    A["xh2T"] = nc.dram_tensor("xh2T", [D, NTOK], BF16, kind="ExternalInput").ap()
    A["WstkT"] = nc.dram_tensor("WstkT", [D, 24], F32, kind="ExternalInput").ap()
    A["bases_bc"] = nc.dram_tensor("bases_bc", [128, 24], F32, kind="ExternalInput").ap()
    A["gate"] = nc.dram_tensor("gate", [1, 1], F32, kind="ExternalInput").ap()
    A["Wbr"] = nc.dram_tensor("Wbr", [1024, 1024], F32, kind="ExternalInput").ap()
    A["xres"] = nc.dram_tensor("xres", [NTOK, D], F32, kind="ExternalInput").ap()
    A["xout"] = nc.dram_tensor("xout", [NTOK, D], F32, kind="ExternalOutput").ap()
    toks = emit_l0ffn(p, nc, A, "", nc.alloc_sbuf_tensor, nc.alloc_psum_tensor)
    p.finish(toks)
    p.emit()
    return nc


def emit_l1qkv(p, nc, A, pfx, asb, aps_):
    xhT = A["xhT"]
    wqkv = A["wqkv"]
    qkvT = A["qkvT"]

    xhT_sb = asb(pfx + "xhT_sb", [128, FC * NTOK], BF16)
    wc_sb = [asb(pfx + f"wc_sb{i}", [128, FC * 128], BF16) for i in range(2)]
    o_sb = [asb(pfx + f"o_sb{i}", [128, 512], BF16) for i in range(2)]
    ps = [aps_(pfx + f"ps{i}", [128, 512], F32) for i in range(2)]

    t_xh = p.dma("sp", xhT_sb[:].rearrange("p (c t) -> p c t", c=FC),
                 xhT.rearrange("(c p) t -> p c t", p=128))
    w_tok = [None, None]
    ps_tok = [None, None]
    o_tok = [None, None]
    st_toks = []
    for colb in range(48):
        wi = colb % 2
        dd = [] if w_tok[wi] is None else [w_tok[wi]]
        t_w = p.dma("pool", wc_sb[wi][:].rearrange("p (c n) -> p c n", c=FC),
                    wqkv[:, colb * 128 : (colb + 1) * 128].rearrange(
                        "(c p) n -> p c n", p=128),
                    deps=dd, stream=f"w{wi}")
        last_any = None
        for tc in range(2):
            i = (colb * 2 + tc) % 2
            last = None
            for c in range(FC):
                dd = [t_w, t_xh] if last is None else [last]
                if last is None and ps_tok[i] is not None:
                    dd.append(ps_tok[i])
                last = p.op("pe", "matmul", ps[i][0:128, 0:512],
                            wc_sb[wi][:, c * 128 : (c + 1) * 128],
                            xhT_sb[:, c * NTOK + tc * 512 : c * NTOK + (tc + 1) * 512],
                            start=(c == 0), stop=(c == FC - 1), deps=dd)
            dd = [last]
            if o_tok[i] is not None:
                dd.append(o_tok[i])
            ev = p.op("dve", "tensor_copy", o_sb[i][:], ps[i][0:128, 0:512], deps=dd)
            ps_tok[i] = ev
            st = p.dma("sp", qkvT[colb * 128 : (colb + 1) * 128, tc * 512 : (tc + 1) * 512],
                       o_sb[i][:], deps=[ev], stream=f"o{i}")
            o_tok[i] = st
            st_toks.append(st)
            last_any = last
        w_tok[wi] = last_any
    return st_toks


def build_l1qkv(nc):
    p = Prog(nc)
    A = {}
    A["xhT"] = nc.dram_tensor("xhT", [D, NTOK], BF16, kind="ExternalInput").ap()
    A["wqkv"] = nc.dram_tensor("wqkv", [D, 6144], F32, kind="ExternalInput").ap()
    A["qkvT"] = nc.dram_tensor("qkvT", [6144, NTOK], BF16, kind="ExternalOutput").ap()
    toks = emit_l1qkv(p, nc, A, "", nc.alloc_sbuf_tensor, nc.alloc_psum_tensor)
    p.finish(toks)
    p.emit()
    return nc


L1_SHIFT = 12.0
NKB_OF_QC = [4 * qc + 4 for qc in range(8)]


A2A_CHUNKS = [[(0, r), (1, 7 - r)] for r in range(4)]


def emit_l1att(p, nc, A, pfx, asb, aps_):
    a2a = "qA2A" in A
    """GQA attention with sqrt-alibi causal bias, head-parallel.

    Per core: 8 (h,g) units of its batch, full 4096-token causal q range.
    in: QTs [1024,4096] bf16 (8 units x 128 d), KTc [256,4096] bf16 (2 kv
    heads), Vc [4096,256] bf16, SDrep [128,4608] f32, scales_bc [128,16] f32
    (col 2u: 1/(slope*sqrt(128)), col 2u+1: slope).
    out: OUTN [1024, 4096] bf16 (normalized attention out^T, unit-major rows).
    """
    SDrep = A["SDrep"]
    scales = A["scales_bc"]
    OUTN = A["OUTN"]
    if not a2a:
        QTs, KTc, Vc = A["QTs"], A["KTc"], A["Vc"]

    sd_sb = asb(pfx + "sd_sb", [128, 4608], F32)
    scl_sb = asb(pfx + "scl_sb", [128, 16], F32)
    kt_sb = [asb(pfx + f"kt_sb{i}", [128, S], BF16) for i in range(2)]
    v_sb = [asb(pfx + f"v_sb{i}", [128, 32 * 128], BF16) for i in range(2)]
    q_sb = [asb(pfx + f"q_sb{i}", [128, S], BF16) for i in range(2)]
    qs_sb = [asb(pfx + f"qs_sb{i}", [128, S], BF16) for i in range(2)]
    t_sb = [asb(pfx + f"t_sb{i}", [128, 512], F32) for i in range(4)]
    at_sb = [asb(pfx + f"at_sb{i}", [128, 512], BF16) for i in range(4)]
    rb_sb = asb(pfx + "rb_sb", [1, 512], BF16)
    rbf_sb = asb(pfx + "rbf_sb", [1, 512], F32)
    rbc_sb = asb(pfx + "rbc_sb", [128, 512], F32)
    on_sb = [asb(pfx + f"on_sb{i}", [128, 512], BF16) for i in range(2)]
    onesc_sb = asb(pfx + "onesc_sb", [128, 1], BF16)
    onesr_sb = asb(pfx + "onesr_sb", [1, 128], BF16)

    psS = [aps_(pfx + f"psS{i}", [128, 512], F32) for i in range(4)]
    psO = aps_(pfx + "psO", [128, 512], F32)
    psR = aps_(pfx + "psR", [128, 512], F32)
    psB = aps_(pfx + "psB", [128, 512], F32)

    negB, t_negB = p.const_col(-L1_SHIFT)
    t_sd = p.dma("sp", sd_sb[:], SDrep)
    t_scl = p.dma("sp", scl_sb[:], scales)
    t_oc = p.op("pool", "memset", onesc_sb[:], 1.0)
    t_or = p.op("pool", "memset", onesr_sb[:], 1.0)

    kv_tok = [[None, None], [None, None]]
    q_tok = [None, None]
    qs_tok = [None, None]
    psS_tok = [None] * 4
    at_tok = [None] * 4
    t_tok = [None] * 4
    psO_free = None
    psR_free = None
    rb_free = None
    rbc_free = None
    on_tok = [None, None]
    st_toks = []
    kt_cur = [None, None]
    v_cur = [None, None]
    for u in range(8):
        bb = u // 4
        hl = bb
        g = u % 4
        if g == 0:
            hb = hl % 2
            dk = [] if kv_tok[0][hb] is None else [kv_tok[0][hb]]
            dv_ = [] if kv_tok[1][hb] is None else [kv_tok[1][hb]]
            if a2a:
                for j in range(8):
                    srcr = j if bb == 0 else 7 - j
                    kt_cur[hb] = p.dma(
                        "sp", kt_sb[hb][:, j * 512 : (j + 1) * 512],
                        A["kA2A"][srcr * 128 : (srcr + 1) * 128,
                                  bb * 512 : (bb + 1) * 512],
                        deps=dk, stream=f"kt{hb}")
                    for sub in range(4):
                        kb = j * 4 + sub
                        v_cur[hb] = p.dma(
                            "act", v_sb[hb][:, kb * 128 : (kb + 1) * 128],
                            A["vA2A"][srcr * 128 : (srcr + 1) * 128,
                                      bb * 512 + sub * 128 : bb * 512 + (sub + 1) * 128],
                            deps=dv_, transpose=True, stream=f"v{hb}")
            else:
                kt_cur[hb] = p.dma("sp", kt_sb[hb][:], KTc[hl * 128 : (hl + 1) * 128, :],
                                   deps=dk, stream=f"kt{hb}")
                v_cur[hb] = p.dma("act", v_sb[hb][:].rearrange("p (k n) -> p k n", k=32),
                                  Vc[:, hl * 128 : (hl + 1) * 128].rearrange(
                                      "(k p) n -> p k n", p=128),
                                  deps=dv_, stream=f"v{hb}")
        hb = hl % 2
        qi = u % 2
        dq = [] if q_tok[qi] is None else [q_tok[qi]]
        if a2a:
            for j in range(8):
                srcr = j if bb == 0 else 7 - j
                t_q = p.dma("sp", q_sb[qi][:, j * 512 : (j + 1) * 512],
                            A["qA2A"][srcr * 512 + g * 128 : srcr * 512 + (g + 1) * 128,
                                      bb * 512 : (bb + 1) * 512],
                            deps=dq, stream=f"q{qi}")
        else:
            t_q = p.dma("sp", q_sb[qi][:], QTs[u * 128 : (u + 1) * 128, :],
                        deps=dq, stream=f"q{qi}")
        dqs = [t_q, t_scl]
        if qs_tok[qi] is not None:
            dqs.append(qs_tok[qi])
        t_qs = p.op("dve", "tensor_scalar", qs_sb[qi][:], q_sb[qi][:],
                    scl_sb[:, 2 * u : 2 * u + 1], None, op0=ALU.mult, deps=dqs)
        q_tok[qi] = t_qs
        last_mm_s = None
        for qc in range(8):
            Q0 = 512 * qc
            nkb = NKB_OF_QC[qc]
            mm_av = None
            mm_sum = None
            mm_s_l = {}
            sub_l = {}
            ex_l = {}
            NB = 4
            for step in range(nkb + 3):
                if step < nkb:
                    kb = step
                    si = kb % NB
                    da = [t_qs, kt_cur[hb]]
                    if psS_tok[si] is not None:
                        da.append(psS_tok[si])
                    mm_s_l[kb] = p.op("pe", "matmul", psS[si][0:128, 0:512],
                                      kt_sb[hb][:, kb * 128 : (kb + 1) * 128],
                                      qs_sb[qi][:, Q0 : Q0 + 512],
                                      start=True, stop=True, deps=da)
                    last_mm_s = mm_s_l[kb]
                if step >= 1 and step - 1 < nkb:
                    kb = step - 1
                    si = kb % NB
                    off = Q0 - 128 * kb + 512
                    dsub = [mm_s_l[kb], t_sd]
                    if t_tok[si] is not None:
                        dsub.append(t_tok[si])
                    sub_l[kb] = p.op("dve", "tensor_tensor", t_sb[si][:],
                                     psS[si][0:128, 0:512], sd_sb[:, off : off + 512],
                                     op=ALU.subtract, deps=dsub)
                    psS_tok[si] = sub_l[kb]
                if step >= 2 and step - 2 < nkb:
                    kb = step - 2
                    si = kb % NB
                    dex = [sub_l[kb], t_negB, t_scl]
                    if at_tok[si] is not None:
                        dex.append(at_tok[si])
                    ex_l[kb] = p.op("act", "activation", at_sb[si][:], t_sb[si][:],
                                    AF.Exp, scale=scl_sb[:, 2 * u + 1 : 2 * u + 2],
                                    bias=negB, deps=dex)
                    t_tok[si] = ex_l[kb]
                if step >= 3:
                    kb = step - 3
                    si = kb % NB
                    dav = [ex_l[kb], v_cur[hb]]
                    if mm_av is not None:
                        dav.append(mm_av)
                    elif psO_free is not None:
                        dav.append(psO_free)
                    mm_av = p.op("pe", "matmul", psO[0:128, 0:512],
                                 v_sb[hb][:, kb * 128 : (kb + 1) * 128],
                                 at_sb[si][:], start=(kb == 0), stop=(kb == nkb - 1),
                                 deps=dav)
                    dsm = [ex_l[kb], t_oc]
                    if mm_sum is not None:
                        dsm.append(mm_sum)
                    elif psR_free is not None:
                        dsm.append(psR_free)
                    mm_sum = p.op("pe", "matmul", psR[0:1, 0:512], onesc_sb[:],
                                  at_sb[si][:], start=(kb == 0), stop=(kb == nkb - 1),
                                  deps=dsm)
                    at_tok[si] = mm_sum
                    del ex_l[kb]
            drb = [mm_sum]
            if rb_free is not None:
                drb.append(rb_free)
            rec0 = p.op("act", "activation", rbf_sb[:], psR[0:1, 0:512],
                        AF.Ln, deps=drb)
            psR_free = rec0
            rec = p.op("act", "activation", rb_sb[:], rbf_sb[:], AF.Exp,
                       scale=-1.0, deps=[rec0])
            dbc = [rec, t_or]
            if rbc_free is not None:
                dbc.append(rbc_free)
            bc = p.op("pe", "matmul", psB[0:128, 0:512], onesr_sb[:], rb_sb[:],
                      start=True, stop=True, deps=dbc)
            rb_free = bc
            evbc = p.op("act", "copy", rbc_sb[:], psB[0:128, 0:512], deps=[bc])
            rbc_free = evbc
            oi = (u * 8 + qc) % 2
            dn = [mm_av, evbc]
            if on_tok[oi] is not None:
                dn.append(on_tok[oi])
            nm = p.op("dve", "tensor_tensor", on_sb[oi][:], psO[0:128, 0:512],
                      rbc_sb[:], op=ALU.mult, deps=dn)
            psO_free = nm
            if a2a:
                jdst = qc if bb == 0 else 7 - qc
                st = p.dma("sp",
                           OUTN[jdst * 512 + g * 128 : jdst * 512 + (g + 1) * 128,
                                bb * 512 : (bb + 1) * 512],
                           on_sb[oi][:], deps=[nm], stream=f"on{oi}")
            else:
                st = p.dma("sp", OUTN[u * 128 : (u + 1) * 128, Q0 : Q0 + 512],
                           on_sb[oi][:], deps=[nm], stream=f"on{oi}")
            on_tok[oi] = st
            st_toks.append(st)
        kv_tok[0][hb] = last_mm_s
        kv_tok[1][hb] = mm_av
        q_tok[qi] = last_mm_s
        qs_tok[qi] = last_mm_s
    return st_toks


def build_l1att(nc):
    p = Prog(nc)
    A = {}
    A["QTs"] = nc.dram_tensor("QTs", [NTOK, S], BF16, kind="ExternalInput").ap()
    A["KTc"] = nc.dram_tensor("KTc", [256, S], BF16, kind="ExternalInput").ap()
    A["Vc"] = nc.dram_tensor("Vc", [S, 256], BF16, kind="ExternalInput").ap()
    A["SDrep"] = nc.dram_tensor("SDrep", [128, 4608], F32, kind="ExternalInput").ap()
    A["scales_bc"] = nc.dram_tensor("scales_bc", [128, 16], F32, kind="ExternalInput").ap()
    A["OUTN"] = nc.dram_tensor("OUTN", [NTOK, S], BF16, kind="ExternalOutput").ap()
    toks = emit_l1att(p, nc, A, "", nc.alloc_sbuf_tensor, nc.alloc_psum_tensor)
    p.finish(toks)
    p.emit()
    return nc


def emit_l1wo(p, nc, A, pfx, asb, aps_):
    OT = A["OT"]
    wo = A["wo"]
    xres = A["xres"]
    xout = A["xout"]

    ot_sb = asb(pfx + "ot_sb", [128, TT * FC * 128], BF16)
    wc_sb = [asb(pfx + f"wc_sb{i}", [128, FC * 512], BF16) for i in range(2)]
    xr_sb = [asb(pfx + f"xr_sb{i}", [128, 512], F32) for i in range(2)]
    xo_sb = [asb(pfx + f"xo_sb{i}", [128, 512], F32) for i in range(2)]
    ps = [aps_(pfx + f"ps{i}", [128, 512], F32) for i in range(2)]

    ot_toks = []
    for tt in range(TT):
        ot_toks.append(p.dma(
            "sp",
            ot_sb[:].rearrange("p (t c n) -> p t c n", t=TT, c=FC)[:, tt : tt + 1, :, :],
            OT[:, tt * 128 : (tt + 1) * 128].rearrange("(c p) n -> p c n", p=128)))
    w_tok = [None, None]
    ps_tok = [None, None]
    xr_tok = [None, None]
    xo_tok = [None, None]
    st_toks = []
    for cb in range(8):
        wi = cb % 2
        dd = [] if w_tok[wi] is None else [w_tok[wi]]
        t_w = p.dma("pool", wc_sb[wi][:].rearrange("p (c n) -> p c n", c=FC),
                    wo[:, cb * 512 : (cb + 1) * 512].rearrange("(c p) n -> p c n", p=128),
                    deps=dd, stream=f"w{wi}")
        last_any = None
        for tt in range(TT):
            i = (cb * TT + tt) % 2
            dx = [] if xr_tok[i] is None else [xr_tok[i]]
            t_xr = p.dma("act", xr_sb[i][:],
                         xres[tt * 128 : (tt + 1) * 128, cb * 512 : (cb + 1) * 512],
                         deps=dx, stream=f"xr{i}")
            last = None
            for c in range(FC):
                dd = [t_w, ot_toks[tt]] if last is None else [last]
                if last is None and ps_tok[i] is not None:
                    dd.append(ps_tok[i])
                last = p.op("pe", "matmul", ps[i][0:128, 0:512],
                            ot_sb[:, tt * FC * 128 + c * 128 : tt * FC * 128 + (c + 1) * 128],
                            wc_sb[wi][:, c * 512 : (c + 1) * 512],
                            start=(c == 0), stop=(c == FC - 1), deps=dd)
            last_any = last
            dadd = [last, t_xr]
            if xo_tok[i] is not None:
                dadd.append(xo_tok[i])
            add = p.op("dve", "tensor_tensor", xo_sb[i][:], ps[i][0:128, 0:512],
                       xr_sb[i][:], op=ALU.add, deps=dadd)
            ps_tok[i] = add
            xr_tok[i] = add
            st = p.dma("sp", xout[tt * 128 : (tt + 1) * 128, cb * 512 : (cb + 1) * 512],
                       xo_sb[i][:], deps=[add], stream=f"xo{i}")
            xo_tok[i] = st
            st_toks.append(st)
        w_tok[wi] = last_any
    return st_toks


def build_l1wo(nc):
    p = Prog(nc)
    A = {}
    A["OT"] = nc.dram_tensor("OT", [D, NTOK], BF16, kind="ExternalInput").ap()
    A["wo"] = nc.dram_tensor("wo", [D, D], F32, kind="ExternalInput").ap()
    A["xres"] = nc.dram_tensor("xres", [NTOK, D], F32, kind="ExternalInput").ap()
    A["xout"] = nc.dram_tensor("xout", [NTOK, D], F32, kind="ExternalOutput").ap()
    toks = emit_l1wo(p, nc, A, "", nc.alloc_sbuf_tensor, nc.alloc_psum_tensor)
    p.finish(toks)
    p.emit()
    return nc


def emit_l1ffa(p, nc, A, pfx, asb, aps_):
    xhT = A["xhT"]
    gw1T = A["gw1T"]
    gb1 = A["gb1"]
    gamma = A["gamma_bc"]
    beta = A["beta_bc"]
    gw2T = A["gw2T"]
    gb2 = A["gb2"]
    oh1 = A["oh1"]
    oh2 = A["oh2"]
    g1o = A["g1"]
    pco = A["pc"]

    xhT_sb = asb(pfx + "xhT_sb", [128, FC * NTOK], BF16)
    gw1_sb = asb(pfx + "gw1_sb", [128, FC * 32], BF16)
    gb1_sb = asb(pfx + "gb1_sb", [1, 32], BF16)
    gam_sb = asb(pfx + "gam_sb", [128, 32], F32)
    bet_sb = asb(pfx + "bet_sb", [128, 32], F32)
    gw2_sb = asb(pfx + "gw2_sb", [32, 8], BF16)
    gb2_sb = asb(pfx + "gb2_sb", [1, 8], BF16)
    h_sb = asb(pfx + "h_sb", [128, TT * 32], F32)
    hr_sb = asb(pfx + "hr_sb", [128, TT * 32], BF16)
    hT_sb = asb(pfx + "hT_sb", [32, TT * 128], BF16)
    st_sb = asb(pfx + "st_sb", [128, 6], F32)
    mv_sb = asb(pfx + "mv_sb", [128, 4], F32)
    g_sb = asb(pfx + "g_sb", [128, TT * 8], F32)
    m_sb = asb(pfx + "m_sb", [128, 8], F32)
    oh1_sb = asb(pfx + "oh1_sb", [128, TT * 8], BF16)
    oh2_sb = asb(pfx + "oh2_sb", [128, TT * 8], BF16)
    msk_sb = asb(pfx + "msk_sb", [128, 8], F32)
    g1_sb = asb(pfx + "g1_sb", [128, TT], F32)
    pc_sb = asb(pfx + "pc_sb", [1, 16], F32)
    onesr_sb = asb(pfx + "onesr_sb", [1, 128], BF16)
    onesc_sb = asb(pfx + "onesc_sb", [128, 1], BF16)

    ps0 = aps_(pfx + "ps0", [128, 512], F32)
    ps1 = aps_(pfx + "ps1", [128, 512], F32)
    ptps = aps_(pfx + "ptps", [128, 1024], BF16)
    psc = aps_(pfx + "psc", [128, 512], F32)
    psd = aps_(pfx + "psd", [128, 512], F32)
    psg = aps_(pfx + "psg", [128, 512], F32)

    ident, t_id = emit_ident(p, nc, asb, pfx + "ident")
    t_xh = p.dma("sp", xhT_sb[:].rearrange("p (c t) -> p c t", c=FC),
                 xhT.rearrange("(c p) t -> p c t", p=128))
    t_w1 = p.dma("pool", gw1_sb[:].rearrange("p (c n) -> p c n", c=FC),
                 gw1T.rearrange("(c p) n -> p c n", p=128))
    t_b1 = p.dma("pool", gb1_sb[:], gb1)
    t_gm = p.dma("sp", gam_sb[:], gamma)
    t_bt = p.dma("sp", bet_sb[:], beta)
    t_w2 = p.dma("pool", gw2_sb[:], gw2T)
    t_b2 = p.dma("pool", gb2_sb[:], gb2)
    t_or = p.op("pool", "memset", onesr_sb[:], 1.0)
    t_oc = p.op("pool", "memset", onesc_sb[:], 1.0)
    eap, etok = p.const_col(LN_EPS)

    ps_tok = [None, None]
    st_free = None
    mv_free = None
    h_toks = []
    for tt in range(TT):
        psi = (ps0, ps1)[tt % 2]
        last = None
        for c in range(FC):
            dd = [t_xh, t_w1] if last is None else [last]
            if last is None and ps_tok[tt % 2] is not None:
                dd.append(ps_tok[tt % 2])
            last = p.op("pe", "matmul", psi[0:128, 0:32],
                        xhT_sb[:, c * NTOK + tt * 128 : c * NTOK + (tt + 1) * 128],
                        gw1_sb[:, c * 32 : (c + 1) * 32],
                        start=(c == 0), stop=False, deps=dd)
        last = p.op("pe", "matmul", psi[0:128, 0:32], onesr_sb[:], gb1_sb[:],
                    start=False, stop=True, deps=[last, t_or, t_b1])
        ev = p.op("dve", "tensor_copy", h_sb[:, tt * 32 : (tt + 1) * 32],
                  psi[0:128, 0:32], deps=[last])
        ps_tok[tt % 2] = ev
        dd = [ev] + ([st_free] if st_free else [])
        stt = p.op("dve", "bn_stats", st_sb[:], h_sb[:, tt * 32 : (tt + 1) * 32],
                   deps=dd)
        dd = [stt] + ([mv_free] if mv_free else [])
        ag = p.op("dve", "bn_aggr", mv_sb[:, 0:2], st_sb[:], deps=dd)
        st_free = ag
        t3 = p.op("act", "activation", mv_sb[:, 2:3], mv_sb[:, 1:2], AF.Ln,
                  bias=eap, scale=1.0, deps=[ag, etok])
        t4 = p.op("act", "activation", mv_sb[:, 3:4], mv_sb[:, 2:3], AF.Exp,
                  scale=-0.5, deps=[t3])
        mv_free = t4
        ap_ = p.op("dve", "tensor_scalar", h_sb[:, tt * 32 : (tt + 1) * 32],
                   h_sb[:, tt * 32 : (tt + 1) * 32], mv_sb[:, 0:1], mv_sb[:, 3:4],
                   op0=ALU.subtract, op1=ALU.mult, deps=[t4])
        gm = p.op("dve", "tensor_tensor", h_sb[:, tt * 32 : (tt + 1) * 32],
                  h_sb[:, tt * 32 : (tt + 1) * 32], gam_sb[:], op=ALU.mult,
                  deps=[ap_, t_gm])
        bt = p.op("dve", "tensor_tensor", h_sb[:, tt * 32 : (tt + 1) * 32],
                  h_sb[:, tt * 32 : (tt + 1) * 32], bet_sb[:], op=ALU.add,
                  deps=[gm, t_bt])
        rl = p.op("dve", "tensor_scalar", hr_sb[:, tt * 32 : (tt + 1) * 32],
                  h_sb[:, tt * 32 : (tt + 1) * 32], 0.0, None, op0=ALU.max,
                  deps=[bt])
        h_toks.append(rl)
    pt_free = None
    hT_toks = []
    for tt in range(TT):
        dd = [h_toks[tt], t_id] + ([pt_free] if pt_free else [])
        tr = p.op("pe", "transpose", ptps[0:32, 0:128],
                  hr_sb[:, tt * 32 : (tt + 1) * 32], ident[:], deps=dd)
        ev = p.op("dve", "tensor_copy", hT_sb[0:32, tt * 128 : (tt + 1) * 128],
                  ptps[0:32, 0:128], deps=[tr])
        pt_free = ev
        hT_toks.append(ev)
    pc1_mm = None
    pc2_mm = None
    g_free = None
    m_free = None
    msk_free = None
    oh_toks = []
    g1_toks = []
    for tt in range(TT):
        dd = [hT_toks[tt], t_w2]
        if g_free is not None:
            dd.append(g_free)
        mm = p.op("pe", "matmul", psg[0:128, 0:8],
                  hT_sb[0:32, tt * 128 : (tt + 1) * 128], gw2_sb[:],
                  start=True, stop=False, deps=dd)
        mm = p.op("pe", "matmul", psg[0:128, 0:8], onesr_sb[:], gb2_sb[:],
                  start=False, stop=True, deps=[mm, t_or, t_b2])
        ev = p.op("dve", "tensor_copy", g_sb[:, tt * 8 : (tt + 1) * 8],
                  psg[0:128, 0:8], deps=[mm])
        g_free = ev
        dd = [ev] + ([m_free] if m_free else [])
        m1 = p.op("dve", "tensor_reduce", m_sb[:, 0:1],
                  g_sb[:, tt * 8 : (tt + 1) * 8], axis=AX.X, op=ALU.max, deps=dd)
        o1 = p.op("dve", "tensor_scalar", oh1_sb[:, tt * 8 : (tt + 1) * 8],
                  g_sb[:, tt * 8 : (tt + 1) * 8], m_sb[:, 0:1], None,
                  op0=ALU.is_equal, deps=[m1])
        dd = [o1] + ([msk_free] if msk_free else [])
        mk = p.op("dve", "scalar_tensor_tensor", msk_sb[:],
                  oh1_sb[:, tt * 8 : (tt + 1) * 8], -1e30,
                  g_sb[:, tt * 8 : (tt + 1) * 8], op0=ALU.mult, op1=ALU.add,
                  deps=dd)
        m2 = p.op("dve", "tensor_reduce", m_sb[:, 1:2], msk_sb[:], axis=AX.X,
                  op=ALU.max, deps=[mk])
        o2 = p.op("dve", "tensor_scalar", oh2_sb[:, tt * 8 : (tt + 1) * 8],
                  msk_sb[:], m_sb[:, 1:2], None, op0=ALU.is_equal, deps=[m2])
        msk_free = o2
        df = p.op("dve", "tensor_tensor", m_sb[:, 2:3], m_sb[:, 0:1],
                  m_sb[:, 1:2], op=ALU.subtract,
                  deps=[m2] + ([g1_toks[-1]] if g1_toks else []))
        m_free = df
        sg = p.op("act", "activation", g1_sb[:, tt : tt + 1], m_sb[:, 2:3],
                  AF.Sigmoid, deps=[df])
        g1_toks.append(sg)
        oh_toks.append((o1, o2))
        dd = [o1, t_oc]
        if pc1_mm is not None:
            dd.append(pc1_mm)
        pc1_mm = p.op("pe", "matmul", psc[0:1, 0:8], onesc_sb[:],
                      oh1_sb[:, tt * 8 : (tt + 1) * 8],
                      start=(tt == 0), stop=(tt == TT - 1), deps=dd)
        dd = [o2, t_oc]
        if pc2_mm is not None:
            dd.append(pc2_mm)
        pc2_mm = p.op("pe", "matmul", psd[0:1, 0:8], onesc_sb[:],
                      oh2_sb[:, tt * 8 : (tt + 1) * 8],
                      start=(tt == 0), stop=(tt == TT - 1), deps=dd)
    e1 = p.op("dve", "tensor_copy", pc_sb[:, 0:8], psc[0:1, 0:8],
              deps=[pc1_mm])
    e2 = p.op("dve", "tensor_copy", pc_sb[:, 8:16], psd[0:1, 0:8],
              deps=[pc2_mm, e1])
    s1 = p.dma("sp", oh1.rearrange("(t p) n -> p t n", p=128),
               oh1_sb[:].rearrange("p (t n) -> p t n", t=TT),
               deps=[t[0] for t in oh_toks])
    s2 = p.dma("sp", oh2.rearrange("(t p) n -> p t n", p=128),
               oh2_sb[:].rearrange("p (t n) -> p t n", t=TT),
               deps=[t[1] for t in oh_toks])
    s3 = p.dma("sp", g1o, g1_sb[:], deps=g1_toks)
    s4 = p.dma("sp", pco, pc_sb[:], deps=[e2])
    return [s1, s2, s3, s4]


def build_l1ffa(nc):
    p = Prog(nc)
    A = {}
    A["xhT"] = nc.dram_tensor("xhT", [D, NTOK], BF16, kind="ExternalInput").ap()
    A["gw1T"] = nc.dram_tensor("gw1T", [D, 32], F32, kind="ExternalInput").ap()
    A["gb1"] = nc.dram_tensor("gb1", [1, 32], F32, kind="ExternalInput").ap()
    A["gamma_bc"] = nc.dram_tensor("gamma_bc", [128, 32], F32, kind="ExternalInput").ap()
    A["beta_bc"] = nc.dram_tensor("beta_bc", [128, 32], F32, kind="ExternalInput").ap()
    A["gw2T"] = nc.dram_tensor("gw2T", [32, 8], F32, kind="ExternalInput").ap()
    A["gb2"] = nc.dram_tensor("gb2", [1, 8], F32, kind="ExternalInput").ap()
    A["oh1"] = nc.dram_tensor("oh1", [NTOK, 8], BF16, kind="ExternalOutput").ap()
    A["oh2"] = nc.dram_tensor("oh2", [NTOK, 8], BF16, kind="ExternalOutput").ap()
    A["g1"] = nc.dram_tensor("g1", [128, TT], F32, kind="ExternalOutput").ap()
    A["pc"] = nc.dram_tensor("pc", [1, 16], F32, kind="ExternalOutput").ap()
    toks = emit_l1ffa(p, nc, A, "", nc.alloc_sbuf_tensor, nc.alloc_psum_tensor)
    p.finish(toks)
    p.emit()
    return nc


CAPACITY = 3072.0
ALPHA = 1.0 / 16.0


def emit_l1ffb(p, nc, A, pfx, asb, aps_):
    xhT = A["xhT"]
    pcall = A.get("pcall")
    oh1 = A["oh1"]
    oh2 = A["oh2"]
    g1 = A["g1"]
    wT = A["wT"]
    lA = A["lA"]
    lB = A["lB"]
    xres = A["xres"]
    xout = A["xout"]

    xhT_sb = asb(pfx + "xhT_sb", [128, FC * NTOK], BF16)
    pc_sb = asb(pfx + "pc_sb", [8, 16], F32)
    cnt_sb = asb(pfx + "cnt_sb", [1, 16], F32)
    alw_sb = asb(pfx + "alw_sb", [1, 16], F32)
    abc_sb = asb(pfx + "abc_sb", [128, 16], F32)
    oh1_sb = asb(pfx + "oh1_sb", [128, TT * 8], BF16)
    oh2_sb = asb(pfx + "oh2_sb", [128, TT * 8], BF16)
    g1_sb = asb(pfx + "g1_sb", [128, TT], F32)
    g2_sb = asb(pfx + "g2_sb", [128, TT], F32)
    wg_sb = asb(pfx + "wg_sb", [128, TT * 8], F32)
    a_sb = asb(pfx + "a_sb", [128, 8], F32)
    lA_sb = asb(pfx + "lA_sb", [128, FC * 128], BF16)
    lB_sb = asb(pfx + "lB_sb", [128, FC * 128], BF16)
    tmp_sb = asb(pfx + "tmp_sb", [128, TT * 128], BF16)
    tmpT_sb = asb(pfx + "tmpT_sb", [128, TT * 128], BF16)
    wc_sb = [asb(pfx + f"wc_sb{i}", [128, FC * 512], BF16) for i in range(2)]
    xr_sb = [asb(pfx + f"xr_sb{i}", [128, 512], F32) for i in range(2)]
    xo_sb = [asb(pfx + f"xo_sb{i}", [128, 512], F32) for i in range(2)]
    onesf_sb = asb(pfx + "onesf_sb", [1, 128], F32)
    ones8_sb = asb(pfx + "ones8_sb", [8, 1], F32)

    ps0 = aps_(pfx + "ps0", [128, 512], F32)
    ps1 = aps_(pfx + "ps1", [128, 512], F32)
    ptps = aps_(pfx + "ptps", [128, 1024], BF16)

    ident, t_id = emit_ident(p, nc, asb, pfx + "ident")
    t_xh = p.dma("sp", xhT_sb[:].rearrange("p (c t) -> p c t", c=FC),
                 xhT.rearrange("(c p) t -> p c t", p=128))
    t_pc = p.dma("sp", pc_sb[:], pcall) if pcall is not None else None
    t_o1 = p.dma("sp", oh1_sb[:].rearrange("p (t n) -> p t n", t=TT),
                 oh1.rearrange("(t p) n -> p t n", p=128))
    t_o2 = p.dma("sp", oh2_sb[:].rearrange("p (t n) -> p t n", t=TT),
                 oh2.rearrange("(t p) n -> p t n", p=128))
    t_g1 = p.dma("sp", g1_sb[:], g1)
    t_lA = p.dma("pool", lA_sb[:].rearrange("p (c n) -> p c n", c=FC),
                 lA.rearrange("(c p) n -> p c n", p=128))
    t_lB = p.dma("pool", lB_sb[:], lB)
    t_of = p.op("pool", "memset", onesf_sb[:], 1.0)
    t_o8 = p.op("pool", "memset", ones8_sb[:], 1.0)

    if "pcr" in A:
        ev_c = p.dma("sp", cnt_sb[:], A["pcr"])
    else:
        mm_c = p.op("pe", "matmul", ps0[0:1, 0:16], ones8_sb[:], pc_sb[:],
                    start=True, stop=True, deps=[t_pc, t_o8])
        ev_c = p.op("dve", "tensor_copy", cnt_sb[:], ps0[0:1, 0:16], deps=[mm_c])
    alw = p.op("dve", "tensor_scalar", alw_sb[:], cnt_sb[:], CAPACITY, None,
               op0=ALU.is_le, deps=[ev_c])
    mm_b = p.op("pe", "matmul", ps0[0:128, 16:32], onesf_sb[:], alw_sb[:],
                start=True, stop=True, deps=[alw, t_of, ev_c])
    ev_b = p.op("dve", "tensor_copy", abc_sb[:], ps0[0:128, 16:32], deps=[mm_b])
    wg_toks = []
    a_free = None
    for tt in range(TT):
        g2 = p.op("dve", "tensor_scalar", g2_sb[:, tt : tt + 1],
                  g1_sb[:, tt : tt + 1], -1.0, 1.0, op0=ALU.mult, op1=ALU.add,
                  deps=[t_g1] + ([wg_toks[-1]] if wg_toks else []))
        dd = [t_o1, g2, ev_b]
        if a_free is not None:
            dd.append(a_free)
        a1 = p.op("dve", "tensor_scalar", a_sb[:], oh1_sb[:, tt * 8 : (tt + 1) * 8],
                  g1_sb[:, tt : tt + 1], None, op0=ALU.mult, deps=dd)
        a1b = p.op("dve", "tensor_tensor", wg_sb[:, tt * 8 : (tt + 1) * 8],
                   a_sb[:], abc_sb[:, 0:8], op=ALU.mult, deps=[a1])
        a2 = p.op("dve", "tensor_scalar", a_sb[:], oh2_sb[:, tt * 8 : (tt + 1) * 8],
                  g2_sb[:, tt : tt + 1], None, op0=ALU.mult, deps=[a1b, t_o2])
        a2b = p.op("dve", "tensor_tensor", a_sb[:], a_sb[:], abc_sb[:, 8:16],
                   op=ALU.mult, deps=[a2])
        a_free = a2b
        w = p.op("dve", "tensor_tensor", wg_sb[:, tt * 8 : (tt + 1) * 8],
                 wg_sb[:, tt * 8 : (tt + 1) * 8], a_sb[:], op=ALU.add,
                 deps=[a2b])
        wg_toks.append(w)
    tmp_toks = []
    ps_tok2 = None
    for tt in range(TT):
        last = None
        for c in range(FC):
            dd = [t_xh, t_lA] if last is None else [last]
            if last is None and ps_tok2 is not None:
                dd.append(ps_tok2)
            last = p.op("pe", "matmul", ps1[0:128, 0:128],
                        xhT_sb[:, c * NTOK + tt * 128 : c * NTOK + (tt + 1) * 128],
                        lA_sb[:, c * 128 : (c + 1) * 128],
                        start=(c == 0), stop=(c == FC - 1), deps=dd)
        sc_toks = []
        for e in range(8):
            sc = p.op("dve", "tensor_scalar",
                      tmp_sb[:, tt * 128 + e * 16 : tt * 128 + (e + 1) * 16],
                      ps1[0:128, e * 16 : (e + 1) * 16],
                      wg_sb[:, tt * 8 + e : tt * 8 + e + 1], ALPHA,
                      op0=ALU.mult, op1=ALU.mult, deps=[last, wg_toks[tt]])
            sc_toks.append(sc)
        ps_tok2 = sc_toks[-1]
        tmp_toks.append(sc_toks)
    pt_free = None
    tmpT_toks = []
    for tt in range(TT):
        dd = tmp_toks[tt] + [t_id] + ([pt_free] if pt_free else [])
        tr = p.op("pe", "transpose", ptps[0:128, 0:128],
                  tmp_sb[:, tt * 128 : (tt + 1) * 128], ident[:], deps=dd)
        ev = p.op("dve", "tensor_copy", tmpT_sb[:, tt * 128 : (tt + 1) * 128],
                  ptps[0:128, 0:128], deps=[tr])
        pt_free = ev
        tmpT_toks.append(ev)
    w_tok = [None, None]
    ps_tok = [ev_b, ps_tok2]
    xr_tok = [None, None]
    xo_tok = [None, None]
    st_toks = []
    for cb in range(8):
        wi = cb % 2
        dd = [] if w_tok[wi] is None else [w_tok[wi]]
        t_w = p.dma("pool", wc_sb[wi][:].rearrange("p (c n) -> p c n", c=FC),
                    wT[:, cb * 512 : (cb + 1) * 512].rearrange("(c p) n -> p c n", p=128),
                    deps=dd, stream=f"w{wi}")
        last_any = None
        for tt in range(TT):
            i = (cb * TT + tt) % 2
            dx = [] if xr_tok[i] is None else [xr_tok[i]]
            t_xr = p.dma("act", xr_sb[i][:],
                         xres[tt * 128 : (tt + 1) * 128, cb * 512 : (cb + 1) * 512],
                         deps=dx, stream=f"xr{i}")
            last = None
            for c in range(FC):
                dd = [t_w, t_xh] if last is None else [last]
                if last is None and ps_tok[i] is not None:
                    dd.append(ps_tok[i])
                last = p.op("pe", "matmul", (ps0, ps1)[i][0:128, 0:512],
                            xhT_sb[:, c * NTOK + tt * 128 : c * NTOK + (tt + 1) * 128],
                            wc_sb[wi][:, c * 512 : (c + 1) * 512],
                            start=(c == 0), stop=False, deps=dd)
            last = p.op("pe", "matmul", (ps0, ps1)[i][0:128, 0:512],
                        tmpT_sb[:, tt * 128 : (tt + 1) * 128],
                        lB_sb[:, cb * 512 : (cb + 1) * 512],
                        start=False, stop=True, deps=[last, tmpT_toks[tt], t_lB])
            last_any = last
            dadd = [last, t_xr]
            if xo_tok[i] is not None:
                dadd.append(xo_tok[i])
            add = p.op("dve", "tensor_tensor", xo_sb[i][:],
                       (ps0, ps1)[i][0:128, 0:512], xr_sb[i][:], op=ALU.add,
                       deps=dadd)
            ps_tok[i] = add
            xr_tok[i] = add
            st = p.dma("sp", xout[tt * 128 : (tt + 1) * 128, cb * 512 : (cb + 1) * 512],
                       xo_sb[i][:], deps=[add], stream=f"xo{i}")
            xo_tok[i] = st
            st_toks.append(st)
        w_tok[wi] = last_any
    return st_toks


def build_l1ffb(nc):
    p = Prog(nc)
    A = {}
    A["xhT"] = nc.dram_tensor("xhT", [D, NTOK], BF16, kind="ExternalInput").ap()
    A["pcall"] = nc.dram_tensor("pcall", [8, 16], F32, kind="ExternalInput").ap()
    A["oh1"] = nc.dram_tensor("oh1", [NTOK, 8], BF16, kind="ExternalInput").ap()
    A["oh2"] = nc.dram_tensor("oh2", [NTOK, 8], BF16, kind="ExternalInput").ap()
    A["g1"] = nc.dram_tensor("g1", [128, TT], F32, kind="ExternalInput").ap()
    A["wT"] = nc.dram_tensor("wT", [D, D], F32, kind="ExternalInput").ap()
    A["lA"] = nc.dram_tensor("lA", [D, 128], F32, kind="ExternalInput").ap()
    A["lB"] = nc.dram_tensor("lB", [128, D], F32, kind="ExternalInput").ap()
    A["xres"] = nc.dram_tensor("xres", [NTOK, D], F32, kind="ExternalInput").ap()
    A["xout"] = nc.dram_tensor("xout", [NTOK, D], F32, kind="ExternalOutput").ap()
    toks = emit_l1ffb(p, nc, A, "", nc.alloc_sbuf_tensor, nc.alloc_psum_tensor)
    p.finish(toks)
    p.emit()
    return nc



FUSED = False


def _run(build_fn, in_maps, out_names):
    nc = bass.Bass("TRN2", target_bir_lowering=False)
    build_fn(nc)
    nc.finalize()
    ins = set()
    for alloc in nc.m.functions[0].allocations:
        if isinstance(alloc, mybir.MemoryLocationSet) and alloc.kind == "ExternalInput":
            ins.add(alloc.memorylocations[0].name)
    in_maps = [{k: v for k, v in m.items() if k in ins} for m in in_maps]
    res = run_bass_kernel_spmd(nc, in_maps, core_ids=list(range(NCORES)))
    return [{k: r[k] for k in out_names} for r in res.results]




def emit_lnT(p, nc, asb, x_ap, xh_h, xhT_h, pfx):
    xh_sb = asb(pfx + "xh_sb", [128, TT * D], BF16)
    trb = [asb(pfx + f"trb{i}", [128, NTOK], BF16) for i in range(2)]
    toks, _, _ = emit_ln(p, nc, x_ap, xh_sb, TT, pfx + "ln", asb=asb)
    st = p.dma("sp", xh_h.rearrange("(t p) d -> p t d", p=128),
               xh_sb[:].rearrange("p (t d) -> p t d", t=TT), deps=toks,
               stream="st")
    tr_tok = [None, None]
    sts = []
    for j in range(FC):
        i = j % 2
        dd = [st] + ([tr_tok[i]] if tr_tok[i] else [])
        t_tr = p.dma("sp", trb[i][:], xh_h[:, j * 128 : (j + 1) * 128],
                     deps=dd, transpose=True, stream=f"tr{i}")
        t_st = p.dma("act", xhT_h[j * 128 : (j + 1) * 128, :], trb[i][:],
                     deps=[t_tr], stream=f"ts{i}")
        tr_tok[i] = t_st
        sts.append(t_st)
    return sts


def build_fused(nc):
    p = Prog(nc)
    EI = lambda n, shp, dt: nc.dram_tensor(n, shp, dt, kind="ExternalInput").ap()
    IN = lambda n, shp, dt: nc.dram_tensor(n, shp, dt, kind="Internal").ap()
    x = EI("x", [NTOK, D], F32)
    xout = nc.dram_tensor("xout", [NTOK, D], F32, kind="ExternalOutput").ap()
    W = {}
    for n, shp, dt in [
        ("WqdT", [12, D], F32), ("Wqk", [12, 128], F32), ("bqd", [12, 1], F32),
        ("bqk", [1, 128], F32), ("Wkvd", [D, 4], F32), ("bkv", [1, 4], F32),
        ("Wo", [D, D], F32), ("WvuT", [128, 128], F32), ("bvu", [128, FC], F32),
        ("bo", [1, D], F32), ("WstkT", [D, 24], F32), ("bases_bc", [128, 24], F32),
        ("gate", [1, 1], F32), ("Wbr", [1024, 1024], F32),
        ("wqkv", [D, 6144], F32), ("SDrep", [128, 4608], F32),
        ("scales_bc", [128, 16], F32), ("wo", [D, D], F32),
        ("gw1T", [D, 32], F32), ("gb1", [1, 32], F32),
        ("gamma_bc", [128, 32], F32), ("beta_bc", [128, 32], F32),
        ("gw2T", [32, 8], F32), ("gb2", [1, 8], F32),
        ("wT", [D, D], F32), ("lA", [D, 128], F32), ("lB", [128, D], F32),
    ]:
        W[n] = EI(n, shp, dt)
    xh_h = IN("xh_h", [NTOK, D], BF16)
    xhT_h = IN("xhT_h", [D, NTOK], BF16)
    x1_h = IN("x1_h", [NTOK, D], F32)
    x2_h = IN("x2_h", [NTOK, D], F32)
    x4_h = IN("x4_h", [NTOK, D], F32)
    qnat_h = IN("qnat_h", [NTOK, 128], BF16)
    qT_h = IN("qT_h", [128, NTOK], BF16)
    ckv_h = IN("ckv_h", [NTOK, 4], F32)
    ckv_g = IN("ckv_g", [B * S, 4], F32)
    wcomb_h = IN("wcomb_h", [128, D], BF16)
    qkvT_h = IN("qkvT_h", [6144, NTOK], BF16)
    qA2A = IN("qA2A", [S, NTOK], BF16)
    kA2A = IN("kA2A", [1024, NTOK], BF16)
    vA2A = IN("vA2A", [1024, NTOK], BF16)
    outn_s = IN("outn_s", [S, NTOK], BF16)
    ot_h = IN("ot_h", [D, NTOK], BF16)
    oh1_h = IN("oh1_h", [NTOK, 8], BF16)
    oh2_h = IN("oh2_h", [NTOK, 8], BF16)
    g1_h = IN("g1_h", [128, TT], F32)
    pc_h = IN("pc_h", [1, 16], F32)
    pcr_h = IN("pcr_h", [1, 16], F32)
    G4 = [[0, 1, 2, 3], [4, 5, 6, 7]]
    G8 = [list(range(8))]

    stack = [None]

    def asb(name, shape, dtype):
        return stack[0].enter_context(nc.sbuf_tensor(name, shape, dtype))

    def aps_(name, shape, dtype):
        return stack[0].enter_context(nc.psum_tensor(name, shape, dtype))

    def begin():
        stack[0] = contextlib.ExitStack()

    def end():
        p.barrier()
        stack[0].close()

    begin()
    emit_lnT(p, nc, asb, x, xh_h, xhT_h, "p1_")
    end()
    begin()
    A = dict(W)
    A.update(xhT=xhT_h, qnat=qnat_h, ckv=ckv_h)
    emit_l0proj(p, nc, A, "p2_", asb, aps_)
    p.barrier()
    qtb = asb("p2_qtb", [128, NTOK], BF16)
    t1 = p.dma("sp", qtb[:], qnat_h, transpose=True, stream="p2_qtr")
    p.dma("sp", qT_h[:, :], qtb[:], deps=[t1], stream="p2_qst")
    p.cc("AllGather", ALU.bypass, G8, [ckv_h], [ckv_g])
    end()
    begin()
    A = dict(W)
    A.update(qT=qT_h, ckv_g=ckv_g, xres=x, xout=x1_h, wcomb_scratch=wcomb_h)
    emit_l0core(p, nc, A, "p3_", asb, aps_)
    end()
    begin()
    emit_lnT(p, nc, asb, x1_h, xh_h, xhT_h, "p4_")
    end()
    begin()
    A = dict(W)
    A.update(xh2=xh_h, xh2T=xhT_h, xres=x1_h, xout=x2_h)
    emit_l0ffn(p, nc, A, "p5_", asb, aps_)
    end()
    begin()
    emit_lnT(p, nc, asb, x2_h, xh_h, xhT_h, "p6_")
    end()
    begin()
    A = dict(W)
    A.update(xhT=xhT_h, qkvT=qkvT_h)
    emit_l1qkv(p, nc, A, "p7_", asb, aps_)
    p.barrier()
    p.cc("AllToAll", ALU.bypass, G8, [qkvT_h[0:4096, :]], [qA2A])
    p.cc("AllToAll", ALU.bypass, G8, [qkvT_h[4096:5120, :]], [kA2A])
    p.cc("AllToAll", ALU.bypass, G8, [qkvT_h[5120:6144, :]], [vA2A])
    end()
    begin()
    A = dict(W)
    A.update(qA2A=qA2A, kA2A=kA2A, vA2A=vA2A, OUTN=outn_s)
    emit_l1att(p, nc, A, "p8_", asb, aps_)
    p.barrier()
    p.cc("AllToAll", ALU.bypass, G8, [outn_s], [ot_h])
    end()
    begin()
    A = dict(W)
    A.update(OT=ot_h, xres=x2_h, xout=x4_h)
    emit_l1wo(p, nc, A, "p9_", asb, aps_)
    end()
    begin()
    emit_lnT(p, nc, asb, x4_h, xh_h, xhT_h, "pg_")
    end()
    begin()
    A = dict(W)
    A.update(xhT=xhT_h, oh1=oh1_h, oh2=oh2_h, g1=g1_h, pc=pc_h)
    emit_l1ffa(p, nc, A, "pa_", asb, aps_)
    p.barrier()
    p.cc("AllReduce", ALU.add, G8, [pc_h], [pcr_h])
    end()
    begin()
    A = dict(W)
    A.update(xhT=xhT_h, pcr=pcr_h, oh1=oh1_h, oh2=oh2_h, g1=g1_h,
             xres=x4_h, xout=xout)
    toks = emit_l1ffb(p, nc, A, "pb_", asb, aps_)
    p.finish(toks)
    stack[0].close()
    p.emit()
    return nc




def _mk_phase_tools(nc, p):
    stack = [None]

    def asb(name, shape, dtype):
        return stack[0].enter_context(nc.sbuf_tensor(name, shape, dtype))

    def aps_(name, shape, dtype):
        return stack[0].enter_context(nc.psum_tensor(name, shape, dtype))

    def begin():
        stack[0] = contextlib.ExitStack()

    def end():
        p.barrier()
        stack[0].close()

    return asb, aps_, begin, end, stack


def _ext_weights(nc, names):
    shapes = {
        "WqdT": ([12, D], F32), "Wqk": ([12, 128], F32), "bqd": ([12, 1], F32),
        "bqk": ([1, 128], F32), "Wkvd": ([D, 4], F32), "bkv": ([1, 4], F32),
        "Wo": ([D, D], F32), "WvuT": ([128, 128], F32), "bvu": ([128, FC], F32),
        "bo": ([1, D], F32), "WstkT": ([D, 24], F32), "bases_bc": ([128, 24], F32),
        "gate": ([1, 1], F32), "Wbr": ([1024, 1024], F32),
        "wqkv": ([D, 6144], F32), "SDrep": ([128, 4608], F32),
        "scales_bc": ([128, 16], F32), "wo": ([D, D], F32),
        "gw1T": ([D, 32], F32), "gb1": ([1, 32], F32),
        "gamma_bc": ([128, 32], F32), "beta_bc": ([128, 32], F32),
        "gw2T": ([32, 8], F32), "gb2": ([1, 8], F32),
        "wT": ([D, D], F32), "lA": ([D, 128], F32), "lB": ([128, D], F32),
    }
    return {n: nc.dram_tensor(n, *shapes[n], kind="ExternalInput").ap() for n in names}


def build_seg1(nc):
    p = Prog(nc)
    x = nc.dram_tensor("x", [NTOK, D], F32, kind="ExternalInput").ap()
    qT = nc.dram_tensor("qT", [128, NTOK], BF16, kind="ExternalOutput").ap()
    ckv = nc.dram_tensor("ckv", [NTOK, 4], F32, kind="ExternalOutput").ap()
    qnat_h = nc.dram_tensor("qnat_h", [NTOK, 128], BF16, kind="Internal").ap()
    xh_h = nc.dram_tensor("xh_h", [NTOK, D], BF16, kind="Internal").ap()
    xhT_h = nc.dram_tensor("xhT_h", [D, NTOK], BF16, kind="Internal").ap()
    W = _ext_weights(nc, ["WqdT", "Wqk", "bqd", "bqk", "Wkvd", "bkv"])
    asb, aps_, begin, end, stack = _mk_phase_tools(nc, p)
    begin()
    emit_lnT(p, nc, asb, x, xh_h, xhT_h, "p1_")
    end()
    begin()
    A = dict(W)
    A.update(xhT=xhT_h, qnat=qnat_h, ckv=ckv)
    toks = emit_l0proj(p, nc, A, "p2_", asb, aps_)
    p.barrier()
    qtb = asb("p2_qtb", [128, NTOK], BF16)
    t1 = p.dma("sp", qtb[:], qnat_h, transpose=True, stream="p2qtr")
    t2 = p.dma("sp", qT[:, :], qtb[:], deps=[t1], stream="p2qst")
    p.finish(list(toks) + [t2])
    stack[0].close()
    p.emit()
    return nc


def build_seg2(nc):
    p = Prog(nc)
    x = nc.dram_tensor("x", [NTOK, D], F32, kind="ExternalInput").ap()
    qT_h = nc.dram_tensor("qT", [128, NTOK], BF16, kind="ExternalInput").ap()
    ckv_g = nc.dram_tensor("ckv_g", [B * S, 4], F32, kind="ExternalInput").ap()
    qkvT = nc.dram_tensor("qkvT", [6144, NTOK], BF16, kind="ExternalOutput").ap()
    x2o = nc.dram_tensor("x2o", [NTOK, D], F32, kind="ExternalOutput").ap()
    x1_h = nc.dram_tensor("x1_h", [NTOK, D], F32, kind="Internal").ap()
    xh_h = nc.dram_tensor("xh_h", [NTOK, D], BF16, kind="Internal").ap()
    xhT_h = nc.dram_tensor("xhT_h", [D, NTOK], BF16, kind="Internal").ap()
    wcomb_h = nc.dram_tensor("wcomb_h", [128, D], BF16, kind="Internal").ap()
    W = _ext_weights(nc, ["Wo", "WvuT", "bvu", "bo", "WstkT", "bases_bc",
                          "gate", "Wbr", "wqkv"])
    asb, aps_, begin, end, stack = _mk_phase_tools(nc, p)
    begin()
    A = dict(W)
    A.update(qT=qT_h, ckv_g=ckv_g, xres=x, xout=x1_h, wcomb_scratch=wcomb_h)
    emit_l0core(p, nc, A, "p3_", asb, aps_)
    end()
    begin()
    emit_lnT(p, nc, asb, x1_h, xh_h, xhT_h, "p4_")
    end()
    begin()
    A = dict(W)
    A.update(xh2=xh_h, xh2T=xhT_h, xres=x1_h, xout=x2o)
    t5 = emit_l0ffn(p, nc, A, "p5_", asb, aps_)
    end()
    begin()
    emit_lnT(p, nc, asb, x2o, xh_h, xhT_h, "p6_")
    end()
    begin()
    A = dict(W)
    A.update(xhT=xhT_h, qkvT=qkvT)
    t7 = emit_l1qkv(p, nc, A, "p7_", asb, aps_)
    p.finish(list(t5) + list(t7))
    stack[0].close()
    p.emit()
    return nc


def build_seg3(nc):
    p = Prog(nc)
    qA2A = nc.dram_tensor("qA2A", [S, NTOK], BF16, kind="ExternalInput").ap()
    kA2A = nc.dram_tensor("kA2A", [1024, NTOK], BF16, kind="ExternalInput").ap()
    vA2A = nc.dram_tensor("vA2A", [1024, NTOK], BF16, kind="ExternalInput").ap()
    outn = nc.dram_tensor("outn", [S, NTOK], BF16, kind="ExternalOutput").ap()
    W = _ext_weights(nc, ["SDrep", "scales_bc"])
    asb, aps_, begin, end, stack = _mk_phase_tools(nc, p)
    begin()
    A = dict(W)
    A.update(qA2A=qA2A, kA2A=kA2A, vA2A=vA2A, OUTN=outn)
    toks = emit_l1att(p, nc, A, "p8_", asb, aps_)
    p.finish(toks)
    stack[0].close()
    p.emit()
    return nc


def build_seg4(nc):
    p = Prog(nc)
    ot = nc.dram_tensor("ot", [D, NTOK], BF16, kind="ExternalInput").ap()
    x2 = nc.dram_tensor("x2", [NTOK, D], F32, kind="ExternalInput").ap()
    x4o = nc.dram_tensor("x4o", [NTOK, D], F32, kind="ExternalOutput").ap()
    xh4T = nc.dram_tensor("xh4T", [D, NTOK], BF16, kind="ExternalOutput").ap()
    oh1 = nc.dram_tensor("oh1", [NTOK, 8], BF16, kind="ExternalOutput").ap()
    oh2 = nc.dram_tensor("oh2", [NTOK, 8], BF16, kind="ExternalOutput").ap()
    g1 = nc.dram_tensor("g1", [128, TT], F32, kind="ExternalOutput").ap()
    pc = nc.dram_tensor("pc", [1, 16], F32, kind="ExternalOutput").ap()
    xh_h = nc.dram_tensor("xh_h", [NTOK, D], BF16, kind="Internal").ap()
    W = _ext_weights(nc, ["wo", "gw1T", "gb1", "gamma_bc", "beta_bc",
                          "gw2T", "gb2"])
    asb, aps_, begin, end, stack = _mk_phase_tools(nc, p)
    begin()
    A = dict(W)
    A.update(OT=ot, xres=x2, xout=x4o)
    emit_l1wo(p, nc, A, "p9_", asb, aps_)
    end()
    begin()
    emit_lnT(p, nc, asb, x4o, xh_h, xh4T, "pg_")
    end()
    begin()
    A = dict(W)
    A.update(xhT=xh4T, oh1=oh1, oh2=oh2, g1=g1, pc=pc)
    toks = emit_l1ffa(p, nc, A, "pa_", asb, aps_)
    p.finish(toks)
    stack[0].close()
    p.emit()
    return nc


def build_seg5(nc):
    p = Prog(nc)
    xh4T = nc.dram_tensor("xh4T", [D, NTOK], BF16, kind="ExternalInput").ap()
    pcall = nc.dram_tensor("pcall", [8, 16], F32, kind="ExternalInput").ap()
    oh1 = nc.dram_tensor("oh1", [NTOK, 8], BF16, kind="ExternalInput").ap()
    oh2 = nc.dram_tensor("oh2", [NTOK, 8], BF16, kind="ExternalInput").ap()
    g1 = nc.dram_tensor("g1", [128, TT], F32, kind="ExternalInput").ap()
    x4 = nc.dram_tensor("x4", [NTOK, D], F32, kind="ExternalInput").ap()
    xout = nc.dram_tensor("xout", [NTOK, D], F32, kind="ExternalOutput").ap()
    W = _ext_weights(nc, ["wT", "lA", "lB"])
    asb, aps_, begin, end, stack = _mk_phase_tools(nc, p)
    begin()
    A = dict(W)
    A.update(xhT=xh4T, pcall=pcall, oh1=oh1, oh2=oh2, g1=g1, xres=x4, xout=xout)
    toks = emit_l1ffb(p, nc, A, "pb_", asb, aps_)
    p.finish(toks)
    stack[0].close()
    p.emit()
    return nc


def _alibi_slopes(n):
    cp2 = 2 ** math.floor(math.log2(n))
    base = 2.0 ** (-8.0 / cp2)
    slopes = base ** np.arange(1, cp2 + 1, dtype=np.float32)
    if cp2 != n:
        eb = 2.0 ** (-4.0 / cp2)
        extra = eb ** np.arange(1, 1 + 2 * (n - cp2), 2, dtype=np.float32)
        slopes = np.concatenate([slopes, extra])
    return slopes.astype(np.float32)


def _sd_rep():
    pp = np.arange(128)[:, None]
    cc = np.arange(4608)[None, :]
    r = cc - 512 - pp
    return np.where(r >= 0, np.sqrt(np.maximum(r, 0).astype(np.float64)),
                    1e30).astype(np.float32)


def _kernel_unfused_obsolete(**inputs):
    asc = np.ascontiguousarray
    NH, NKV, Dh = 32, 8, 128

    x = np.asarray(inputs["L0_attn_x"], np.float32).reshape(B * S, D)
    xc = [shard_rows(x, c) for c in range(NCORES)]

    def run_ln(xcur):
        r = _run(build_ln_launch, [{"x_in": xcur[c]} for c in range(NCORES)],
                 ["xh_out"])
        xh = [r[c]["xh_out"] for c in range(NCORES)]
        xhT = [asc(h.T) for h in xh]
        return xh, xhT

    _, xh1T = run_ln(xc)
    WqdT = asc(np.asarray(inputs["L0_attn_Wq_d"], np.float32).T)
    com = {
        "WqdT": WqdT,
        "Wqk": np.asarray(inputs["L0_attn_W_qk"], np.float32),
        "bqd": asc(np.asarray(inputs["L0_attn_bq_d"], np.float32)[:, None]),
        "bqk": asc(np.asarray(inputs["L0_attn_b_qk"], np.float32)[None, :]),
        "Wkvd": np.asarray(inputs["L0_attn_Wkv_d"], np.float32),
        "bkv": asc(np.asarray(inputs["L0_attn_bkv_d"], np.float32)[None, :]),
    }
    r = _run(build_l0proj, [dict(com, xhT=xh1T[c]) for c in range(NCORES)],
             ["qnat", "ckv"])
    ckv_full = np.zeros((B * S, 4), np.float32)
    unshard_rows([r[c]["ckv"] for c in range(NCORES)], ckv_full)
    Wv_u = np.asarray(inputs["L0_attn_Wv_u"], np.float32)
    WvuT = asc(np.concatenate(
        [Wv_u[:, h * 128 : (h + 1) * 128].T for h in range(32)], axis=1))
    in3 = []
    for c in range(NCORES):
        b = c // 4
        cf = ckv_full[b * S : (b + 1) * S]
        ckvT32 = np.zeros((32, S), np.float32)
        ckvT32[0:4] = cf.T
        ckv5 = asc(np.concatenate([cf, np.ones((S, 1), np.float32)], 1))
        in3.append({
            "qT": asc(r[c]["qnat"].T),
            "ckvT32": ckvT32,
            "ckv5": ckv5,
            "Wo": np.asarray(inputs["L0_attn_Wo"], np.float32),
            "WvuT": WvuT,
            "bvu": asc(np.asarray(inputs["L0_attn_bv_u"], np.float32).reshape(32, 128).T),
            "bo": asc(np.asarray(inputs["L0_attn_bo"], np.float32)[None, :]),
            "xres": xc[c],
        })
    r = _run(build_l0core, in3, ["xout"])
    x1c = [r[c]["xout"] for c in range(NCORES)]

    xh2, xh2T = run_ln(x1c)
    WstkT = asc(np.concatenate(
        [np.asarray(inputs["L0_ffn_W_res"], np.float32),
         np.asarray(inputs["L0_ffn_W_pre"], np.float32),
         np.asarray(inputs["L0_ffn_W_post"], np.float32)], 0).T)
    bases = np.concatenate(
        [np.asarray(inputs["L0_ffn_H_res_base"], np.float32).reshape(-1),
         np.asarray(inputs["L0_ffn_H_pre_base"], np.float32),
         np.asarray(inputs["L0_ffn_H_post_base"], np.float32)]).astype(np.float32)
    bases_bc = asc(np.broadcast_to(bases[None, :], (128, 24)))
    com = {
        "WstkT": WstkT, "bases_bc": bases_bc,
        "gate": np.asarray(inputs["L0_ffn_gate"], np.float32).reshape(1, 1),
        "Wbr": np.asarray(inputs["L0_ffn_W_branch"], np.float32),
    }
    r = _run(build_l0ffn,
             [dict(com, xh2=xh2[c], xh2T=xh2T[c], xres=x1c[c])
              for c in range(NCORES)], ["xout"])
    x2c = [r[c]["xout"] for c in range(NCORES)]

    _, xh3T = run_ln(x2c)
    w_qkv = np.asarray(inputs["L1_attn_w_qkv"], np.float32)
    r = _run(build_l1qkv, [{"xhT": xh3T[c], "wqkv": w_qkv} for c in range(NCORES)],
             ["qkvT"])
    qkvTc = [r[c]["qkvT"] for c in range(NCORES)]
    qkvT_b = [np.zeros((6144, S), BF) for _ in range(B)]
    for c in range(NCORES):
        b = c // 4
        o = 0
        for (st, n) in core_rows_slices(c):
            qkvT_b[b][:, st - b * S : st - b * S + n] = qkvTc[c][:, o : o + n]
            o += n
    slopes = _alibi_slopes(NH)
    SDrep = _sd_rep()
    in_att = []
    for c in range(NCORES):
        b, s_ = c // 4, c % 4
        units = [s_ * 8 + u for u in range(8)]
        QTs = asc(qkvT_b[b][units[0] * 128 : (units[-1] + 1) * 128, :])
        kvh = [units[0] // 4, units[4] // 4]
        KTc = np.concatenate(
            [qkvT_b[b][NH * Dh + hh * Dh : NH * Dh + (hh + 1) * Dh, :] for hh in kvh], 0)
        Vc = asc(np.concatenate(
            [qkvT_b[b][NH * Dh + NKV * Dh + hh * Dh : NH * Dh + NKV * Dh + (hh + 1) * Dh, :]
             for hh in kvh], 0).T)
        scales_bc = np.zeros((128, 16), np.float32)
        for i, uu in enumerate(units):
            scales_bc[:, 2 * i] = 1.0 / (slopes[uu] * np.sqrt(Dh))
            scales_bc[:, 2 * i + 1] = slopes[uu]
        in_att.append({"QTs": QTs, "KTc": asc(KTc), "Vc": Vc, "SDrep": SDrep,
                       "scales_bc": scales_bc})
    r = _run(build_l1att, in_att, ["OUTN"])
    OUTN_b = [np.zeros((D, S), BF) for _ in range(B)]
    for c in range(NCORES):
        b, s_ = c // 4, c % 4
        OUTN_b[b][s_ * 8 * 128 : (s_ * 8 + 8) * 128, :] = r[c]["OUTN"]
    in_wo = []
    w_o = np.asarray(inputs["L1_attn_w_o"], np.float32)
    for c in range(NCORES):
        b = c // 4
        cols = []
        for (st, n) in core_rows_slices(c):
            cols.append(OUTN_b[b][:, st - b * S : st - b * S + n])
        in_wo.append({"OT": asc(np.concatenate(cols, 1)), "wo": w_o,
                      "xres": x2c[c]})
    r = _run(build_l1wo, in_wo, ["xout"])
    x4c = [r[c]["xout"] for c in range(NCORES)]

    _, xh4T = run_ln(x4c)
    gw1 = np.asarray(inputs["L1_ffn_gw1"], np.float32)
    com = {
        "gw1T": asc(gw1.T),
        "gb1": asc(np.asarray(inputs["L1_ffn_gb1"], np.float32)[None, :]),
        "gamma_bc": asc(np.broadcast_to(
            np.asarray(inputs["L1_ffn_ln_gamma"], np.float32)[None, :], (128, 32))),
        "beta_bc": asc(np.broadcast_to(
            np.asarray(inputs["L1_ffn_ln_beta"], np.float32)[None, :], (128, 32))),
        "gw2T": asc(np.asarray(inputs["L1_ffn_gw2"], np.float32).T),
        "gb2": asc(np.asarray(inputs["L1_ffn_gb2"], np.float32)[None, :]),
    }
    r = _run(build_l1ffa, [dict(com, xhT=xh4T[c]) for c in range(NCORES)],
             ["oh1", "oh2", "g1", "pc"])
    pcall = asc(np.concatenate([r[c]["pc"] for c in range(NCORES)], 0))
    lora_A = np.asarray(inputs["L1_ffn_lora_A"], np.float32)
    lora_B = np.asarray(inputs["L1_ffn_lora_B"], np.float32)
    lA_cat = asc(np.concatenate([lora_A[e] for e in range(8)], axis=1))
    lB_cat = asc(np.concatenate([lora_B[e] for e in range(8)], axis=0))
    wT = asc(np.asarray(inputs["L1_ffn_weight"], np.float32).T)
    in_b = [{"xhT": xh4T[c], "pcall": pcall, "oh1": r[c]["oh1"],
             "oh2": r[c]["oh2"], "g1": r[c]["g1"], "wT": wT, "lA": lA_cat,
             "lB": lB_cat, "xres": x4c[c]} for c in range(NCORES)]
    r = _run(build_l1ffb, in_b, ["xout"])
    out = np.zeros((B * S, D), np.float32)
    unshard_rows([r[c]["xout"] for c in range(NCORES)], out)
    return out.reshape(B, S, D)


def kernel(**inputs):
    asc = np.ascontiguousarray
    NH, Dh = 32, 128
    x = np.asarray(inputs["L0_attn_x"], np.float32).reshape(B * S, D)
    xc = [shard_rows(x, c) for c in range(NCORES)]
    slopes = _alibi_slopes(NH)
    SDrep = _sd_rep()
    Wv_u = np.asarray(inputs["L0_attn_Wv_u"], np.float32)
    W = {
        "WqdT": asc(np.asarray(inputs["L0_attn_Wq_d"], np.float32).T),
        "Wqk": np.asarray(inputs["L0_attn_W_qk"], np.float32),
        "bqd": asc(np.asarray(inputs["L0_attn_bq_d"], np.float32)[:, None]),
        "bqk": asc(np.asarray(inputs["L0_attn_b_qk"], np.float32)[None, :]),
        "Wkvd": np.asarray(inputs["L0_attn_Wkv_d"], np.float32),
        "bkv": asc(np.asarray(inputs["L0_attn_bkv_d"], np.float32)[None, :]),
        "Wo": np.asarray(inputs["L0_attn_Wo"], np.float32),
        "WvuT": asc(np.concatenate(
            [Wv_u[:, h * 128 : (h + 1) * 128].T for h in range(32)], axis=1)),
        "bvu": asc(np.asarray(inputs["L0_attn_bv_u"], np.float32).reshape(32, 128).T),
        "bo": asc(np.asarray(inputs["L0_attn_bo"], np.float32)[None, :]),
        "WstkT": asc(np.concatenate(
            [np.asarray(inputs["L0_ffn_W_res"], np.float32),
             np.asarray(inputs["L0_ffn_W_pre"], np.float32),
             np.asarray(inputs["L0_ffn_W_post"], np.float32)], 0).T),
        "bases_bc": asc(np.broadcast_to(np.concatenate(
            [np.asarray(inputs["L0_ffn_H_res_base"], np.float32).reshape(-1),
             np.asarray(inputs["L0_ffn_H_pre_base"], np.float32),
             np.asarray(inputs["L0_ffn_H_post_base"], np.float32)]
        ).astype(np.float32)[None, :], (128, 24))),
        "gate": np.asarray(inputs["L0_ffn_gate"], np.float32).reshape(1, 1),
        "Wbr": np.asarray(inputs["L0_ffn_W_branch"], np.float32),
        "wqkv": np.asarray(inputs["L1_attn_w_qkv"], np.float32),
        "SDrep": SDrep,
        "wo": np.asarray(inputs["L1_attn_w_o"], np.float32),
        "gw1T": asc(np.asarray(inputs["L1_ffn_gw1"], np.float32).T),
        "gb1": asc(np.asarray(inputs["L1_ffn_gb1"], np.float32)[None, :]),
        "gamma_bc": asc(np.broadcast_to(
            np.asarray(inputs["L1_ffn_ln_gamma"], np.float32)[None, :], (128, 32))),
        "beta_bc": asc(np.broadcast_to(
            np.asarray(inputs["L1_ffn_ln_beta"], np.float32)[None, :], (128, 32))),
        "gw2T": asc(np.asarray(inputs["L1_ffn_gw2"], np.float32).T),
        "gb2": asc(np.asarray(inputs["L1_ffn_gb2"], np.float32)[None, :]),
        "wT": asc(np.asarray(inputs["L1_ffn_weight"], np.float32).T),
        "lA": asc(np.concatenate(
            [np.asarray(inputs["L1_ffn_lora_A"], np.float32)[e] for e in range(8)],
            axis=1)),
        "lB": asc(np.concatenate(
            [np.asarray(inputs["L1_ffn_lora_B"], np.float32)[e] for e in range(8)],
            axis=0)),
    }
    scs = []
    for c in range(NCORES):
        sc = np.zeros((128, 16), np.float32)
        for u in range(8):
            g = u % 4
            sl = slopes[c * 4 + g]
            sc[:, 2 * u] = 1.0 / (sl * np.sqrt(Dh))
            sc[:, 2 * u + 1] = sl
        scs.append(sc)
    if FUSED:
        in_maps = [dict(W, x=xc[c], scales_bc=scs[c]) for c in range(NCORES)]
        nc = bass.Bass("TRN2", target_bir_lowering=False)
        build_fused(nc)
        nc.finalize()
        res = run_bass_kernel_spmd(nc, in_maps, core_ids=list(range(NCORES)))
        out = np.zeros((B * S, D), np.float32)
        unshard_rows([res.results[c]["xout"] for c in range(NCORES)], out)
        return out.reshape(B, S, D)

    r1 = _run(build_seg1, [dict(W, x=xc[c]) for c in range(NCORES)],
              ["qT", "ckv"])
    ckv_g = asc(np.concatenate([r1[c]["ckv"] for c in range(NCORES)], 0))
    r2 = _run(build_seg2,
              [dict(W, x=xc[c], qT=r1[c]["qT"], ckv_g=ckv_g)
               for c in range(NCORES)],
              ["qkvT", "x2o"])
    qkvTs = [r2[c]["qkvT"] for c in range(NCORES)]
    in3 = []
    for c in range(NCORES):
        qA2A = asc(np.concatenate(
            [qkvTs[r][c * 512 : (c + 1) * 512, :] for r in range(NCORES)], 0))
        kA2A = asc(np.concatenate(
            [qkvTs[r][4096 + c * 128 : 4096 + (c + 1) * 128, :] for r in range(NCORES)], 0))
        vA2A = asc(np.concatenate(
            [qkvTs[r][5120 + c * 128 : 5120 + (c + 1) * 128, :] for r in range(NCORES)], 0))
        in3.append(dict(W, qA2A=qA2A, kA2A=kA2A, vA2A=vA2A, scales_bc=scs[c]))
    r3 = _run(build_seg3, in3, ["outn"])
    in4 = []
    for c in range(NCORES):
        ot = asc(np.concatenate(
            [r3[r]["outn"][c * 512 : (c + 1) * 512, :] for r in range(NCORES)], 0))
        in4.append(dict(W, ot=ot, x2=r2[c]["x2o"]))
    r4 = _run(build_seg4, in4, ["x4o", "xh4T", "oh1", "oh2", "g1", "pc"])
    pcall = asc(np.concatenate([r4[c]["pc"] for c in range(NCORES)], 0))
    in5 = [dict(W, xh4T=r4[c]["xh4T"], pcall=pcall, oh1=r4[c]["oh1"],
                oh2=r4[c]["oh2"], g1=r4[c]["g1"], x4=r4[c]["x4o"])
           for c in range(NCORES)]
    r5 = _run(build_seg5, in5, ["xout"])
    out = np.zeros((B * S, D), np.float32)
    unshard_rows([r5[c]["xout"] for c in range(NCORES)], out)
    return out.reshape(B, S, D)


# version 14
# speedup vs baseline: 1.1625x; 1.0022x over previous
import sys
import os
sys.path.insert(0, "/opt/trn_rl_repo")
os.environ.setdefault("TRN_ALLOW_ICI_ALL_TO_ALL", "1")
import contextlib
import math
import numpy as np
import ml_dtypes
import concourse.bass as bass
import concourse.mybir as mybir
from concourse.bass_utils import run_bass_kernel_spmd

F32 = mybir.dt.float32
BF16 = mybir.dt.bfloat16
AF = mybir.ActivationFunctionType
ALU = mybir.AluOpType
AX = mybir.AxisListType
BF = ml_dtypes.bfloat16

D = 4096
S = 4096
B = 2
NTOK = 1024
TT = 8
FC = 32
CH = 512
LN_EPS = 1e-5
RMS_EPS = 1.1920929e-07
NCORES = 8

ENG_ORDER = ("sp", "pe", "dve", "act", "pool")
SECT = {"sp": "sync", "pe": "tensor", "dve": "vector", "act": "scalar", "pool": "gpsimd"}


def core_chunks(c):
    return [(0, c), (1, 7 - c)]


def core_rows_slices(c):
    return [(b * S + j * CH, CH) for (b, j) in core_chunks(c)]


def shard_rows(full, c):
    parts = [full[s : s + n] for (s, n) in core_rows_slices(c)]
    return np.ascontiguousarray(np.concatenate(parts, axis=0))


def unshard_rows(shards, out):
    for c, sh in enumerate(shards):
        o = 0
        for (s, n) in core_rows_slices(c):
            out[s : s + n] = sh[o : o + n]
            o += n
    return out


class Prog:

    def __init__(self, nc):
        self.nc = nc
        self.engs = {"pe": nc.tensor, "dve": nc.vector, "act": nc.scalar,
                     "pool": nc.gpsimd, "sp": nc.sync}
        self.sem = {}
        for s in ["pe", "dve", "act", "pool", "sp", "dsp", "dact", "dpool", "ddve"]:
            self.sem[s] = nc.alloc_semaphore("sem_" + s)
        self.cnt = {k: 0 for k in self.sem}
        self.plan = {e: [] for e in self.engs}
        self.wm = {e: {k: 0 for k in self.sem} for e in self.engs}
        self._cpool = nc.alloc_sbuf_tensor("const_pool", [128, 32], F32)
        self._consts = {}
        self._pool_auto = 0
        self._phase_auto = 0

    def const_col(self, value):
        v = float(value)
        if v not in self._consts:
            idx = len(self._consts)
            assert idx < 32
            tok = self.op("pool", "memset", self._cpool[:, idx : idx + 1], v)
            self._consts[v] = (self._cpool[:, idx : idx + 1], tok)
        return self._consts[v]

    def _waits(self, eng, deps):
        best = {}
        for (s, v) in deps:
            if v > best.get(s, 0):
                best[s] = v
        waits = []
        wm = self.wm[eng]
        for s, v in best.items():
            if v > wm[s]:
                waits.append((s, v))
                wm[s] = v
        return waits

    def op(self, eng, method, *args, deps=(), **kwargs):
        if eng == "pe":
            deps = [d for d in deps if d[0] != "pe"]
        waits = self._waits(eng, deps)
        self.cnt[eng] += 1
        self.plan[eng].append((method, args, kwargs, waits, [(eng, 1)]))
        return (eng, self.cnt[eng])

    def dma(self, eng, out, in_, deps=(), transpose=False, stream=None, **kwargs):
        assert eng in ("sp", "act", "pool")
        waits = self._waits(eng, deps)
        if stream is None:
            stream = f"_a{eng}{self._phase_auto}"
            self._phase_auto += 1
        d = "dma_" + stream
        if d not in self.sem:
            self.sem[d] = self.nc.alloc_semaphore("sem_" + d)
            self.cnt[d] = 0
            for e in self.wm:
                self.wm[e][d] = 0
        self.cnt[d] += 16
        meth = "dma_start_transpose" if transpose else "dma_start"
        self.plan[eng].append((meth, (out, in_), kwargs, waits, [(d, 16)]))
        return (d, self.cnt[d])

    def cc(self, kind, op, groups, ins, outs, deps=()):
        waits = self._waits("pool", deps)
        d = f"cc_{self._pool_auto}"
        self._pool_auto += 1
        self.sem[d] = self.nc.alloc_semaphore("sem_" + d)
        self.cnt[d] = 16
        for e in self.wm:
            self.wm[e][d] = 0
        self.plan["pool"].append((
            "collective_compute", (kind, op),
            {"replica_groups": groups, "ins": ins, "outs": outs},
            waits, [(d, 16)]))
        return (d, 16)

    def barrier(self):
        self._phase_auto = 0
        snapshot = [(s, self.cnt[s]) for s in self.cnt if self.cnt[s] > 0]
        for e in self.engs:
            waits = []
            wm = self.wm[e]
            for (s, v) in snapshot:
                if s != e and v > wm[s]:
                    waits.append((s, v))
                    wm[s] = v
            if waits:
                self.plan[e].append((None, (), {}, waits, []))

    def finish(self, final_deps):
        waits = self._waits("sp", final_deps)
        if waits:
            self.plan["sp"].append((None, (), {}, waits, []))

    def emit(self):
        nc = self.nc
        with nc.Block() as blk:
            for ename in ENG_ORDER:
                ops = self.plan[ename]
                if not ops:
                    continue
                sect = getattr(blk, SECT[ename])

                def make_body(body_ops):
                    def body(e):
                        for (method, args, kwargs, waits, incs) in body_ops:
                            for (s, v) in waits:
                                e.wait_ge(self.sem[s], v)
                            if method is None:
                                continue
                            instr = getattr(e, method)(*args, **kwargs)
                            for (s, amt) in incs:
                                instr.then_inc(self.sem[s], amt)
                    return body

                sect(make_body(ops))



def emit_ln(p, nc, x_ap, out_sb, n_tiles, sb_prefix, eps=LN_EPS, rms=False,
            dma_eng="sp", x_deps=(), asb=None):
    asb = asb or nc.alloc_sbuf_tensor
    """LayerNorm (or RMS-stat) over last dim 4096 of token rows.

    x_ap: DRAM [n_tiles*128, 4096] f32. out_sb: SBUF [128, n_tiles*4096] BF16.
    Returns (apply_tokens per tile, mv_sb, stat tokens).
    If rms=True computes 1/sqrt(mean(x^2)+eps) instead and applies ONLY the
    stats (returns rinv column tokens), still writing (x-m)*r for rms=False.
    """
    xt = asb(sb_prefix + "_x", [128, n_tiles * D], F32)
    st = asb(sb_prefix + "_st", [128, 48], F32)
    mv = asb(sb_prefix + "_mv", [128, 2 * n_tiles], F32)
    rc = asb(sb_prefix + "_rc", [128, 2 * n_tiles], F32)
    toks = []
    t_ld = p.dma(dma_eng, xt[:].rearrange("p (t d) -> p t d", t=n_tiles),
                 x_ap.rearrange("(t p) d -> p t d", p=128), deps=x_deps)
    prev_aggr = None
    for t in range(n_tiles):
        stats = []
        for j in range(8):
            dep = [t_ld]
            if prev_aggr is not None and t >= 1 and j == 0:
                pass
            stats.append(p.op("dve", "bn_stats", st[:, 6 * j : 6 * j + 6],
                              xt[:, t * D + 512 * j : t * D + 512 * (j + 1)],
                              deps=[t_ld] + ([prev_aggr] if prev_aggr else [])))
        aggr = p.op("dve", "bn_aggr", mv[:, 2 * t : 2 * t + 2], st[:], deps=stats)
        prev_aggr = aggr
        if rms:
            t1 = p.op("dve", "tensor_tensor", rc[:, 2 * t : 2 * t + 1],
                      mv[:, 2 * t : 2 * t + 1], mv[:, 2 * t : 2 * t + 1],
                      op=ALU.mult, deps=[aggr])
            t2 = p.op("dve", "tensor_tensor", rc[:, 2 * t : 2 * t + 1],
                      rc[:, 2 * t : 2 * t + 1], mv[:, 2 * t + 1 : 2 * t + 2],
                      op=ALU.add, deps=[aggr, t1])
            eap, etok = p.const_col(eps)
            t3 = p.op("act", "activation", rc[:, 2 * t + 1 : 2 * t + 2],
                      rc[:, 2 * t : 2 * t + 1], AF.Ln, bias=eap, scale=1.0,
                      deps=[t2, etok])
            t4 = p.op("act", "activation", rc[:, 2 * t : 2 * t + 1],
                      rc[:, 2 * t + 1 : 2 * t + 2], AF.Exp, scale=-0.5,
                      deps=[t3])
            toks.append((t4, rc[:, 2 * t : 2 * t + 1]))
        else:
            eap, etok = p.const_col(eps)
            t3 = p.op("act", "activation", rc[:, 2 * t + 1 : 2 * t + 2],
                      mv[:, 2 * t + 1 : 2 * t + 2], AF.Ln, bias=eap, scale=1.0,
                      deps=[aggr, etok])
            t4 = p.op("act", "activation", rc[:, 2 * t : 2 * t + 1],
                      rc[:, 2 * t + 1 : 2 * t + 2], AF.Exp, scale=-0.5,
                      deps=[t3])
            ap = p.op("dve", "tensor_scalar", out_sb[:, t * D : (t + 1) * D],
                      xt[:, t * D : (t + 1) * D],
                      mv[:, 2 * t : 2 * t + 1], rc[:, 2 * t : 2 * t + 1],
                      op0=ALU.subtract, op1=ALU.mult, deps=[aggr, t4])
            toks.append(ap)
    return toks, xt, t_ld


def build_ln_launch(nc):
    p = Prog(nc)
    x = nc.dram_tensor("x_in", [NTOK, D], F32, kind="ExternalInput").ap()
    xh = nc.dram_tensor("xh_out", [NTOK, D], BF16, kind="ExternalOutput").ap()
    xh_sb = nc.alloc_sbuf_tensor("xh_sb", [128, TT * D], BF16)
    toks, _, _ = emit_ln(p, nc, x, xh_sb, TT, "ln")
    st = p.dma("sp", xh.rearrange("(t p) d -> p t d", p=128),
               xh_sb[:].rearrange("p (t d) -> p t d", t=TT), deps=toks)
    p.finish([st])
    p.emit()
    return nc


def emit_ident(p, nc, asb=None, name="ident", dtype=BF16):
    asb = asb or nc.alloc_sbuf_tensor
    ident = asb(name, [128, 128], dtype)
    t0 = p.op("pool", "memset", ident[:], 0.0)
    t1 = p.op("pool", "affine_select", out=ident[:], in_=ident[:],
              compare_op=ALU.not_equal, fill=1.0, base=0,
              pattern=[[-1, 128]], channel_multiplier=1, deps=[t0])
    return ident, t1


def emit_ones(p, nc, asb, name, shape, dtype=BF16):
    t = asb(name, shape, dtype)
    tok = p.op("pool", "memset", t[:], 1.0)
    return t, tok


def emit_l0proj(p, nc, A, pfx, asb, aps_):
    xhT = A["xhT"]
    WqdT = A["WqdT"]
    Wqk = A["Wqk"]
    bqd = A["bqd"]
    bqk = A["bqk"]
    Wkvd = A["Wkvd"]
    bkv = A["bkv"]
    qnat = A["qnat"]
    ckv = A["ckv"]

    xhT_sb = asb(pfx + "xhT_sb", [128, FC * NTOK], BF16)
    wqdT_sb = asb(pfx + "wqdT_sb", [12, D], BF16)
    wqk_sb = asb(pfx + "wqk_sb", [12, 128], BF16)
    bqd_sb = asb(pfx + "bqd_sb", [12, 1], BF16)
    bqk_sb = asb(pfx + "bqk_sb", [1, 128], F32)
    wkvd_sb = asb(pfx + "wkvd_sb", [128, FC * 4], BF16)
    bkv_sb = asb(pfx + "bkv_sb", [1, 4], BF16)
    wqc_sb = asb(pfx + "wqc_sb", [128, FC * 128], BF16)
    bqc_sb = asb(pfx + "bqc_sb", [1, 128], F32)
    bqcb_sb = asb(pfx + "bqcb_sb", [1, 128], BF16)
    qnat_sb = asb(pfx + "qnat_sb", [128, TT * 128], BF16)
    ckv_sb = asb(pfx + "ckv_sb", [128, TT * 4], F32)

    ps0 = aps_(pfx + "ps0", [128, 512], F32)
    ps1 = aps_(pfx + "ps1", [128, 512], F32)
    ps2 = aps_(pfx + "ps2", [128, 512], F32)

    t_xh = p.dma("sp", xhT_sb[:].rearrange("p (c t) -> p c t", c=FC),
                 xhT.rearrange("(c p) t -> p c t", p=128))
    t_wqd = p.dma("pool", wqdT_sb[:], WqdT)
    t_wqk = p.dma("pool", wqk_sb[:], Wqk)
    t_bqd = p.dma("pool", bqd_sb[:], bqd)
    t_bqk = p.dma("sp", bqk_sb[:], bqk)
    t_wkv = p.dma("pool", wkvd_sb[:].rearrange("p (c n) -> p c n", c=FC),
                  Wkvd.rearrange("(c p) n -> p c n", p=128))
    t_bkv = p.dma("pool", bkv_sb[:], bkv)
    ones_row, t_ones = emit_ones(p, nc, asb, pfx + "ones_row", [1, 128])

    wqc_toks = []
    ps_tok = [None, None]
    for c in range(FC):
        ps = (ps0, ps1)[c % 2]
        deps = [t_wqd, t_wqk]
        if ps_tok[c % 2] is not None:
            deps.append(ps_tok[c % 2])
        mm = p.op("pe", "matmul", ps[0:128, 0:128], wqdT_sb[:, c * 128 : (c + 1) * 128],
                  wqk_sb[:], start=True, stop=True, deps=deps)
        ev = p.op("dve", "tensor_copy", wqc_sb[:, c * 128 : (c + 1) * 128],
                  ps[0:128, 0:128], deps=[mm])
        ps_tok[c % 2] = ev
        wqc_toks.append(ev)
    mm_b = p.op("pe", "matmul", ps2[0:1, 0:128], bqd_sb[:], wqk_sb[:],
                start=True, stop=True, deps=[t_bqd, t_wqk])
    ev_b = p.op("dve", "tensor_copy", bqc_sb[:], ps2[0:1, 0:128], deps=[mm_b, t_bqk])
    ad_b = p.op("dve", "tensor_tensor", bqc_sb[:], bqc_sb[:], bqk_sb[:],
                op=ALU.add, deps=[ev_b])
    cv_b = p.op("dve", "tensor_copy", bqcb_sb[:], bqc_sb[:], deps=[ad_b])

    q_toks, k_toks = [], []
    for tt in range(TT):
        psq = (ps0, ps1)[tt % 2]
        deps = [t_xh, wqc_toks[-1]]
        if tt >= 2:
            deps.append(q_toks[tt - 2][0])
        last = None
        for c in range(FC):
            dd = list(deps) if last is None else [last]
            last = p.op("pe", "matmul", psq[0:128, 0:128],
                        xhT_sb[:, c * NTOK + tt * 128 : c * NTOK + (tt + 1) * 128],
                        wqc_sb[:, c * 128 : (c + 1) * 128],
                        start=(c == 0), stop=False, deps=dd)
        last = p.op("pe", "matmul", psq[0:128, 0:128], ones_row[:],
                    bqcb_sb[:], start=False, stop=True, deps=[last, t_ones, cv_b])
        ev = p.op("dve", "tensor_copy", qnat_sb[:, tt * 128 : (tt + 1) * 128],
                  psq[0:128, 0:128], deps=[last])
        q_toks.append((ev, last))

        deps2 = [t_xh, t_wkv, ev_b if tt == 0 else k_toks[tt - 1][0]]
        last2 = None
        for c in range(FC):
            dd = list(deps2) if last2 is None else [last2]
            last2 = p.op("pe", "matmul", ps2[0:128, 0:4],
                         xhT_sb[:, c * NTOK + tt * 128 : c * NTOK + (tt + 1) * 128],
                         wkvd_sb[:, c * 4 : (c + 1) * 4],
                         start=(c == 0), stop=False, deps=dd)
        last2 = p.op("pe", "matmul", ps2[0:128, 0:4], ones_row[:], bkv_sb[:],
                     start=False, stop=True, deps=[last2, t_ones, t_bkv])
        ev2 = p.op("dve", "tensor_copy", ckv_sb[:, tt * 4 : (tt + 1) * 4],
                   ps2[0:128, 0:4], deps=[last2])
        k_toks.append((ev2, last2))

    s1 = p.dma("sp", qnat.rearrange("(t p) n -> p t n", p=128),
               qnat_sb[:].rearrange("p (t n) -> p t n", t=TT),
               deps=[t[0] for t in q_toks])
    s2 = p.dma("sp", ckv.rearrange("(t p) n -> p t n", p=128),
               ckv_sb[:].rearrange("p (t n) -> p t n", t=TT),
               deps=[t[0] for t in k_toks])
    return [s1, s2]


def build_l0proj(nc):
    p = Prog(nc)
    A = {}
    A["xhT"] = nc.dram_tensor("xhT", [D, NTOK], BF16, kind="ExternalInput").ap()
    A["WqdT"] = nc.dram_tensor("WqdT", [12, D], F32, kind="ExternalInput").ap()
    A["Wqk"] = nc.dram_tensor("Wqk", [12, 128], F32, kind="ExternalInput").ap()
    A["bqd"] = nc.dram_tensor("bqd", [12, 1], F32, kind="ExternalInput").ap()
    A["bqk"] = nc.dram_tensor("bqk", [1, 128], F32, kind="ExternalInput").ap()
    A["Wkvd"] = nc.dram_tensor("Wkvd", [D, 4], F32, kind="ExternalInput").ap()
    A["bkv"] = nc.dram_tensor("bkv", [1, 4], F32, kind="ExternalInput").ap()
    A["qnat"] = nc.dram_tensor("qnat", [NTOK, 128], BF16, kind="ExternalOutput").ap()
    A["ckv"] = nc.dram_tensor("ckv", [NTOK, 4], F32, kind="ExternalOutput").ap()
    toks = emit_l0proj(p, nc, A, "", nc.alloc_sbuf_tensor, nc.alloc_psum_tensor)
    p.finish(toks)
    p.emit()
    return nc


L0_SHIFT = 20.0


def emit_l0core(p, nc, A, pfx, asb, aps_):
    qT = A["qT"]
    ckvT32 = A.get("ckvT32")
    ckv5 = A.get("ckv5")
    Wo = A["Wo"]
    WvuT = A["WvuT"]
    bvu = A["bvu"]
    bo = A["bo"]
    xres = A["xres"]
    xout = A["xout"]
    wcomb_h = A["wcomb_scratch"]

    nkb2 = 64 if "ckv_g" in A else 32
    ckvT_sb = asb(pfx + "ckvT_sb", [32, nkb2 * 128], BF16)
    ckv5_sb = asb(pfx + "ckv5_sb", [128, nkb2 * 5], BF16)
    wvuT_sb = asb(pfx + "wvuT_sb", [128, 128], BF16)
    bvu_sb = asb(pfx + "bvu_sb", [128, FC], BF16)
    bo_sb = asb(pfx + "bo_sb", [1, D], F32)
    wo_sb = [asb(pfx + f"wo_sb{i}", [128, 32 * 512], BF16) for i in range(2)]
    wcs_sb = asb(pfx + "wcs_sb", [4, 512], BF16)
    bcomb_sb = asb(pfx + "bcomb_sb", [1, D], F32)
    bcombb_sb = asb(pfx + "bcombb_sb", [1, D], BF16)
    wcomb_sb = asb(pfx + "wcomb_sb", [128, D], BF16)
    q4h_sb = [asb(pfx + f"q4h_sb{i}", [32, NTOK], BF16) for i in range(2)]
    attnT_sb = asb(pfx + "attnT_sb", [128, 2048], BF16)
    o5_sb = asb(pfx + "o5_sb", [5, 512], BF16)
    ot_sb = asb(pfx + "ot_sb", [128, 20], F32)
    rcol_sb = asb(pfx + "rcol_sb", [128, 2], F32)
    oall_sb = asb(pfx + "oall_sb", [128, TT * 128], BF16)
    otT_sb = asb(pfx + "otT_sb", [128, TT * 128], BF16)
    xr_sb = [asb(pfx + f"xr_sb{i}", [128, 512], F32) for i in range(2)]
    xo_sb = [asb(pfx + f"xo_sb{i}", [128, 512], F32) for i in range(2)]

    psA = aps_(pfx + "psA", [128, 2048], F32)
    psO = aps_(pfx + "psO", [128, 512], F32)
    psOut = [aps_(pfx + f"psOut{i}", [128, 512], F32) for i in range(2)]
    psW = psOut[0]
    psB = psOut[1]
    ptps = aps_(pfx + "ptps", [128, 1024], BF16)

    ident, t_id = emit_ident(p, nc, asb, pfx + "ident")
    ones_row, t_ones = emit_ones(p, nc, asb, pfx + "ones_row", [1, 128])
    negB, t_negB = p.const_col(-L0_SHIFT)

    ckv_ev = None
    if "ckv_g" in A:
        ckv_g = A["ckv_g"]
        ckv_q = asb(pfx + "ckv_q", [128, 64 * 4], BF16)
        m5 = p.op("pool", "memset", ckv5_sb[:], 1.0)
        t5toks, tqtoks = [], []
        for b2 in range(2):
            for j in range(8):
                srcr = j if b2 == 0 else 7 - j
                r0 = srcr * 1024 + b2 * 512
                k0 = b2 * 32 + j * 4
                t5toks.append(p.dma(
                    "pool",
                    ckv5_sb[:].rearrange("p (k c) -> p k c", k=64)[:, k0 : k0 + 4, 0:4],
                    ckv_g[r0 : r0 + 512, :].rearrange("(k p) c -> p k c", p=128),
                    deps=[m5], stream="ckv5l"))
                tqtoks.append(p.dma(
                    "pool",
                    ckv_q[:].rearrange("p (k c) -> p k c", k=64)[:, k0 : k0 + 4, :],
                    ckv_g[r0 : r0 + 512, :].rearrange("(k p) c -> p k c", p=128),
                    stream="ckvql"))
        t_ckv5 = t5toks[-1]
        for tok in t5toks:
            pass
        mT = p.op("pool", "memset", ckvT_sb[:], 0.0)
        for kb in range(64):
            dtr = tqtoks + [t_id] + ([ckv_ev] if ckv_ev else [])
            tr = p.op("pe", "transpose", ptps[0:4, 0:128],
                      ckv_q[:, kb * 4 : (kb + 1) * 4], ident[:], deps=dtr)
            ckv_ev = p.op("dve", "tensor_copy",
                          ckvT_sb[0:4, kb * 128 : (kb + 1) * 128],
                          ptps[0:4, 0:128], deps=[tr, mT])
        t_ckvT = ckv_ev
        t_ckv5 = t5toks[-1]
    else:
        t_ckvT = p.dma("pool", ckvT_sb[:], ckvT32)
        t_ckv5 = p.dma("pool", ckv5_sb[:].rearrange("p (k c) -> p k c", k=32),
                       ckv5.rearrange("(k p) c -> p k c", p=128))
    t_wvuT = p.dma("pool", wvuT_sb[:], WvuT)
    t_bvu = p.dma("pool", bvu_sb[:], bvu)
    t_bo = p.dma("sp", bo_sb[:], bo)

    wc_state = {"dma": [], "psB_free": None, "wc_ev": None, "wcs_free": None,
                "wo_tok": [None, None], "bdone": None}

    def emit_wcomb_cb(cb):
        wi = cb % 2
        deps_w = [] if wc_state["wo_tok"][wi] is None else [wc_state["wo_tok"][wi]]
        t_wo = p.dma("pool", wo_sb[wi][:].rearrange("p (c n) -> p c n", c=32),
                     Wo[:, cb * 512 : (cb + 1) * 512].rearrange(
                         "(c p) n -> p c n", p=128),
                     deps=deps_w, stream=f"wo{wi}")
        mmb_last = None
        for h in range(32):
            dw = [t_wo, t_wvuT]
            if wc_state["wc_ev"] is not None:
                dw.append(wc_state["wc_ev"])
            mmw = p.op("pe", "matmul", psW[0:4, 0:512],
                       wvuT_sb[:, 4 * h : 4 * h + 4],
                       wo_sb[wi][:, h * 512 : (h + 1) * 512],
                       start=True, stop=True, deps=dw)
            de = [mmw]
            if wc_state["wcs_free"] is not None:
                de.append(wc_state["wcs_free"])
            wc_ev = p.op("dve", "tensor_copy", wcs_sb[:], psW[0:4, 0:512], deps=de)
            wc_state["wc_ev"] = wc_ev
            t_st = p.dma("sp", wcomb_h[4 * h : 4 * h + 4, cb * 512 : (cb + 1) * 512],
                         wcs_sb[:], deps=[wc_ev], stream="wcs")
            wc_state["wcs_free"] = t_st
            wc_state["dma"].append(t_st)
            db = [t_bvu]
            if mmb_last is not None:
                db.append(mmb_last)
            if h == 0 and wc_state["psB_free"] is not None:
                db.append(wc_state["psB_free"])
            mmb_last = p.op("pe", "matmul", psB[0:1, 0:512],
                            bvu_sb[:, h : h + 1],
                            wo_sb[wi][:, h * 512 : (h + 1) * 512],
                            start=(h == 0), stop=(h == 31), deps=db)
        wc_state["wo_tok"][wi] = mmb_last
        evb = p.op("dve", "tensor_copy", bcomb_sb[:, cb * 512 : (cb + 1) * 512],
                   psB[0:1, 0:512], deps=[mmb_last, t_bo])
        wc_state["psB_free"] = evb
        adb = p.op("dve", "tensor_tensor", bcomb_sb[:, cb * 512 : (cb + 1) * 512],
                   bcomb_sb[:, cb * 512 : (cb + 1) * 512],
                   bo_sb[:, cb * 512 : (cb + 1) * 512], op=ALU.add, deps=[evb])
        wc_state["bdone"] = p.op(
            "dve", "tensor_copy", bcombb_sb[:, cb * 512 : (cb + 1) * 512],
            bcomb_sb[:, cb * 512 : (cb + 1) * 512], deps=[adb])

    q4h_init = [p.op("pool", "memset", q4h_sb[i][:], 0.0) for i in range(2)]
    q4h_user = [q4h_init[0], q4h_init[1]]
    att_user = [None] * 4
    psA_user = [None] * 4
    psO_free = None
    o5_free = None
    ev_ot = ckv_ev
    nrm = None
    oall_toks = []
    for h in range(32):
        bi = h % 2
        t_q = p.dma("sp", q4h_sb[bi][0:4, :], qT[4 * h : 4 * h + 4, :],
                    deps=[q4h_user[bi]], stream=f"q4h{bi}")
        last_mm_s = None
        for qc in range(2):
            mm_av_last = None
            mm_s_l = {}
            ex_l = {}
            for step in range(35):
                if step < 32:
                    kb = step
                    si = kb % 4
                    da = [t_q, t_ckvT]
                    if psA_user[si] is not None:
                        da.append(psA_user[si])
                    kb2 = (qc * 32 + kb) if nkb2 == 64 else kb
                    mm_s_l[kb] = p.op("pe", "matmul",
                                      psA[0:128, si * 512 : (si + 1) * 512],
                                      ckvT_sb[0:32, kb2 * 128 : (kb2 + 1) * 128],
                                      q4h_sb[bi][:, qc * 512 : (qc + 1) * 512],
                                      start=True, stop=True, deps=da)
                    last_mm_s = mm_s_l[kb]
                if step >= 2 and (step - 2) % 2 == 0 and step - 2 < 32:
                    kb = step - 2
                    si = kb % 4
                    de = [mm_s_l[kb], mm_s_l[kb + 1], t_negB]
                    if att_user[si] is not None:
                        de.append(att_user[si])
                    if att_user[si + 1] is not None:
                        de.append(att_user[si + 1])
                    ex = p.op("act", "activation",
                              attnT_sb[:, si * 512 : (si + 2) * 512],
                              psA[0:128, si * 512 : (si + 2) * 512],
                              AF.Exp, scale=0.5, bias=negB, deps=de)
                    ex_l[kb] = ex
                    ex_l[kb + 1] = ex
                    psA_user[si] = ex
                    psA_user[si + 1] = ex
                if step >= 3:
                    kb = step - 3
                    si = kb % 4
                    kb2 = (qc * 32 + kb) if nkb2 == 64 else kb
                    dv = [ex_l[kb], t_ckv5]
                    if mm_av_last is not None:
                        dv.append(mm_av_last)
                    elif psO_free is not None:
                        dv.append(psO_free)
                    mm_av_last = p.op("pe", "matmul", psO[0:5, 0:512],
                                      ckv5_sb[:, kb2 * 5 : kb2 * 5 + 5],
                                      attnT_sb[:, si * 512 : (si + 1) * 512],
                                      start=(kb == 0), stop=(kb == 31), deps=dv)
                    att_user[si] = mm_av_last
                    del ex_l[kb]
            do5 = [mm_av_last]
            if o5_free is not None:
                do5.append(o5_free)
            ev_o5 = p.op("dve", "tensor_copy", o5_sb[:], psO[0:5, 0:512], deps=do5)
            psO_free = ev_o5
            tr_toks = []
            for i in range(4):
                tr = p.op("pe", "transpose", ptps[0:128, 8 * i : 8 * i + 5],
                          o5_sb[0:5, 128 * i : 128 * (i + 1)], ident[0:5, 0:5],
                          deps=[ev_o5, t_id] + ([ev_ot] if ev_ot is not None else []))
                tr_toks.append(tr)
            o5_free = tr_toks[-1]
            for i in range(4):
                ev_ot = p.op("dve", "tensor_copy", ot_sb[:, 5 * i : 5 * i + 5],
                             ptps[0:128, 8 * i : 8 * i + 5],
                             deps=tr_toks if i == 0 else [tr_toks[-1], ev_ot])
            for i in range(4):
                drc = [ev_ot]
                if nrm is not None:
                    drc.append(nrm)
                rc0 = p.op("act", "activation", rcol_sb[:, 1:2],
                           ot_sb[:, 5 * i + 4 : 5 * i + 5], AF.Ln, deps=drc)
                rc = p.op("act", "activation", rcol_sb[:, 0:1],
                          rcol_sb[:, 1:2], AF.Exp, scale=-1.0, deps=[rc0])
                qtile = qc * 4 + i
                nrm = p.op("dve", "tensor_scalar",
                           oall_sb[:, qtile * 128 + 4 * h : qtile * 128 + 4 * h + 4],
                           ot_sb[:, 5 * i : 5 * i + 4], rcol_sb[:, 0:1], None,
                           op0=ALU.mult, deps=[rc])
                oall_toks.append(nrm)
        q4h_user[bi] = last_mm_s
        if h % 4 == 3:
            emit_wcomb_cb(h // 4)

    t_wcomb_rd = p.dma("sp", wcomb_sb[:], wcomb_h[:, :], deps=wc_state["dma"])
    t_bcomb_done = wc_state["bdone"]

    ot_toks = []
    ev_t = ev_ot
    for tt in range(TT):
        tr = p.op("pe", "transpose", ptps[0:128, 0:128],
                  oall_sb[:, tt * 128 : (tt + 1) * 128], ident[:],
                  deps=[ev_t, oall_toks[-1]])
        ev_t = p.op("dve", "tensor_copy", otT_sb[:, tt * 128 : (tt + 1) * 128],
                    ptps[0:128, 0:128], deps=[tr])
        ot_toks.append(ev_t)

    xr_tok = [None, None]
    xo_tok = [None, None]
    psOut_tok = [None, None]
    final_toks = []
    for tt in range(TT):
        for cb in range(8):
            i = (tt * 8 + cb) % 2
            dx = [] if xr_tok[i] is None else [xr_tok[i]]
            t_xr = p.dma("act", xr_sb[i][:],
                         xres[tt * 128 : (tt + 1) * 128, cb * 512 : (cb + 1) * 512],
                         deps=dx, stream=f"xr{i}")
            dm = [ot_toks[tt], t_wcomb_rd]
            if psOut_tok[i] is not None:
                dm.append(psOut_tok[i])
            mm1 = p.op("pe", "matmul", psOut[i][0:128, 0:512],
                       otT_sb[:, tt * 128 : (tt + 1) * 128],
                       wcomb_sb[:, cb * 512 : (cb + 1) * 512],
                       start=True, stop=False, deps=dm)
            mm2 = p.op("pe", "matmul", psOut[i][0:128, 0:512],
                       ones_row[:], bcombb_sb[:, cb * 512 : (cb + 1) * 512],
                       start=False, stop=True, deps=[mm1, t_ones, t_bcomb_done])
            dadd = [mm2, t_xr]
            if xo_tok[i] is not None:
                dadd.append(xo_tok[i])
            add = p.op("dve", "tensor_tensor", xo_sb[i][:], psOut[i][0:128, 0:512],
                       xr_sb[i][:], op=ALU.add, deps=dadd)
            psOut_tok[i] = add
            xr_tok[i] = add
            st = p.dma("sp", xout[tt * 128 : (tt + 1) * 128, cb * 512 : (cb + 1) * 512],
                       xo_sb[i][:], deps=[add], stream=f"xo{i}")
            xo_tok[i] = st
            final_toks.append(st)
    return final_toks


def build_l0core(nc):
    p = Prog(nc)
    A = {}
    A["qT"] = nc.dram_tensor("qT", [128, NTOK], BF16, kind="ExternalInput").ap()
    A["ckvT32"] = nc.dram_tensor("ckvT32", [32, S], F32, kind="ExternalInput").ap()
    A["ckv5"] = nc.dram_tensor("ckv5", [S, 5], F32, kind="ExternalInput").ap()
    A["Wo"] = nc.dram_tensor("Wo", [D, D], F32, kind="ExternalInput").ap()
    A["WvuT"] = nc.dram_tensor("WvuT", [128, 128], F32, kind="ExternalInput").ap()
    A["bvu"] = nc.dram_tensor("bvu", [128, FC], F32, kind="ExternalInput").ap()
    A["bo"] = nc.dram_tensor("bo", [1, D], F32, kind="ExternalInput").ap()
    A["xres"] = nc.dram_tensor("xres", [NTOK, D], F32, kind="ExternalInput").ap()
    A["xout"] = nc.dram_tensor("xout", [NTOK, D], F32, kind="ExternalOutput").ap()
    A["wcomb_scratch"] = nc.dram_tensor("wcomb_scratch", [128, D], BF16, kind="Internal").ap()
    toks = emit_l0core(p, nc, A, "", nc.alloc_sbuf_tensor, nc.alloc_psum_tensor)
    p.finish(toks)
    p.emit()
    return nc


SINK_ITERS = 50
SINK_TAU = 0.1


def emit_l0ffn(p, nc, A, pfx, asb, aps_):
    xh2 = A["xh2"]
    xh2T = A["xh2T"]
    WstkT = A["WstkT"]
    bases = A["bases_bc"]
    gate = A["gate"]
    Wbr = A["Wbr"]
    xres = A["xres"]
    xout = A["xout"]

    xh_sb = asb(pfx + "xh_sb", [128, TT * D], BF16)
    xhT_sb = asb(pfx + "xhT_sb", [128, FC * NTOK], BF16)
    wstk_sb = asb(pfx + "wstk_sb", [128, FC * 24], BF16)
    bases_sb = asb(pfx + "bases_sb", [128, 24], F32)
    gate_sb = asb(pfx + "gate_sb", [1, 4], F32)
    st_sb = asb(pfx + "st_sb", [128, 48], F32)
    mv_sb = asb(pfx + "mv_sb", [128, 2], F32)
    rin_sb = asb(pfx + "rin_sb", [128, 2 * TT], F32)
    lg_sb = asb(pfx + "lg_sb", [128, TT * 24], F32)
    sm_sb = asb(pfx + "sm_sb", [128, 8], F32)
    hpre_sb = asb(pfx + "hpre_sb", [128, TT * 4], F32)
    hpost_sb = asb(pfx + "hpost_sb", [128, TT * 4], F32)
    Z_sb = asb(pfx + "Z_sb", [128, TT * 16], F32)
    Zs_sb = asb(pfx + "Zs_sb", [128, TT * 16], F32)
    T_sb = asb(pfx + "T_sb", [128, TT * 16], F32)
    E_sb = asb(pfx + "E_sb", [128, TT * 16], F32)
    S_sb = asb(pfx + "S_sb", [128, TT * 4], F32)
    U_sb = asb(pfx + "U_sb", [128, TT * 4], F32)
    V_sb = asb(pfx + "V_sb", [128, TT * 4], F32)
    H_sb = asb(pfx + "H_sb", [128, TT * 16], F32)
    gbc_sb = asb(pfx + "gbc_sb", [128, 2], F32)
    onesf_sb = asb(pfx + "onesf_sb", [1, 128], F32)
    xpre_sb = [asb(pfx + f"xpre_sb{i}", [128, 1024], BF16) for i in range(2)]
    xpreT_sb = asb(pfx + "xpreT_sb", [128, 8 * NTOK], BF16)
    wbr_sb = asb(pfx + "wbr_sb", [128, 8 * 1024], BF16)
    y2_sb = [asb(pfx + f"y2_sb{i}", [128, 1024], BF16) for i in range(2)]
    acc_sb = [asb(pfx + f"acc_sb{i}", [128, 1024], F32) for i in range(2)]
    xr_sb = [asb(pfx + f"xr_sb{i}", [128, 1024], F32) for i in range(2)]
    xo_sb = [asb(pfx + f"xo_sb{i}", [128, 1024], F32) for i in range(2)]

    ps0 = aps_(pfx + "ps0", [128, 512], F32)
    ps1 = aps_(pfx + "ps1", [128, 512], F32)
    ptps = aps_(pfx + "ptps", [128, 1024], BF16)

    ident, t_id = emit_ident(p, nc, asb, pfx + "ident")

    t_xh = p.dma("sp", xh_sb[:].rearrange("p (t d) -> p t d", t=TT),
                 xh2.rearrange("(t p) d -> p t d", p=128))
    t_xhT = p.dma("sp", xhT_sb[:].rearrange("p (c t) -> p c t", c=FC),
                  xh2T.rearrange("(c p) t -> p c t", p=128))
    t_ws = p.dma("pool", wstk_sb[:].rearrange("p (c n) -> p c n", c=FC),
                 WstkT.rearrange("(c p) n -> p c n", p=128))
    t_bs = p.dma("sp", bases_sb[:], bases)
    t_g = p.dma("sp", gate_sb[0:1, 0:1], gate)
    t_of = p.op("pool", "memset", onesf_sb[:], 1.0)
    t_wbr = p.dma("pool", wbr_sb[:].rearrange("p (c n) -> p c n", c=8),
                  Wbr.rearrange("(c p) n -> p c n", p=128), stream="wbr")

    rin_toks = []
    st_free = None
    mv_free = None
    for t in range(TT):
        stats = []
        for j in range(8):
            dd = [t_xh]
            if st_free is not None:
                dd.append(st_free)
            stats.append(p.op("dve", "bn_stats", st_sb[:, 6 * j : 6 * j + 6],
                              xh_sb[:, t * D + 512 * j : t * D + 512 * (j + 1)],
                              deps=dd))
        dd = stats + ([mv_free] if mv_free else [])
        aggr = p.op("dve", "bn_aggr", mv_sb[:], st_sb[:], deps=dd)
        st_free = aggr
        t1 = p.op("dve", "tensor_tensor", rin_sb[:, 2 * t : 2 * t + 1],
                  mv_sb[:, 0:1], mv_sb[:, 0:1], op=ALU.mult, deps=[aggr])
        t2 = p.op("dve", "tensor_tensor", rin_sb[:, 2 * t : 2 * t + 1],
                  rin_sb[:, 2 * t : 2 * t + 1], mv_sb[:, 1:2], op=ALU.add,
                  deps=[t1])
        mv_free = t2
        eap, etok = p.const_col(RMS_EPS)
        t3 = p.op("act", "activation", rin_sb[:, 2 * t + 1 : 2 * t + 2],
                  rin_sb[:, 2 * t : 2 * t + 1], AF.Ln, bias=eap, scale=1.0,
                  deps=[t2, etok])
        t4 = p.op("act", "activation", rin_sb[:, 2 * t : 2 * t + 1],
                  rin_sb[:, 2 * t + 1 : 2 * t + 2], AF.Exp, scale=-0.5,
                  deps=[t3])
        rin_toks.append(t4)

    lg_toks = []
    ps_free = [None, None]
    for t in range(TT):
        ps = (ps0, ps1)[t % 2]
        last = None
        for c in range(FC):
            dd = [t_xhT, t_ws] if last is None else [last]
            if last is None and ps_free[t % 2] is not None:
                dd.append(ps_free[t % 2])
            last = p.op("pe", "matmul", ps[0:128, 0:24],
                        xhT_sb[:, c * NTOK + t * 128 : c * NTOK + (t + 1) * 128],
                        wstk_sb[:, c * 24 : (c + 1) * 24],
                        start=(c == 0), stop=(c == FC - 1), deps=dd)
        ts1 = p.op("dve", "tensor_scalar", lg_sb[:, t * 24 : (t + 1) * 24],
                   ps[0:128, 0:24], rin_sb[:, 2 * t : 2 * t + 1], None,
                   op0=ALU.mult, deps=[last, rin_toks[t]])
        ps_free[t % 2] = ts1
        ad = p.op("dve", "tensor_tensor", lg_sb[:, t * 24 : (t + 1) * 24],
                  lg_sb[:, t * 24 : (t + 1) * 24], bases_sb[:], op=ALU.add,
                  deps=[ts1, t_bs])
        lg_toks.append(ad)

    hpre_toks, hpost_toks = [], []
    sm_free = None
    for t in range(TT):
        for (off, dst, toks) in ((16, hpre_sb, hpre_toks), (20, hpost_sb, hpost_toks)):
            sl = lg_sb[:, t * 24 + off : t * 24 + off + 4]
            dd = [lg_toks[t]]
            if sm_free is not None:
                dd.append(sm_free)
            m1 = p.op("dve", "tensor_reduce", sm_sb[:, 0:1], sl, axis=AX.X,
                      op=ALU.max, negate=True, deps=dd)
            e = p.op("act", "activation", sm_sb[:, 4:8], sl, AF.Exp,
                     scale=1.0, bias=sm_sb[:, 0:1], deps=[m1])
            s = p.op("dve", "reduce_sum", sm_sb[:, 1:2], sm_sb[:, 4:8],
                     axis=AX.X, deps=[e])
            r0 = p.op("act", "activation", sm_sb[:, 3:4], sm_sb[:, 1:2],
                      AF.Ln, deps=[s])
            r = p.op("act", "activation", sm_sb[:, 2:3], sm_sb[:, 3:4],
                     AF.Exp, scale=-1.0, deps=[r0])
            n = p.op("dve", "tensor_scalar", dst[:, t * 4 : (t + 1) * 4],
                     sm_sb[:, 4:8], sm_sb[:, 2:3], None, op0=ALU.mult, deps=[r])
            sm_free = n
            toks.append(n)

    xpre_toks = []
    xpre_free = [None, None]
    xpreT_toks = []
    pt_free = t_id
    for t in range(TT):
        xi = t % 2
        xp = xpre_sb[xi]
        dd = [t_xh, hpre_toks[t]]
        if xpre_free[xi] is not None:
            dd.append(xpre_free[xi])
        a = p.op("dve", "tensor_scalar", xp[:],
                 xh_sb[:, t * D : t * D + 1024], hpre_sb[:, t * 4 : t * 4 + 1],
                 None, op0=ALU.mult, deps=dd)
        for n in range(1, 4):
            a = p.op("dve", "scalar_tensor_tensor", xp[:],
                     xh_sb[:, t * D + n * 1024 : t * D + (n + 1) * 1024],
                     hpre_sb[:, t * 4 + n : t * 4 + n + 1], xp[:],
                     op0=ALU.mult, op1=ALU.add, deps=[a])
        xpre_toks.append(a)
        trs = []
        for cc in range(8):
            tr = p.op("pe", "transpose", ptps[0:128, cc * 128 : (cc + 1) * 128],
                      xp[:, cc * 128 : (cc + 1) * 128],
                      ident[:], deps=[a, t_id, pt_free])
            trs.append(tr)
        xpre_free[xi] = trs[-1]
        ev = p.op("dve", "tensor_copy",
                  xpreT_sb[:].rearrange("p (c t2) -> p c t2", c=8)[:, :, t * 128 : (t + 1) * 128],
                  ptps[0:128, 0:1024].rearrange("p (c x) -> p c x", c=8),
                  deps=trs)
        pt_free = ev
        xpreT_toks.append(ev)
    Z3 = Z_sb[:].rearrange("p (t i j) -> p t i j", t=TT, i=4)
    Zs3 = Zs_sb[:].rearrange("p (t j i) -> p t j i", t=TT, j=4)
    T3 = T_sb[:].rearrange("p (t a b) -> p t a b", t=TT, a=4)
    E3 = E_sb[:].rearrange("p (t a b) -> p t a b", t=TT, a=4)
    S3 = S_sb[:].rearrange("p (t a) -> p t a", t=TT)
    U3 = U_sb[:].rearrange("p (t a) -> p t a", t=TT)
    V3 = V_sb[:].rearrange("p (t a) -> p t a", t=TT)
    tz = p.op("dve", "tensor_scalar", Z_sb[:].rearrange("p (t x) -> p t x", t=TT),
              lg_sb[:].rearrange("p (t x) -> p t x", t=TT)[:, :, 0:16],
              1.0 / SINK_TAU, 20.0, op0=ALU.mult, op1=ALU.subtract,
              deps=[lg_toks[-1]])
    tzs = p.op("dve", "tensor_copy",
               Zs_sb[:].rearrange("p (t j i) -> p t j i", t=TT, j=4),
               Z_sb[:].rearrange("p (t i j) -> p t j i", t=TT, i=4),
               deps=[tz])
    tu = p.op("pool", "memset", U_sb[:], 0.0)
    tv = p.op("pool", "memset", V_sb[:], 0.0)
    u_tok, v_tok = tu, tv
    T_free = None
    E_free = None
    S_free = None
    for it in range(SINK_ITERS):
        adds = []
        for i in range(4):
            dd = [tz, v_tok]
            if T_free is not None:
                dd.append(T_free)
            adds.append(p.op("dve", "tensor_tensor", T3[:, :, i : i + 1, :], Z3[:, :, i : i + 1, :],
                             V3[:], op=ALU.add, deps=dd))
        dd = adds + ([E_free] if E_free else [])
        ex = p.op("act", "activation", E_sb[:], T_sb[:], AF.Exp, deps=dd)
        T_free = ex
        dd = [ex] + ([S_free] if S_free else [])
        s1 = p.op("dve", "tensor_tensor", S3[:], E3[:, :, :, 0:1], E3[:, :, :, 1:2],
                  op=ALU.add, deps=dd)
        s2 = p.op("dve", "tensor_tensor", S3[:], S3[:], E3[:, :, :, 2:3],
                  op=ALU.add, deps=[s1])
        s3 = p.op("dve", "tensor_tensor", S3[:], S3[:], E3[:, :, :, 3:4],
                  op=ALU.add, deps=[s2])
        E_free = s3
        ln = p.op("act", "activation", S_sb[:], S_sb[:], AF.Ln, deps=[s3])
        u_tok = p.op("dve", "tensor_scalar", U_sb[:], S_sb[:], -1.0, None,
                     op0=ALU.mult, deps=[ln, u_tok])
        S_free = u_tok
        adds = []
        for j in range(4):
            adds.append(p.op("dve", "tensor_tensor", T3[:, :, j : j + 1, :], Zs3[:, :, j : j + 1, :],
                             U3[:], op=ALU.add, deps=[tzs, u_tok, T_free]))
        ex = p.op("act", "activation", E_sb[:], T_sb[:], AF.Exp, deps=adds + [E_free])
        T_free = ex
        s1 = p.op("dve", "tensor_tensor", S3[:], E3[:, :, :, 0:1], E3[:, :, :, 1:2],
                  op=ALU.add, deps=[ex, S_free])
        s2 = p.op("dve", "tensor_tensor", S3[:], S3[:], E3[:, :, :, 2:3],
                  op=ALU.add, deps=[s1])
        s3 = p.op("dve", "tensor_tensor", S3[:], S3[:], E3[:, :, :, 3:4],
                  op=ALU.add, deps=[s2])
        E_free = s3
        ln = p.op("act", "activation", S_sb[:], S_sb[:], AF.Ln, deps=[s3])
        v_tok = p.op("dve", "tensor_scalar", V_sb[:], S_sb[:], -1.0, None,
                     op0=ALU.mult, deps=[ln, v_tok])
        S_free = v_tok
    adds = []
    for i in range(4):
        adds.append(p.op("dve", "tensor_tensor", T3[:, :, i : i + 1, :], Z3[:, :, i : i + 1, :],
                         V3[:], op=ALU.add, deps=[tz, v_tok, T_free]))
    sw = p.op("dve", "tensor_copy",
              E_sb[:].rearrange("p (t j i) -> p t j i", t=TT, j=4),
              T_sb[:].rearrange("p (t i j) -> p t j i", t=TT, i=4),
              deps=adds + [E_free])
    adds2 = []
    for j in range(4):
        adds2.append(p.op("dve", "tensor_tensor", E3[:, :, j : j + 1, :], E3[:, :, j : j + 1, :],
                          U3[:], op=ALU.add, deps=[sw, u_tok]))
    hx = p.op("act", "activation", H_sb[:], E_sb[:], AF.Exp, deps=adds2)
    sg = p.op("act", "activation", gate_sb[0:1, 1:2], gate_sb[0:1, 0:1],
              AF.Sigmoid, deps=[t_g])
    bcg = p.op("pe", "matmul", ps0[0:128, 0:1], onesf_sb[:], gate_sb[0:1, 1:2],
               start=True, stop=True, deps=[sg, t_of, ps_free[0]])
    evg = p.op("dve", "tensor_copy", gbc_sb[:, 0:1], ps0[0:128, 0:1], deps=[bcg])
    onemg = p.op("dve", "tensor_scalar", gbc_sb[:, 1:2], gbc_sb[:, 0:1],
                 -1.0, 1.0, op0=ALU.mult, op1=ALU.add, deps=[evg])
    hg = p.op("dve", "tensor_scalar", H_sb[:], H_sb[:], gbc_sb[:, 0:1], None,
              op0=ALU.mult, deps=[hx, evg])


    psy_free = [evg, ps_free[1]]
    y_free = [None, None]
    xr_tok = [None, None]
    final_toks = []
    acc_tok = [None, None]
    xo_tok = [None, None]
    y_done = []
    for t in range(TT):
        ps = (ps0, ps1)[t % 2]
        yb = y2_sb[t % 2]
        last = None
        for cc in range(8):
            dd = [t_wbr, xpreT_toks[t]] if last is None else [last]
            if last is None and psy_free[t % 2] is not None:
                dd.append(psy_free[t % 2])
            if last is None and y_free[t % 2] is not None:
                dd.append(y_free[t % 2])
            last = p.op("pe", "matmul", ps[0:128, 0:512],
                        xpreT_sb[:, cc * NTOK + t * 128 : cc * NTOK + (t + 1) * 128],
                        wbr_sb[:, cc * 1024 : cc * 1024 + 512],
                        start=(cc == 0), stop=(cc == 7), deps=dd)
        ev1 = p.op("act", "copy", yb[:, 0:512], ps[0:128, 0:512],
                   deps=[last] + ([y_free[t % 2]] if y_free[t % 2] else []))
        last = None
        for cc in range(8):
            dd = [t_wbr, xpreT_toks[t], ev1] if last is None else [last]
            last = p.op("pe", "matmul", ps[0:128, 0:512],
                        xpreT_sb[:, cc * NTOK + t * 128 : cc * NTOK + (t + 1) * 128],
                        wbr_sb[:, cc * 1024 + 512 : (cc + 1) * 1024],
                        start=(cc == 0), stop=(cc == 7), deps=dd)
        ev2 = p.op("act", "copy", yb[:, 512:1024], ps[0:128, 0:512], deps=[last])
        psy_free[t % 2] = ev2
        y_done.append(ev2)
        last_add = None
        for i in range(4):
            ri = (t * 4 + i) % 2
            dd = [] if xr_tok[ri] is None else [xr_tok[ri]]
            t_xr = p.dma("act", xr_sb[ri][:],
                         xres[t * 128 : (t + 1) * 128, i * 1024 : (i + 1) * 1024],
                         deps=dd, stream=f"xr{ri}")
            ai = i % 2
            dd = [t_xr, t_xh, onemg, hg, ev1, ev2, hpost_toks[t]]
            if acc_tok[ai] is not None:
                dd.append(acc_tok[ai])
            a = p.op("dve", "scalar_tensor_tensor", acc_sb[ai][:],
                     xh_sb[:, t * D + i * 1024 : t * D + (i + 1) * 1024],
                     gbc_sb[:, 1:2], xr_sb[ri][:],
                     op0=ALU.mult, op1=ALU.add, deps=dd)
            for j in range(4):
                a = p.op("dve", "scalar_tensor_tensor", acc_sb[ai][:],
                         xh_sb[:, t * D + j * 1024 : t * D + (j + 1) * 1024],
                         H_sb[:, t * 16 + 4 * j + i : t * 16 + 4 * j + i + 1],
                         acc_sb[ai][:], op0=ALU.mult, op1=ALU.add, deps=[a])
            oi = i % 2
            do = [a]
            if xo_tok[oi] is not None:
                do.append(xo_tok[oi])
            fin = p.op("dve", "scalar_tensor_tensor", xo_sb[oi][:],
                       yb[:], hpost_sb[:, t * 4 + i : t * 4 + i + 1],
                       acc_sb[ai][:], op0=ALU.mult, op1=ALU.add, deps=do)
            acc_tok[ai] = fin
            last_add = fin
            xr_tok[ri] = a
            y_free[t % 2] = fin
            st = p.dma("sp", xout[t * 128 : (t + 1) * 128, i * 1024 : (i + 1) * 1024],
                       xo_sb[oi][:], deps=[fin], stream=f"xo{oi}")
            xo_tok[oi] = st
            final_toks.append(st)
    return final_toks + y_done


def build_l0ffn(nc):
    p = Prog(nc)
    A = {}
    A["xh2"] = nc.dram_tensor("xh2", [NTOK, D], BF16, kind="ExternalInput").ap()
    A["xh2T"] = nc.dram_tensor("xh2T", [D, NTOK], BF16, kind="ExternalInput").ap()
    A["WstkT"] = nc.dram_tensor("WstkT", [D, 24], F32, kind="ExternalInput").ap()
    A["bases_bc"] = nc.dram_tensor("bases_bc", [128, 24], F32, kind="ExternalInput").ap()
    A["gate"] = nc.dram_tensor("gate", [1, 1], F32, kind="ExternalInput").ap()
    A["Wbr"] = nc.dram_tensor("Wbr", [1024, 1024], F32, kind="ExternalInput").ap()
    A["xres"] = nc.dram_tensor("xres", [NTOK, D], F32, kind="ExternalInput").ap()
    A["xout"] = nc.dram_tensor("xout", [NTOK, D], F32, kind="ExternalOutput").ap()
    toks = emit_l0ffn(p, nc, A, "", nc.alloc_sbuf_tensor, nc.alloc_psum_tensor)
    p.finish(toks)
    p.emit()
    return nc


def emit_l1qkv(p, nc, A, pfx, asb, aps_):
    xhT = A["xhT"]
    wqkv = A["wqkv"]
    qkvT = A["qkvT"]

    xhT_sb = asb(pfx + "xhT_sb", [128, FC * NTOK], BF16)
    wc_sb = [asb(pfx + f"wc_sb{i}", [128, FC * 128], BF16) for i in range(2)]
    o_sb = [asb(pfx + f"o_sb{i}", [128, 512], BF16) for i in range(2)]
    ps = [aps_(pfx + f"ps{i}", [128, 512], F32) for i in range(2)]

    t_xh = p.dma("sp", xhT_sb[:].rearrange("p (c t) -> p c t", c=FC),
                 xhT.rearrange("(c p) t -> p c t", p=128))
    w_tok = [None, None]
    ps_tok = [None, None]
    o_tok = [None, None]
    st_toks = []
    for colb in range(48):
        wi = colb % 2
        dd = [] if w_tok[wi] is None else [w_tok[wi]]
        t_w = p.dma("pool", wc_sb[wi][:].rearrange("p (c n) -> p c n", c=FC),
                    wqkv[:, colb * 128 : (colb + 1) * 128].rearrange(
                        "(c p) n -> p c n", p=128),
                    deps=dd, stream=f"w{wi}")
        last_any = None
        for tc in range(2):
            i = (colb * 2 + tc) % 2
            last = None
            for c in range(FC):
                dd = [t_w, t_xh] if last is None else [last]
                if last is None and ps_tok[i] is not None:
                    dd.append(ps_tok[i])
                last = p.op("pe", "matmul", ps[i][0:128, 0:512],
                            wc_sb[wi][:, c * 128 : (c + 1) * 128],
                            xhT_sb[:, c * NTOK + tc * 512 : c * NTOK + (tc + 1) * 512],
                            start=(c == 0), stop=(c == FC - 1), deps=dd)
            dd = [last]
            if o_tok[i] is not None:
                dd.append(o_tok[i])
            ev = p.op("dve", "tensor_copy", o_sb[i][:], ps[i][0:128, 0:512], deps=dd)
            ps_tok[i] = ev
            st = p.dma("sp", qkvT[colb * 128 : (colb + 1) * 128, tc * 512 : (tc + 1) * 512],
                       o_sb[i][:], deps=[ev], stream=f"o{i}")
            o_tok[i] = st
            st_toks.append(st)
            last_any = last
        w_tok[wi] = last_any
    return st_toks


def build_l1qkv(nc):
    p = Prog(nc)
    A = {}
    A["xhT"] = nc.dram_tensor("xhT", [D, NTOK], BF16, kind="ExternalInput").ap()
    A["wqkv"] = nc.dram_tensor("wqkv", [D, 6144], F32, kind="ExternalInput").ap()
    A["qkvT"] = nc.dram_tensor("qkvT", [6144, NTOK], BF16, kind="ExternalOutput").ap()
    toks = emit_l1qkv(p, nc, A, "", nc.alloc_sbuf_tensor, nc.alloc_psum_tensor)
    p.finish(toks)
    p.emit()
    return nc


L1_SHIFT = 12.0
NKB_OF_QC = [4 * qc + 4 for qc in range(8)]


A2A_CHUNKS = [[(0, r), (1, 7 - r)] for r in range(4)]


def emit_l1att(p, nc, A, pfx, asb, aps_):
    a2a = "qA2A" in A
    """GQA attention with sqrt-alibi causal bias, head-parallel.

    Per core: 8 (h,g) units of its batch, full 4096-token causal q range.
    in: QTs [1024,4096] bf16 (8 units x 128 d), KTc [256,4096] bf16 (2 kv
    heads), Vc [4096,256] bf16, SDrep [128,4608] f32, scales_bc [128,16] f32
    (col 2u: 1/(slope*sqrt(128)), col 2u+1: slope).
    out: OUTN [1024, 4096] bf16 (normalized attention out^T, unit-major rows).
    """
    SDrep = A["SDrep"]
    scales = A["scales_bc"]
    OUTN = A["OUTN"]
    if not a2a:
        QTs, KTc, Vc = A["QTs"], A["KTc"], A["Vc"]

    sd_sb = asb(pfx + "sd_sb", [128, 4608], F32)
    scl_sb = asb(pfx + "scl_sb", [128, 16], F32)
    kt_sb = [asb(pfx + f"kt_sb{i}", [128, S], BF16) for i in range(2)]
    v_sb = [asb(pfx + f"v_sb{i}", [128, 32 * 128], BF16) for i in range(2)]
    q_sb = [asb(pfx + f"q_sb{i}", [128, S], BF16) for i in range(2)]
    qs_sb = [asb(pfx + f"qs_sb{i}", [128, S], BF16) for i in range(2)]
    t_sb = [asb(pfx + f"t_sb{i}", [128, 512], F32) for i in range(4)]
    at_sb = [asb(pfx + f"at_sb{i}", [128, 512], BF16) for i in range(4)]
    rb_sb = [asb(pfx + f"rb_sb{i}", [1, 512], BF16) for i in range(2)]
    rbf_sb = asb(pfx + "rbf_sb", [1, 512], F32)
    rbc_sb = [asb(pfx + f"rbc_sb{i}", [128, 512], F32) for i in range(2)]
    on_sb = [asb(pfx + f"on_sb{i}", [128, 512], BF16) for i in range(2)]
    onesc_sb = asb(pfx + "onesc_sb", [128, 1], BF16)
    onesr_sb = asb(pfx + "onesr_sb", [1, 128], BF16)

    psS = [aps_(pfx + f"psS{i}", [128, 512], F32) for i in range(4)]
    psO = [aps_(pfx + f"psO{i}", [128, 512], F32) for i in range(2)]
    psR = aps_(pfx + "psR", [128, 512], F32)
    psB = aps_(pfx + "psB", [128, 512], F32)

    negB, t_negB = p.const_col(-L1_SHIFT)
    t_sd = p.dma("sp", sd_sb[:], SDrep)
    t_scl = p.dma("sp", scl_sb[:], scales)
    t_oc = p.op("pool", "memset", onesc_sb[:], 1.0)
    t_or = p.op("pool", "memset", onesr_sb[:], 1.0)

    kv_tok = [[None, None], [None, None]]
    q_tok = [None, None]
    qs_tok = [None, None]
    psS_tok = [None] * 4
    at_tok = [None] * 4
    t_tok = [None] * 4
    psO_free = [None, None]
    psR_free = None
    rb_free = [None, None]
    rbc_free = [None, None]
    psB_guard = None
    on_tok = [None, None]
    st_toks = []
    kt_cur = [None, None]
    v_cur = [None, None]
    for u in range(8):
        bb = u // 4
        hl = bb
        g = u % 4
        if g == 0:
            hb = hl % 2
            dk = [] if kv_tok[0][hb] is None else [kv_tok[0][hb]]
            dv_ = [] if kv_tok[1][hb] is None else [kv_tok[1][hb]]
            if a2a:
                for j in range(8):
                    srcr = j if bb == 0 else 7 - j
                    kt_cur[hb] = p.dma(
                        "sp", kt_sb[hb][:, j * 512 : (j + 1) * 512],
                        A["kA2A"][srcr * 128 : (srcr + 1) * 128,
                                  bb * 512 : (bb + 1) * 512],
                        deps=dk, stream=f"kt{hb}")
                    for sub in range(4):
                        kb = j * 4 + sub
                        v_cur[hb] = p.dma(
                            "act", v_sb[hb][:, kb * 128 : (kb + 1) * 128],
                            A["vA2A"][srcr * 128 : (srcr + 1) * 128,
                                      bb * 512 + sub * 128 : bb * 512 + (sub + 1) * 128],
                            deps=dv_, transpose=True, stream=f"v{hb}")
            else:
                kt_cur[hb] = p.dma("sp", kt_sb[hb][:], KTc[hl * 128 : (hl + 1) * 128, :],
                                   deps=dk, stream=f"kt{hb}")
                v_cur[hb] = p.dma("act", v_sb[hb][:].rearrange("p (k n) -> p k n", k=32),
                                  Vc[:, hl * 128 : (hl + 1) * 128].rearrange(
                                      "(k p) n -> p k n", p=128),
                                  deps=dv_, stream=f"v{hb}")
        hb = hl % 2
        qi = u % 2
        dq = [] if q_tok[qi] is None else [q_tok[qi]]
        if a2a:
            for j in range(8):
                srcr = j if bb == 0 else 7 - j
                t_q = p.dma("sp", q_sb[qi][:, j * 512 : (j + 1) * 512],
                            A["qA2A"][srcr * 512 + g * 128 : srcr * 512 + (g + 1) * 128,
                                      bb * 512 : (bb + 1) * 512],
                            deps=dq, stream=f"q{qi}")
        else:
            t_q = p.dma("sp", q_sb[qi][:], QTs[u * 128 : (u + 1) * 128, :],
                        deps=dq, stream=f"q{qi}")
        dqs = [t_q, t_scl]
        if qs_tok[qi] is not None:
            dqs.append(qs_tok[qi])
        t_qs = p.op("dve", "tensor_scalar", qs_sb[qi][:], q_sb[qi][:],
                    scl_sb[:, 2 * u : 2 * u + 1], None, op0=ALU.mult, deps=dqs)
        q_tok[qi] = t_qs
        last_mm_s = None
        for qc in range(8):
            Q0 = 512 * qc
            nkb = NKB_OF_QC[qc]
            oi = (u * 8 + qc) % 2
            mm_av = None
            mm_sum = None
            mm_s_l = {}
            sub_l = {}
            ex_l = {}
            NB = 4
            for step in range(nkb + 3):
                if step < nkb:
                    kb = step
                    si = kb % NB
                    da = [t_qs, kt_cur[hb]]
                    if psS_tok[si] is not None:
                        da.append(psS_tok[si])
                    mm_s_l[kb] = p.op("pe", "matmul", psS[si][0:128, 0:512],
                                      kt_sb[hb][:, kb * 128 : (kb + 1) * 128],
                                      qs_sb[qi][:, Q0 : Q0 + 512],
                                      start=True, stop=True, deps=da)
                    last_mm_s = mm_s_l[kb]
                if step >= 1 and step - 1 < nkb:
                    kb = step - 1
                    si = kb % NB
                    off = Q0 - 128 * kb + 512
                    dsub = [mm_s_l[kb], t_sd]
                    if t_tok[si] is not None:
                        dsub.append(t_tok[si])
                    sub_l[kb] = p.op("dve", "tensor_tensor", t_sb[si][:],
                                     psS[si][0:128, 0:512], sd_sb[:, off : off + 512],
                                     op=ALU.subtract, deps=dsub)
                    psS_tok[si] = sub_l[kb]
                if step >= 2 and step - 2 < nkb:
                    kb = step - 2
                    si = kb % NB
                    dex = [sub_l[kb], t_negB, t_scl]
                    if at_tok[si] is not None:
                        dex.append(at_tok[si])
                    ex_l[kb] = p.op("act", "activation", at_sb[si][:], t_sb[si][:],
                                    AF.Exp, scale=scl_sb[:, 2 * u + 1 : 2 * u + 2],
                                    bias=negB, deps=dex)
                    t_tok[si] = ex_l[kb]
                if step >= 3:
                    kb = step - 3
                    si = kb % NB
                    dav = [ex_l[kb], v_cur[hb]]
                    if mm_av is not None:
                        dav.append(mm_av)
                    elif psO_free[oi] is not None:
                        dav.append(psO_free[oi])
                    mm_av = p.op("pe", "matmul", psO[oi][0:128, 0:512],
                                 v_sb[hb][:, kb * 128 : (kb + 1) * 128],
                                 at_sb[si][:], start=(kb == 0), stop=(kb == nkb - 1),
                                 deps=dav)
                    dsm = [ex_l[kb], t_oc]
                    if mm_sum is not None:
                        dsm.append(mm_sum)
                    elif psR_free is not None:
                        dsm.append(psR_free)
                    mm_sum = p.op("pe", "matmul", psR[0:1, 0:512], onesc_sb[:],
                                  at_sb[si][:], start=(kb == 0), stop=(kb == nkb - 1),
                                  deps=dsm)
                    at_tok[si] = mm_sum
                    del ex_l[kb]
            drb = [mm_sum]
            if rb_free[oi] is not None:
                drb.append(rb_free[oi])
            rec0 = p.op("act", "activation", rbf_sb[:], psR[0:1, 0:512],
                        AF.Ln, deps=drb)
            psR_free = rec0
            rec = p.op("act", "activation", rb_sb[oi][:], rbf_sb[:], AF.Exp,
                       scale=-1.0, deps=[rec0])
            dbc = [rec, t_or]
            if psB_guard is not None:
                dbc.append(psB_guard)
            bc = p.op("pe", "matmul", psB[0:128, 0:512], onesr_sb[:], rb_sb[oi][:],
                      start=True, stop=True, deps=dbc)
            rb_free[oi] = bc
            dev = [bc]
            if rbc_free[oi] is not None:
                dev.append(rbc_free[oi])
            evbc = p.op("act", "copy", rbc_sb[oi][:], psB[0:128, 0:512], deps=dev)
            psB_guard = evbc
            dn = [mm_av, evbc]
            if on_tok[oi] is not None:
                dn.append(on_tok[oi])
            nm = p.op("dve", "tensor_tensor", on_sb[oi][:], psO[oi][0:128, 0:512],
                      rbc_sb[oi][:], op=ALU.mult, deps=dn)
            psO_free[oi] = nm
            rbc_free[oi] = nm
            if a2a:
                jdst = qc if bb == 0 else 7 - qc
                st = p.dma("sp",
                           OUTN[jdst * 512 + g * 128 : jdst * 512 + (g + 1) * 128,
                                bb * 512 : (bb + 1) * 512],
                           on_sb[oi][:], deps=[nm], stream=f"on{oi}")
            else:
                st = p.dma("sp", OUTN[u * 128 : (u + 1) * 128, Q0 : Q0 + 512],
                           on_sb[oi][:], deps=[nm], stream=f"on{oi}")
            on_tok[oi] = st
            st_toks.append(st)
        kv_tok[0][hb] = last_mm_s
        kv_tok[1][hb] = mm_av
        q_tok[qi] = last_mm_s
        qs_tok[qi] = last_mm_s
    return st_toks


def build_l1att(nc):
    p = Prog(nc)
    A = {}
    A["QTs"] = nc.dram_tensor("QTs", [NTOK, S], BF16, kind="ExternalInput").ap()
    A["KTc"] = nc.dram_tensor("KTc", [256, S], BF16, kind="ExternalInput").ap()
    A["Vc"] = nc.dram_tensor("Vc", [S, 256], BF16, kind="ExternalInput").ap()
    A["SDrep"] = nc.dram_tensor("SDrep", [128, 4608], F32, kind="ExternalInput").ap()
    A["scales_bc"] = nc.dram_tensor("scales_bc", [128, 16], F32, kind="ExternalInput").ap()
    A["OUTN"] = nc.dram_tensor("OUTN", [NTOK, S], BF16, kind="ExternalOutput").ap()
    toks = emit_l1att(p, nc, A, "", nc.alloc_sbuf_tensor, nc.alloc_psum_tensor)
    p.finish(toks)
    p.emit()
    return nc


def emit_l1wo(p, nc, A, pfx, asb, aps_):
    OT = A["OT"]
    wo = A["wo"]
    xres = A["xres"]
    xout = A["xout"]

    ot_sb = asb(pfx + "ot_sb", [128, TT * FC * 128], BF16)
    wc_sb = [asb(pfx + f"wc_sb{i}", [128, FC * 512], BF16) for i in range(2)]
    xr_sb = [asb(pfx + f"xr_sb{i}", [128, 512], F32) for i in range(2)]
    xo_sb = [asb(pfx + f"xo_sb{i}", [128, 512], F32) for i in range(2)]
    ps = [aps_(pfx + f"ps{i}", [128, 512], F32) for i in range(2)]

    ot_toks = []
    for tt in range(TT):
        ot_toks.append(p.dma(
            "sp",
            ot_sb[:].rearrange("p (t c n) -> p t c n", t=TT, c=FC)[:, tt : tt + 1, :, :],
            OT[:, tt * 128 : (tt + 1) * 128].rearrange("(c p) n -> p c n", p=128)))
    w_tok = [None, None]
    ps_tok = [None, None]
    xr_tok = [None, None]
    xo_tok = [None, None]
    st_toks = []
    for cb in range(8):
        wi = cb % 2
        dd = [] if w_tok[wi] is None else [w_tok[wi]]
        t_w = p.dma("pool", wc_sb[wi][:].rearrange("p (c n) -> p c n", c=FC),
                    wo[:, cb * 512 : (cb + 1) * 512].rearrange("(c p) n -> p c n", p=128),
                    deps=dd, stream=f"w{wi}")
        last_any = None
        for tt in range(TT):
            i = (cb * TT + tt) % 2
            dx = [] if xr_tok[i] is None else [xr_tok[i]]
            t_xr = p.dma("act", xr_sb[i][:],
                         xres[tt * 128 : (tt + 1) * 128, cb * 512 : (cb + 1) * 512],
                         deps=dx, stream=f"xr{i}")
            last = None
            for c in range(FC):
                dd = [t_w, ot_toks[tt]] if last is None else [last]
                if last is None and ps_tok[i] is not None:
                    dd.append(ps_tok[i])
                last = p.op("pe", "matmul", ps[i][0:128, 0:512],
                            ot_sb[:, tt * FC * 128 + c * 128 : tt * FC * 128 + (c + 1) * 128],
                            wc_sb[wi][:, c * 512 : (c + 1) * 512],
                            start=(c == 0), stop=(c == FC - 1), deps=dd)
            last_any = last
            dadd = [last, t_xr]
            if xo_tok[i] is not None:
                dadd.append(xo_tok[i])
            add = p.op("dve", "tensor_tensor", xo_sb[i][:], ps[i][0:128, 0:512],
                       xr_sb[i][:], op=ALU.add, deps=dadd)
            ps_tok[i] = add
            xr_tok[i] = add
            st = p.dma("sp", xout[tt * 128 : (tt + 1) * 128, cb * 512 : (cb + 1) * 512],
                       xo_sb[i][:], deps=[add], stream=f"xo{i}")
            xo_tok[i] = st
            st_toks.append(st)
        w_tok[wi] = last_any
    return st_toks


def build_l1wo(nc):
    p = Prog(nc)
    A = {}
    A["OT"] = nc.dram_tensor("OT", [D, NTOK], BF16, kind="ExternalInput").ap()
    A["wo"] = nc.dram_tensor("wo", [D, D], F32, kind="ExternalInput").ap()
    A["xres"] = nc.dram_tensor("xres", [NTOK, D], F32, kind="ExternalInput").ap()
    A["xout"] = nc.dram_tensor("xout", [NTOK, D], F32, kind="ExternalOutput").ap()
    toks = emit_l1wo(p, nc, A, "", nc.alloc_sbuf_tensor, nc.alloc_psum_tensor)
    p.finish(toks)
    p.emit()
    return nc


def emit_l1ffa(p, nc, A, pfx, asb, aps_):
    xhT = A["xhT"]
    gw1T = A["gw1T"]
    gb1 = A["gb1"]
    gamma = A["gamma_bc"]
    beta = A["beta_bc"]
    gw2T = A["gw2T"]
    gb2 = A["gb2"]
    oh1 = A["oh1"]
    oh2 = A["oh2"]
    g1o = A["g1"]
    pco = A["pc"]

    xhT_sb = asb(pfx + "xhT_sb", [128, FC * NTOK], BF16)
    gw1_sb = asb(pfx + "gw1_sb", [128, FC * 32], BF16)
    gb1_sb = asb(pfx + "gb1_sb", [1, 32], BF16)
    gam_sb = asb(pfx + "gam_sb", [128, 32], F32)
    bet_sb = asb(pfx + "bet_sb", [128, 32], F32)
    gw2_sb = asb(pfx + "gw2_sb", [32, 8], BF16)
    gb2_sb = asb(pfx + "gb2_sb", [1, 8], BF16)
    h_sb = asb(pfx + "h_sb", [128, TT * 32], F32)
    hr_sb = asb(pfx + "hr_sb", [128, TT * 32], BF16)
    hT_sb = asb(pfx + "hT_sb", [32, TT * 128], BF16)
    st_sb = asb(pfx + "st_sb", [128, 6], F32)
    mv_sb = asb(pfx + "mv_sb", [128, 4], F32)
    g_sb = asb(pfx + "g_sb", [128, TT * 8], F32)
    m_sb = asb(pfx + "m_sb", [128, 8], F32)
    oh1_sb = asb(pfx + "oh1_sb", [128, TT * 8], BF16)
    oh2_sb = asb(pfx + "oh2_sb", [128, TT * 8], BF16)
    msk_sb = asb(pfx + "msk_sb", [128, 8], F32)
    g1_sb = asb(pfx + "g1_sb", [128, TT], F32)
    pc_sb = asb(pfx + "pc_sb", [1, 16], F32)
    onesr_sb = asb(pfx + "onesr_sb", [1, 128], BF16)
    onesc_sb = asb(pfx + "onesc_sb", [128, 1], BF16)

    ps0 = aps_(pfx + "ps0", [128, 512], F32)
    ps1 = aps_(pfx + "ps1", [128, 512], F32)
    ptps = aps_(pfx + "ptps", [128, 1024], BF16)
    psc = aps_(pfx + "psc", [128, 512], F32)
    psd = aps_(pfx + "psd", [128, 512], F32)
    psg = aps_(pfx + "psg", [128, 512], F32)

    ident, t_id = emit_ident(p, nc, asb, pfx + "ident")
    t_xh = p.dma("sp", xhT_sb[:].rearrange("p (c t) -> p c t", c=FC),
                 xhT.rearrange("(c p) t -> p c t", p=128))
    t_w1 = p.dma("pool", gw1_sb[:].rearrange("p (c n) -> p c n", c=FC),
                 gw1T.rearrange("(c p) n -> p c n", p=128))
    t_b1 = p.dma("pool", gb1_sb[:], gb1)
    t_gm = p.dma("sp", gam_sb[:], gamma)
    t_bt = p.dma("sp", bet_sb[:], beta)
    t_w2 = p.dma("pool", gw2_sb[:], gw2T)
    t_b2 = p.dma("pool", gb2_sb[:], gb2)
    t_or = p.op("pool", "memset", onesr_sb[:], 1.0)
    t_oc = p.op("pool", "memset", onesc_sb[:], 1.0)
    eap, etok = p.const_col(LN_EPS)

    ps_tok = [None, None]
    st_free = None
    mv_free = None
    h_toks = []
    for tt in range(TT):
        psi = (ps0, ps1)[tt % 2]
        last = None
        for c in range(FC):
            dd = [t_xh, t_w1] if last is None else [last]
            if last is None and ps_tok[tt % 2] is not None:
                dd.append(ps_tok[tt % 2])
            last = p.op("pe", "matmul", psi[0:128, 0:32],
                        xhT_sb[:, c * NTOK + tt * 128 : c * NTOK + (tt + 1) * 128],
                        gw1_sb[:, c * 32 : (c + 1) * 32],
                        start=(c == 0), stop=False, deps=dd)
        last = p.op("pe", "matmul", psi[0:128, 0:32], onesr_sb[:], gb1_sb[:],
                    start=False, stop=True, deps=[last, t_or, t_b1])
        ev = p.op("dve", "tensor_copy", h_sb[:, tt * 32 : (tt + 1) * 32],
                  psi[0:128, 0:32], deps=[last])
        ps_tok[tt % 2] = ev
        dd = [ev] + ([st_free] if st_free else [])
        stt = p.op("dve", "bn_stats", st_sb[:], h_sb[:, tt * 32 : (tt + 1) * 32],
                   deps=dd)
        dd = [stt] + ([mv_free] if mv_free else [])
        ag = p.op("dve", "bn_aggr", mv_sb[:, 0:2], st_sb[:], deps=dd)
        st_free = ag
        t3 = p.op("act", "activation", mv_sb[:, 2:3], mv_sb[:, 1:2], AF.Ln,
                  bias=eap, scale=1.0, deps=[ag, etok])
        t4 = p.op("act", "activation", mv_sb[:, 3:4], mv_sb[:, 2:3], AF.Exp,
                  scale=-0.5, deps=[t3])
        mv_free = t4
        ap_ = p.op("dve", "tensor_scalar", h_sb[:, tt * 32 : (tt + 1) * 32],
                   h_sb[:, tt * 32 : (tt + 1) * 32], mv_sb[:, 0:1], mv_sb[:, 3:4],
                   op0=ALU.subtract, op1=ALU.mult, deps=[t4])
        gm = p.op("dve", "tensor_tensor", h_sb[:, tt * 32 : (tt + 1) * 32],
                  h_sb[:, tt * 32 : (tt + 1) * 32], gam_sb[:], op=ALU.mult,
                  deps=[ap_, t_gm])
        bt = p.op("dve", "tensor_tensor", h_sb[:, tt * 32 : (tt + 1) * 32],
                  h_sb[:, tt * 32 : (tt + 1) * 32], bet_sb[:], op=ALU.add,
                  deps=[gm, t_bt])
        rl = p.op("dve", "tensor_scalar", hr_sb[:, tt * 32 : (tt + 1) * 32],
                  h_sb[:, tt * 32 : (tt + 1) * 32], 0.0, None, op0=ALU.max,
                  deps=[bt])
        h_toks.append(rl)
    pt_free = None
    hT_toks = []
    for tt in range(TT):
        dd = [h_toks[tt], t_id] + ([pt_free] if pt_free else [])
        tr = p.op("pe", "transpose", ptps[0:32, 0:128],
                  hr_sb[:, tt * 32 : (tt + 1) * 32], ident[:], deps=dd)
        ev = p.op("dve", "tensor_copy", hT_sb[0:32, tt * 128 : (tt + 1) * 128],
                  ptps[0:32, 0:128], deps=[tr])
        pt_free = ev
        hT_toks.append(ev)
    pc1_mm = None
    pc2_mm = None
    g_free = None
    m_free = None
    msk_free = None
    oh_toks = []
    g1_toks = []
    for tt in range(TT):
        dd = [hT_toks[tt], t_w2]
        if g_free is not None:
            dd.append(g_free)
        mm = p.op("pe", "matmul", psg[0:128, 0:8],
                  hT_sb[0:32, tt * 128 : (tt + 1) * 128], gw2_sb[:],
                  start=True, stop=False, deps=dd)
        mm = p.op("pe", "matmul", psg[0:128, 0:8], onesr_sb[:], gb2_sb[:],
                  start=False, stop=True, deps=[mm, t_or, t_b2])
        ev = p.op("dve", "tensor_copy", g_sb[:, tt * 8 : (tt + 1) * 8],
                  psg[0:128, 0:8], deps=[mm])
        g_free = ev
        dd = [ev] + ([m_free] if m_free else [])
        m1 = p.op("dve", "tensor_reduce", m_sb[:, 0:1],
                  g_sb[:, tt * 8 : (tt + 1) * 8], axis=AX.X, op=ALU.max, deps=dd)
        o1 = p.op("dve", "tensor_scalar", oh1_sb[:, tt * 8 : (tt + 1) * 8],
                  g_sb[:, tt * 8 : (tt + 1) * 8], m_sb[:, 0:1], None,
                  op0=ALU.is_equal, deps=[m1])
        dd = [o1] + ([msk_free] if msk_free else [])
        mk = p.op("dve", "scalar_tensor_tensor", msk_sb[:],
                  oh1_sb[:, tt * 8 : (tt + 1) * 8], -1e30,
                  g_sb[:, tt * 8 : (tt + 1) * 8], op0=ALU.mult, op1=ALU.add,
                  deps=dd)
        m2 = p.op("dve", "tensor_reduce", m_sb[:, 1:2], msk_sb[:], axis=AX.X,
                  op=ALU.max, deps=[mk])
        o2 = p.op("dve", "tensor_scalar", oh2_sb[:, tt * 8 : (tt + 1) * 8],
                  msk_sb[:], m_sb[:, 1:2], None, op0=ALU.is_equal, deps=[m2])
        msk_free = o2
        df = p.op("dve", "tensor_tensor", m_sb[:, 2:3], m_sb[:, 0:1],
                  m_sb[:, 1:2], op=ALU.subtract,
                  deps=[m2] + ([g1_toks[-1]] if g1_toks else []))
        m_free = df
        sg = p.op("act", "activation", g1_sb[:, tt : tt + 1], m_sb[:, 2:3],
                  AF.Sigmoid, deps=[df])
        g1_toks.append(sg)
        oh_toks.append((o1, o2))
        dd = [o1, t_oc]
        if pc1_mm is not None:
            dd.append(pc1_mm)
        pc1_mm = p.op("pe", "matmul", psc[0:1, 0:8], onesc_sb[:],
                      oh1_sb[:, tt * 8 : (tt + 1) * 8],
                      start=(tt == 0), stop=(tt == TT - 1), deps=dd)
        dd = [o2, t_oc]
        if pc2_mm is not None:
            dd.append(pc2_mm)
        pc2_mm = p.op("pe", "matmul", psd[0:1, 0:8], onesc_sb[:],
                      oh2_sb[:, tt * 8 : (tt + 1) * 8],
                      start=(tt == 0), stop=(tt == TT - 1), deps=dd)
    e1 = p.op("dve", "tensor_copy", pc_sb[:, 0:8], psc[0:1, 0:8],
              deps=[pc1_mm])
    e2 = p.op("dve", "tensor_copy", pc_sb[:, 8:16], psd[0:1, 0:8],
              deps=[pc2_mm, e1])
    s1 = p.dma("sp", oh1.rearrange("(t p) n -> p t n", p=128),
               oh1_sb[:].rearrange("p (t n) -> p t n", t=TT),
               deps=[t[0] for t in oh_toks])
    s2 = p.dma("sp", oh2.rearrange("(t p) n -> p t n", p=128),
               oh2_sb[:].rearrange("p (t n) -> p t n", t=TT),
               deps=[t[1] for t in oh_toks])
    s3 = p.dma("sp", g1o, g1_sb[:], deps=g1_toks)
    s4 = p.dma("sp", pco, pc_sb[:], deps=[e2])
    return [s1, s2, s3, s4]


def build_l1ffa(nc):
    p = Prog(nc)
    A = {}
    A["xhT"] = nc.dram_tensor("xhT", [D, NTOK], BF16, kind="ExternalInput").ap()
    A["gw1T"] = nc.dram_tensor("gw1T", [D, 32], F32, kind="ExternalInput").ap()
    A["gb1"] = nc.dram_tensor("gb1", [1, 32], F32, kind="ExternalInput").ap()
    A["gamma_bc"] = nc.dram_tensor("gamma_bc", [128, 32], F32, kind="ExternalInput").ap()
    A["beta_bc"] = nc.dram_tensor("beta_bc", [128, 32], F32, kind="ExternalInput").ap()
    A["gw2T"] = nc.dram_tensor("gw2T", [32, 8], F32, kind="ExternalInput").ap()
    A["gb2"] = nc.dram_tensor("gb2", [1, 8], F32, kind="ExternalInput").ap()
    A["oh1"] = nc.dram_tensor("oh1", [NTOK, 8], BF16, kind="ExternalOutput").ap()
    A["oh2"] = nc.dram_tensor("oh2", [NTOK, 8], BF16, kind="ExternalOutput").ap()
    A["g1"] = nc.dram_tensor("g1", [128, TT], F32, kind="ExternalOutput").ap()
    A["pc"] = nc.dram_tensor("pc", [1, 16], F32, kind="ExternalOutput").ap()
    toks = emit_l1ffa(p, nc, A, "", nc.alloc_sbuf_tensor, nc.alloc_psum_tensor)
    p.finish(toks)
    p.emit()
    return nc


CAPACITY = 3072.0
ALPHA = 1.0 / 16.0


def emit_l1ffb(p, nc, A, pfx, asb, aps_):
    xhT = A["xhT"]
    pcall = A.get("pcall")
    oh1 = A["oh1"]
    oh2 = A["oh2"]
    g1 = A["g1"]
    wT = A["wT"]
    lA = A["lA"]
    lB = A["lB"]
    xres = A["xres"]
    xout = A["xout"]

    xhT_sb = asb(pfx + "xhT_sb", [128, FC * NTOK], BF16)
    pc_sb = asb(pfx + "pc_sb", [8, 16], F32)
    cnt_sb = asb(pfx + "cnt_sb", [1, 16], F32)
    alw_sb = asb(pfx + "alw_sb", [1, 16], F32)
    abc_sb = asb(pfx + "abc_sb", [128, 16], F32)
    oh1_sb = asb(pfx + "oh1_sb", [128, TT * 8], BF16)
    oh2_sb = asb(pfx + "oh2_sb", [128, TT * 8], BF16)
    g1_sb = asb(pfx + "g1_sb", [128, TT], F32)
    g2_sb = asb(pfx + "g2_sb", [128, TT], F32)
    wg_sb = asb(pfx + "wg_sb", [128, TT * 8], F32)
    a_sb = asb(pfx + "a_sb", [128, 8], F32)
    lA_sb = asb(pfx + "lA_sb", [128, FC * 128], BF16)
    lB_sb = asb(pfx + "lB_sb", [128, FC * 128], BF16)
    tmp_sb = asb(pfx + "tmp_sb", [128, TT * 128], BF16)
    tmpT_sb = asb(pfx + "tmpT_sb", [128, TT * 128], BF16)
    wc_sb = [asb(pfx + f"wc_sb{i}", [128, FC * 512], BF16) for i in range(2)]
    xr_sb = [asb(pfx + f"xr_sb{i}", [128, 512], F32) for i in range(2)]
    xo_sb = [asb(pfx + f"xo_sb{i}", [128, 512], F32) for i in range(2)]
    onesf_sb = asb(pfx + "onesf_sb", [1, 128], F32)
    ones8_sb = asb(pfx + "ones8_sb", [8, 1], F32)

    ps0 = aps_(pfx + "ps0", [128, 512], F32)
    ps1 = aps_(pfx + "ps1", [128, 512], F32)
    ptps = aps_(pfx + "ptps", [128, 1024], BF16)

    ident, t_id = emit_ident(p, nc, asb, pfx + "ident")
    t_xh = p.dma("sp", xhT_sb[:].rearrange("p (c t) -> p c t", c=FC),
                 xhT.rearrange("(c p) t -> p c t", p=128))
    t_pc = p.dma("sp", pc_sb[:], pcall) if pcall is not None else None
    t_o1 = p.dma("sp", oh1_sb[:].rearrange("p (t n) -> p t n", t=TT),
                 oh1.rearrange("(t p) n -> p t n", p=128))
    t_o2 = p.dma("sp", oh2_sb[:].rearrange("p (t n) -> p t n", t=TT),
                 oh2.rearrange("(t p) n -> p t n", p=128))
    t_g1 = p.dma("sp", g1_sb[:], g1)
    t_lA = p.dma("pool", lA_sb[:].rearrange("p (c n) -> p c n", c=FC),
                 lA.rearrange("(c p) n -> p c n", p=128))
    t_lB = p.dma("pool", lB_sb[:], lB)
    t_of = p.op("pool", "memset", onesf_sb[:], 1.0)
    t_o8 = p.op("pool", "memset", ones8_sb[:], 1.0)

    if "pcr" in A:
        ev_c = p.dma("sp", cnt_sb[:], A["pcr"])
    else:
        mm_c = p.op("pe", "matmul", ps0[0:1, 0:16], ones8_sb[:], pc_sb[:],
                    start=True, stop=True, deps=[t_pc, t_o8])
        ev_c = p.op("dve", "tensor_copy", cnt_sb[:], ps0[0:1, 0:16], deps=[mm_c])
    alw = p.op("dve", "tensor_scalar", alw_sb[:], cnt_sb[:], CAPACITY, None,
               op0=ALU.is_le, deps=[ev_c])
    mm_b = p.op("pe", "matmul", ps0[0:128, 16:32], onesf_sb[:], alw_sb[:],
                start=True, stop=True, deps=[alw, t_of, ev_c])
    ev_b = p.op("dve", "tensor_copy", abc_sb[:], ps0[0:128, 16:32], deps=[mm_b])
    wg_toks = []
    a_free = None
    for tt in range(TT):
        g2 = p.op("dve", "tensor_scalar", g2_sb[:, tt : tt + 1],
                  g1_sb[:, tt : tt + 1], -1.0, 1.0, op0=ALU.mult, op1=ALU.add,
                  deps=[t_g1] + ([wg_toks[-1]] if wg_toks else []))
        dd = [t_o1, g2, ev_b]
        if a_free is not None:
            dd.append(a_free)
        a1 = p.op("dve", "tensor_scalar", a_sb[:], oh1_sb[:, tt * 8 : (tt + 1) * 8],
                  g1_sb[:, tt : tt + 1], None, op0=ALU.mult, deps=dd)
        a1b = p.op("dve", "tensor_tensor", wg_sb[:, tt * 8 : (tt + 1) * 8],
                   a_sb[:], abc_sb[:, 0:8], op=ALU.mult, deps=[a1])
        a2 = p.op("dve", "tensor_scalar", a_sb[:], oh2_sb[:, tt * 8 : (tt + 1) * 8],
                  g2_sb[:, tt : tt + 1], None, op0=ALU.mult, deps=[a1b, t_o2])
        a2b = p.op("dve", "tensor_tensor", a_sb[:], a_sb[:], abc_sb[:, 8:16],
                   op=ALU.mult, deps=[a2])
        a_free = a2b
        w = p.op("dve", "tensor_tensor", wg_sb[:, tt * 8 : (tt + 1) * 8],
                 wg_sb[:, tt * 8 : (tt + 1) * 8], a_sb[:], op=ALU.add,
                 deps=[a2b])
        wg_toks.append(w)
    tmp_toks = []
    ps_tok2 = None
    for tt in range(TT):
        last = None
        for c in range(FC):
            dd = [t_xh, t_lA] if last is None else [last]
            if last is None and ps_tok2 is not None:
                dd.append(ps_tok2)
            last = p.op("pe", "matmul", ps1[0:128, 0:128],
                        xhT_sb[:, c * NTOK + tt * 128 : c * NTOK + (tt + 1) * 128],
                        lA_sb[:, c * 128 : (c + 1) * 128],
                        start=(c == 0), stop=(c == FC - 1), deps=dd)
        sc_toks = []
        for e in range(8):
            sc = p.op("dve", "tensor_scalar",
                      tmp_sb[:, tt * 128 + e * 16 : tt * 128 + (e + 1) * 16],
                      ps1[0:128, e * 16 : (e + 1) * 16],
                      wg_sb[:, tt * 8 + e : tt * 8 + e + 1], ALPHA,
                      op0=ALU.mult, op1=ALU.mult, deps=[last, wg_toks[tt]])
            sc_toks.append(sc)
        ps_tok2 = sc_toks[-1]
        tmp_toks.append(sc_toks)
    pt_free = None
    tmpT_toks = []
    for tt in range(TT):
        dd = tmp_toks[tt] + [t_id] + ([pt_free] if pt_free else [])
        tr = p.op("pe", "transpose", ptps[0:128, 0:128],
                  tmp_sb[:, tt * 128 : (tt + 1) * 128], ident[:], deps=dd)
        ev = p.op("dve", "tensor_copy", tmpT_sb[:, tt * 128 : (tt + 1) * 128],
                  ptps[0:128, 0:128], deps=[tr])
        pt_free = ev
        tmpT_toks.append(ev)
    w_tok = [None, None]
    ps_tok = [ev_b, ps_tok2]
    xr_tok = [None, None]
    xo_tok = [None, None]
    st_toks = []
    for cb in range(8):
        wi = cb % 2
        dd = [] if w_tok[wi] is None else [w_tok[wi]]
        t_w = p.dma("pool", wc_sb[wi][:].rearrange("p (c n) -> p c n", c=FC),
                    wT[:, cb * 512 : (cb + 1) * 512].rearrange("(c p) n -> p c n", p=128),
                    deps=dd, stream=f"w{wi}")
        last_any = None
        for tt in range(TT):
            i = (cb * TT + tt) % 2
            dx = [] if xr_tok[i] is None else [xr_tok[i]]
            t_xr = p.dma("act", xr_sb[i][:],
                         xres[tt * 128 : (tt + 1) * 128, cb * 512 : (cb + 1) * 512],
                         deps=dx, stream=f"xr{i}")
            last = None
            for c in range(FC):
                dd = [t_w, t_xh] if last is None else [last]
                if last is None and ps_tok[i] is not None:
                    dd.append(ps_tok[i])
                last = p.op("pe", "matmul", (ps0, ps1)[i][0:128, 0:512],
                            xhT_sb[:, c * NTOK + tt * 128 : c * NTOK + (tt + 1) * 128],
                            wc_sb[wi][:, c * 512 : (c + 1) * 512],
                            start=(c == 0), stop=False, deps=dd)
            last = p.op("pe", "matmul", (ps0, ps1)[i][0:128, 0:512],
                        tmpT_sb[:, tt * 128 : (tt + 1) * 128],
                        lB_sb[:, cb * 512 : (cb + 1) * 512],
                        start=False, stop=True, deps=[last, tmpT_toks[tt], t_lB])
            last_any = last
            dadd = [last, t_xr]
            if xo_tok[i] is not None:
                dadd.append(xo_tok[i])
            add = p.op("dve", "tensor_tensor", xo_sb[i][:],
                       (ps0, ps1)[i][0:128, 0:512], xr_sb[i][:], op=ALU.add,
                       deps=dadd)
            ps_tok[i] = add
            xr_tok[i] = add
            st = p.dma("sp", xout[tt * 128 : (tt + 1) * 128, cb * 512 : (cb + 1) * 512],
                       xo_sb[i][:], deps=[add], stream=f"xo{i}")
            xo_tok[i] = st
            st_toks.append(st)
        w_tok[wi] = last_any
    return st_toks


def build_l1ffb(nc):
    p = Prog(nc)
    A = {}
    A["xhT"] = nc.dram_tensor("xhT", [D, NTOK], BF16, kind="ExternalInput").ap()
    A["pcall"] = nc.dram_tensor("pcall", [8, 16], F32, kind="ExternalInput").ap()
    A["oh1"] = nc.dram_tensor("oh1", [NTOK, 8], BF16, kind="ExternalInput").ap()
    A["oh2"] = nc.dram_tensor("oh2", [NTOK, 8], BF16, kind="ExternalInput").ap()
    A["g1"] = nc.dram_tensor("g1", [128, TT], F32, kind="ExternalInput").ap()
    A["wT"] = nc.dram_tensor("wT", [D, D], F32, kind="ExternalInput").ap()
    A["lA"] = nc.dram_tensor("lA", [D, 128], F32, kind="ExternalInput").ap()
    A["lB"] = nc.dram_tensor("lB", [128, D], F32, kind="ExternalInput").ap()
    A["xres"] = nc.dram_tensor("xres", [NTOK, D], F32, kind="ExternalInput").ap()
    A["xout"] = nc.dram_tensor("xout", [NTOK, D], F32, kind="ExternalOutput").ap()
    toks = emit_l1ffb(p, nc, A, "", nc.alloc_sbuf_tensor, nc.alloc_psum_tensor)
    p.finish(toks)
    p.emit()
    return nc



FUSED = False


def _run(build_fn, in_maps, out_names):
    nc = bass.Bass("TRN2", target_bir_lowering=False)
    build_fn(nc)
    nc.finalize()
    ins = set()
    for alloc in nc.m.functions[0].allocations:
        if isinstance(alloc, mybir.MemoryLocationSet) and alloc.kind == "ExternalInput":
            ins.add(alloc.memorylocations[0].name)
    in_maps = [{k: v for k, v in m.items() if k in ins} for m in in_maps]
    res = run_bass_kernel_spmd(nc, in_maps, core_ids=list(range(NCORES)))
    return [{k: r[k] for k in out_names} for r in res.results]




def emit_lnT(p, nc, asb, x_ap, xh_h, xhT_h, pfx):
    xh_sb = asb(pfx + "xh_sb", [128, TT * D], BF16)
    trb = [asb(pfx + f"trb{i}", [128, NTOK], BF16) for i in range(2)]
    toks, _, _ = emit_ln(p, nc, x_ap, xh_sb, TT, pfx + "ln", asb=asb)
    st = p.dma("sp", xh_h.rearrange("(t p) d -> p t d", p=128),
               xh_sb[:].rearrange("p (t d) -> p t d", t=TT), deps=toks,
               stream="st")
    tr_tok = [None, None]
    sts = []
    for j in range(FC):
        i = j % 2
        dd = [st] + ([tr_tok[i]] if tr_tok[i] else [])
        t_tr = p.dma("sp", trb[i][:], xh_h[:, j * 128 : (j + 1) * 128],
                     deps=dd, transpose=True, stream=f"tr{i}")
        t_st = p.dma("act", xhT_h[j * 128 : (j + 1) * 128, :], trb[i][:],
                     deps=[t_tr], stream=f"ts{i}")
        tr_tok[i] = t_st
        sts.append(t_st)
    return sts


def build_fused(nc):
    p = Prog(nc)
    EI = lambda n, shp, dt: nc.dram_tensor(n, shp, dt, kind="ExternalInput").ap()
    IN = lambda n, shp, dt: nc.dram_tensor(n, shp, dt, kind="Internal").ap()
    x = EI("x", [NTOK, D], F32)
    xout = nc.dram_tensor("xout", [NTOK, D], F32, kind="ExternalOutput").ap()
    W = {}
    for n, shp, dt in [
        ("WqdT", [12, D], F32), ("Wqk", [12, 128], F32), ("bqd", [12, 1], F32),
        ("bqk", [1, 128], F32), ("Wkvd", [D, 4], F32), ("bkv", [1, 4], F32),
        ("Wo", [D, D], F32), ("WvuT", [128, 128], F32), ("bvu", [128, FC], F32),
        ("bo", [1, D], F32), ("WstkT", [D, 24], F32), ("bases_bc", [128, 24], F32),
        ("gate", [1, 1], F32), ("Wbr", [1024, 1024], F32),
        ("wqkv", [D, 6144], F32), ("SDrep", [128, 4608], F32),
        ("scales_bc", [128, 16], F32), ("wo", [D, D], F32),
        ("gw1T", [D, 32], F32), ("gb1", [1, 32], F32),
        ("gamma_bc", [128, 32], F32), ("beta_bc", [128, 32], F32),
        ("gw2T", [32, 8], F32), ("gb2", [1, 8], F32),
        ("wT", [D, D], F32), ("lA", [D, 128], F32), ("lB", [128, D], F32),
    ]:
        W[n] = EI(n, shp, dt)
    xh_h = IN("xh_h", [NTOK, D], BF16)
    xhT_h = IN("xhT_h", [D, NTOK], BF16)
    x1_h = IN("x1_h", [NTOK, D], F32)
    x2_h = IN("x2_h", [NTOK, D], F32)
    x4_h = IN("x4_h", [NTOK, D], F32)
    qnat_h = IN("qnat_h", [NTOK, 128], BF16)
    qT_h = IN("qT_h", [128, NTOK], BF16)
    ckv_h = IN("ckv_h", [NTOK, 4], F32)
    ckv_g = IN("ckv_g", [B * S, 4], F32)
    wcomb_h = IN("wcomb_h", [128, D], BF16)
    qkvT_h = IN("qkvT_h", [6144, NTOK], BF16)
    qA2A = IN("qA2A", [S, NTOK], BF16)
    kA2A = IN("kA2A", [1024, NTOK], BF16)
    vA2A = IN("vA2A", [1024, NTOK], BF16)
    outn_s = IN("outn_s", [S, NTOK], BF16)
    ot_h = IN("ot_h", [D, NTOK], BF16)
    oh1_h = IN("oh1_h", [NTOK, 8], BF16)
    oh2_h = IN("oh2_h", [NTOK, 8], BF16)
    g1_h = IN("g1_h", [128, TT], F32)
    pc_h = IN("pc_h", [1, 16], F32)
    pcr_h = IN("pcr_h", [1, 16], F32)
    G4 = [[0, 1, 2, 3], [4, 5, 6, 7]]
    G8 = [list(range(8))]

    stack = [None]

    def asb(name, shape, dtype):
        return stack[0].enter_context(nc.sbuf_tensor(name, shape, dtype))

    def aps_(name, shape, dtype):
        return stack[0].enter_context(nc.psum_tensor(name, shape, dtype))

    def begin():
        stack[0] = contextlib.ExitStack()

    def end():
        p.barrier()
        stack[0].close()

    begin()
    emit_lnT(p, nc, asb, x, xh_h, xhT_h, "p1_")
    end()
    begin()
    A = dict(W)
    A.update(xhT=xhT_h, qnat=qnat_h, ckv=ckv_h)
    emit_l0proj(p, nc, A, "p2_", asb, aps_)
    p.barrier()
    qtb = asb("p2_qtb", [128, NTOK], BF16)
    t1 = p.dma("sp", qtb[:], qnat_h, transpose=True, stream="p2_qtr")
    p.dma("sp", qT_h[:, :], qtb[:], deps=[t1], stream="p2_qst")
    p.cc("AllGather", ALU.bypass, G8, [ckv_h], [ckv_g])
    end()
    begin()
    A = dict(W)
    A.update(qT=qT_h, ckv_g=ckv_g, xres=x, xout=x1_h, wcomb_scratch=wcomb_h)
    emit_l0core(p, nc, A, "p3_", asb, aps_)
    end()
    begin()
    emit_lnT(p, nc, asb, x1_h, xh_h, xhT_h, "p4_")
    end()
    begin()
    A = dict(W)
    A.update(xh2=xh_h, xh2T=xhT_h, xres=x1_h, xout=x2_h)
    emit_l0ffn(p, nc, A, "p5_", asb, aps_)
    end()
    begin()
    emit_lnT(p, nc, asb, x2_h, xh_h, xhT_h, "p6_")
    end()
    begin()
    A = dict(W)
    A.update(xhT=xhT_h, qkvT=qkvT_h)
    emit_l1qkv(p, nc, A, "p7_", asb, aps_)
    p.barrier()
    p.cc("AllToAll", ALU.bypass, G8, [qkvT_h[0:4096, :]], [qA2A])
    p.cc("AllToAll", ALU.bypass, G8, [qkvT_h[4096:5120, :]], [kA2A])
    p.cc("AllToAll", ALU.bypass, G8, [qkvT_h[5120:6144, :]], [vA2A])
    end()
    begin()
    A = dict(W)
    A.update(qA2A=qA2A, kA2A=kA2A, vA2A=vA2A, OUTN=outn_s)
    emit_l1att(p, nc, A, "p8_", asb, aps_)
    p.barrier()
    p.cc("AllToAll", ALU.bypass, G8, [outn_s], [ot_h])
    end()
    begin()
    A = dict(W)
    A.update(OT=ot_h, xres=x2_h, xout=x4_h)
    emit_l1wo(p, nc, A, "p9_", asb, aps_)
    end()
    begin()
    emit_lnT(p, nc, asb, x4_h, xh_h, xhT_h, "pg_")
    end()
    begin()
    A = dict(W)
    A.update(xhT=xhT_h, oh1=oh1_h, oh2=oh2_h, g1=g1_h, pc=pc_h)
    emit_l1ffa(p, nc, A, "pa_", asb, aps_)
    p.barrier()
    p.cc("AllReduce", ALU.add, G8, [pc_h], [pcr_h])
    end()
    begin()
    A = dict(W)
    A.update(xhT=xhT_h, pcr=pcr_h, oh1=oh1_h, oh2=oh2_h, g1=g1_h,
             xres=x4_h, xout=xout)
    toks = emit_l1ffb(p, nc, A, "pb_", asb, aps_)
    p.finish(toks)
    stack[0].close()
    p.emit()
    return nc




def _mk_phase_tools(nc, p):
    stack = [None]

    def asb(name, shape, dtype):
        return stack[0].enter_context(nc.sbuf_tensor(name, shape, dtype))

    def aps_(name, shape, dtype):
        return stack[0].enter_context(nc.psum_tensor(name, shape, dtype))

    def begin():
        stack[0] = contextlib.ExitStack()

    def end():
        p.barrier()
        stack[0].close()

    return asb, aps_, begin, end, stack


def _ext_weights(nc, names):
    shapes = {
        "WqdT": ([12, D], F32), "Wqk": ([12, 128], F32), "bqd": ([12, 1], F32),
        "bqk": ([1, 128], F32), "Wkvd": ([D, 4], F32), "bkv": ([1, 4], F32),
        "Wo": ([D, D], F32), "WvuT": ([128, 128], F32), "bvu": ([128, FC], F32),
        "bo": ([1, D], F32), "WstkT": ([D, 24], F32), "bases_bc": ([128, 24], F32),
        "gate": ([1, 1], F32), "Wbr": ([1024, 1024], F32),
        "wqkv": ([D, 6144], F32), "SDrep": ([128, 4608], F32),
        "scales_bc": ([128, 16], F32), "wo": ([D, D], F32),
        "gw1T": ([D, 32], F32), "gb1": ([1, 32], F32),
        "gamma_bc": ([128, 32], F32), "beta_bc": ([128, 32], F32),
        "gw2T": ([32, 8], F32), "gb2": ([1, 8], F32),
        "wT": ([D, D], F32), "lA": ([D, 128], F32), "lB": ([128, D], F32),
    }
    return {n: nc.dram_tensor(n, *shapes[n], kind="ExternalInput").ap() for n in names}


def build_seg1(nc):
    p = Prog(nc)
    x = nc.dram_tensor("x", [NTOK, D], F32, kind="ExternalInput").ap()
    qT = nc.dram_tensor("qT", [128, NTOK], BF16, kind="ExternalOutput").ap()
    ckv = nc.dram_tensor("ckv", [NTOK, 4], F32, kind="ExternalOutput").ap()
    qnat_h = nc.dram_tensor("qnat_h", [NTOK, 128], BF16, kind="Internal").ap()
    xh_h = nc.dram_tensor("xh_h", [NTOK, D], BF16, kind="Internal").ap()
    xhT_h = nc.dram_tensor("xhT_h", [D, NTOK], BF16, kind="Internal").ap()
    W = _ext_weights(nc, ["WqdT", "Wqk", "bqd", "bqk", "Wkvd", "bkv"])
    asb, aps_, begin, end, stack = _mk_phase_tools(nc, p)
    begin()
    emit_lnT(p, nc, asb, x, xh_h, xhT_h, "p1_")
    end()
    begin()
    A = dict(W)
    A.update(xhT=xhT_h, qnat=qnat_h, ckv=ckv)
    toks = emit_l0proj(p, nc, A, "p2_", asb, aps_)
    p.barrier()
    qtb = asb("p2_qtb", [128, NTOK], BF16)
    t1 = p.dma("sp", qtb[:], qnat_h, transpose=True, stream="p2qtr")
    t2 = p.dma("sp", qT[:, :], qtb[:], deps=[t1], stream="p2qst")
    p.finish(list(toks) + [t2])
    stack[0].close()
    p.emit()
    return nc


def build_seg2(nc):
    p = Prog(nc)
    x = nc.dram_tensor("x", [NTOK, D], F32, kind="ExternalInput").ap()
    qT_h = nc.dram_tensor("qT", [128, NTOK], BF16, kind="ExternalInput").ap()
    ckv_g = nc.dram_tensor("ckv_g", [B * S, 4], F32, kind="ExternalInput").ap()
    qkvT = nc.dram_tensor("qkvT", [6144, NTOK], BF16, kind="ExternalOutput").ap()
    x2o = nc.dram_tensor("x2o", [NTOK, D], F32, kind="ExternalOutput").ap()
    x1_h = nc.dram_tensor("x1_h", [NTOK, D], F32, kind="Internal").ap()
    xh_h = nc.dram_tensor("xh_h", [NTOK, D], BF16, kind="Internal").ap()
    xhT_h = nc.dram_tensor("xhT_h", [D, NTOK], BF16, kind="Internal").ap()
    wcomb_h = nc.dram_tensor("wcomb_h", [128, D], BF16, kind="Internal").ap()
    W = _ext_weights(nc, ["Wo", "WvuT", "bvu", "bo", "WstkT", "bases_bc",
                          "gate", "Wbr", "wqkv"])
    asb, aps_, begin, end, stack = _mk_phase_tools(nc, p)
    begin()
    A = dict(W)
    A.update(qT=qT_h, ckv_g=ckv_g, xres=x, xout=x1_h, wcomb_scratch=wcomb_h)
    emit_l0core(p, nc, A, "p3_", asb, aps_)
    end()
    begin()
    emit_lnT(p, nc, asb, x1_h, xh_h, xhT_h, "p4_")
    end()
    begin()
    A = dict(W)
    A.update(xh2=xh_h, xh2T=xhT_h, xres=x1_h, xout=x2o)
    t5 = emit_l0ffn(p, nc, A, "p5_", asb, aps_)
    end()
    begin()
    emit_lnT(p, nc, asb, x2o, xh_h, xhT_h, "p6_")
    end()
    begin()
    A = dict(W)
    A.update(xhT=xhT_h, qkvT=qkvT)
    t7 = emit_l1qkv(p, nc, A, "p7_", asb, aps_)
    p.finish(list(t5) + list(t7))
    stack[0].close()
    p.emit()
    return nc


def build_seg3(nc):
    p = Prog(nc)
    qA2A = nc.dram_tensor("qA2A", [S, NTOK], BF16, kind="ExternalInput").ap()
    kA2A = nc.dram_tensor("kA2A", [1024, NTOK], BF16, kind="ExternalInput").ap()
    vA2A = nc.dram_tensor("vA2A", [1024, NTOK], BF16, kind="ExternalInput").ap()
    outn = nc.dram_tensor("outn", [S, NTOK], BF16, kind="ExternalOutput").ap()
    W = _ext_weights(nc, ["SDrep", "scales_bc"])
    asb, aps_, begin, end, stack = _mk_phase_tools(nc, p)
    begin()
    A = dict(W)
    A.update(qA2A=qA2A, kA2A=kA2A, vA2A=vA2A, OUTN=outn)
    toks = emit_l1att(p, nc, A, "p8_", asb, aps_)
    p.finish(toks)
    stack[0].close()
    p.emit()
    return nc


def build_seg4(nc):
    p = Prog(nc)
    ot = nc.dram_tensor("ot", [D, NTOK], BF16, kind="ExternalInput").ap()
    x2 = nc.dram_tensor("x2", [NTOK, D], F32, kind="ExternalInput").ap()
    x4o = nc.dram_tensor("x4o", [NTOK, D], F32, kind="ExternalOutput").ap()
    xh4T = nc.dram_tensor("xh4T", [D, NTOK], BF16, kind="ExternalOutput").ap()
    oh1 = nc.dram_tensor("oh1", [NTOK, 8], BF16, kind="ExternalOutput").ap()
    oh2 = nc.dram_tensor("oh2", [NTOK, 8], BF16, kind="ExternalOutput").ap()
    g1 = nc.dram_tensor("g1", [128, TT], F32, kind="ExternalOutput").ap()
    pc = nc.dram_tensor("pc", [1, 16], F32, kind="ExternalOutput").ap()
    xh_h = nc.dram_tensor("xh_h", [NTOK, D], BF16, kind="Internal").ap()
    W = _ext_weights(nc, ["wo", "gw1T", "gb1", "gamma_bc", "beta_bc",
                          "gw2T", "gb2"])
    asb, aps_, begin, end, stack = _mk_phase_tools(nc, p)
    begin()
    A = dict(W)
    A.update(OT=ot, xres=x2, xout=x4o)
    emit_l1wo(p, nc, A, "p9_", asb, aps_)
    end()
    begin()
    emit_lnT(p, nc, asb, x4o, xh_h, xh4T, "pg_")
    end()
    begin()
    A = dict(W)
    A.update(xhT=xh4T, oh1=oh1, oh2=oh2, g1=g1, pc=pc)
    toks = emit_l1ffa(p, nc, A, "pa_", asb, aps_)
    p.finish(toks)
    stack[0].close()
    p.emit()
    return nc


def build_seg5(nc):
    p = Prog(nc)
    xh4T = nc.dram_tensor("xh4T", [D, NTOK], BF16, kind="ExternalInput").ap()
    pcall = nc.dram_tensor("pcall", [8, 16], F32, kind="ExternalInput").ap()
    oh1 = nc.dram_tensor("oh1", [NTOK, 8], BF16, kind="ExternalInput").ap()
    oh2 = nc.dram_tensor("oh2", [NTOK, 8], BF16, kind="ExternalInput").ap()
    g1 = nc.dram_tensor("g1", [128, TT], F32, kind="ExternalInput").ap()
    x4 = nc.dram_tensor("x4", [NTOK, D], F32, kind="ExternalInput").ap()
    xout = nc.dram_tensor("xout", [NTOK, D], F32, kind="ExternalOutput").ap()
    W = _ext_weights(nc, ["wT", "lA", "lB"])
    asb, aps_, begin, end, stack = _mk_phase_tools(nc, p)
    begin()
    A = dict(W)
    A.update(xhT=xh4T, pcall=pcall, oh1=oh1, oh2=oh2, g1=g1, xres=x4, xout=xout)
    toks = emit_l1ffb(p, nc, A, "pb_", asb, aps_)
    p.finish(toks)
    stack[0].close()
    p.emit()
    return nc


def _alibi_slopes(n):
    cp2 = 2 ** math.floor(math.log2(n))
    base = 2.0 ** (-8.0 / cp2)
    slopes = base ** np.arange(1, cp2 + 1, dtype=np.float32)
    if cp2 != n:
        eb = 2.0 ** (-4.0 / cp2)
        extra = eb ** np.arange(1, 1 + 2 * (n - cp2), 2, dtype=np.float32)
        slopes = np.concatenate([slopes, extra])
    return slopes.astype(np.float32)


def _sd_rep():
    pp = np.arange(128)[:, None]
    cc = np.arange(4608)[None, :]
    r = cc - 512 - pp
    return np.where(r >= 0, np.sqrt(np.maximum(r, 0).astype(np.float64)),
                    1e30).astype(np.float32)


def _kernel_unfused_obsolete(**inputs):
    asc = np.ascontiguousarray
    NH, NKV, Dh = 32, 8, 128

    x = np.asarray(inputs["L0_attn_x"], np.float32).reshape(B * S, D)
    xc = [shard_rows(x, c) for c in range(NCORES)]

    def run_ln(xcur):
        r = _run(build_ln_launch, [{"x_in": xcur[c]} for c in range(NCORES)],
                 ["xh_out"])
        xh = [r[c]["xh_out"] for c in range(NCORES)]
        xhT = [asc(h.T) for h in xh]
        return xh, xhT

    _, xh1T = run_ln(xc)
    WqdT = asc(np.asarray(inputs["L0_attn_Wq_d"], np.float32).T)
    com = {
        "WqdT": WqdT,
        "Wqk": np.asarray(inputs["L0_attn_W_qk"], np.float32),
        "bqd": asc(np.asarray(inputs["L0_attn_bq_d"], np.float32)[:, None]),
        "bqk": asc(np.asarray(inputs["L0_attn_b_qk"], np.float32)[None, :]),
        "Wkvd": np.asarray(inputs["L0_attn_Wkv_d"], np.float32),
        "bkv": asc(np.asarray(inputs["L0_attn_bkv_d"], np.float32)[None, :]),
    }
    r = _run(build_l0proj, [dict(com, xhT=xh1T[c]) for c in range(NCORES)],
             ["qnat", "ckv"])
    ckv_full = np.zeros((B * S, 4), np.float32)
    unshard_rows([r[c]["ckv"] for c in range(NCORES)], ckv_full)
    Wv_u = np.asarray(inputs["L0_attn_Wv_u"], np.float32)
    WvuT = asc(np.concatenate(
        [Wv_u[:, h * 128 : (h + 1) * 128].T for h in range(32)], axis=1))
    in3 = []
    for c in range(NCORES):
        b = c // 4
        cf = ckv_full[b * S : (b + 1) * S]
        ckvT32 = np.zeros((32, S), np.float32)
        ckvT32[0:4] = cf.T
        ckv5 = asc(np.concatenate([cf, np.ones((S, 1), np.float32)], 1))
        in3.append({
            "qT": asc(r[c]["qnat"].T),
            "ckvT32": ckvT32,
            "ckv5": ckv5,
            "Wo": np.asarray(inputs["L0_attn_Wo"], np.float32),
            "WvuT": WvuT,
            "bvu": asc(np.asarray(inputs["L0_attn_bv_u"], np.float32).reshape(32, 128).T),
            "bo": asc(np.asarray(inputs["L0_attn_bo"], np.float32)[None, :]),
            "xres": xc[c],
        })
    r = _run(build_l0core, in3, ["xout"])
    x1c = [r[c]["xout"] for c in range(NCORES)]

    xh2, xh2T = run_ln(x1c)
    WstkT = asc(np.concatenate(
        [np.asarray(inputs["L0_ffn_W_res"], np.float32),
         np.asarray(inputs["L0_ffn_W_pre"], np.float32),
         np.asarray(inputs["L0_ffn_W_post"], np.float32)], 0).T)
    bases = np.concatenate(
        [np.asarray(inputs["L0_ffn_H_res_base"], np.float32).reshape(-1),
         np.asarray(inputs["L0_ffn_H_pre_base"], np.float32),
         np.asarray(inputs["L0_ffn_H_post_base"], np.float32)]).astype(np.float32)
    bases_bc = asc(np.broadcast_to(bases[None, :], (128, 24)))
    com = {
        "WstkT": WstkT, "bases_bc": bases_bc,
        "gate": np.asarray(inputs["L0_ffn_gate"], np.float32).reshape(1, 1),
        "Wbr": np.asarray(inputs["L0_ffn_W_branch"], np.float32),
    }
    r = _run(build_l0ffn,
             [dict(com, xh2=xh2[c], xh2T=xh2T[c], xres=x1c[c])
              for c in range(NCORES)], ["xout"])
    x2c = [r[c]["xout"] for c in range(NCORES)]

    _, xh3T = run_ln(x2c)
    w_qkv = np.asarray(inputs["L1_attn_w_qkv"], np.float32)
    r = _run(build_l1qkv, [{"xhT": xh3T[c], "wqkv": w_qkv} for c in range(NCORES)],
             ["qkvT"])
    qkvTc = [r[c]["qkvT"] for c in range(NCORES)]
    qkvT_b = [np.zeros((6144, S), BF) for _ in range(B)]
    for c in range(NCORES):
        b = c // 4
        o = 0
        for (st, n) in core_rows_slices(c):
            qkvT_b[b][:, st - b * S : st - b * S + n] = qkvTc[c][:, o : o + n]
            o += n
    slopes = _alibi_slopes(NH)
    SDrep = _sd_rep()
    in_att = []
    for c in range(NCORES):
        b, s_ = c // 4, c % 4
        units = [s_ * 8 + u for u in range(8)]
        QTs = asc(qkvT_b[b][units[0] * 128 : (units[-1] + 1) * 128, :])
        kvh = [units[0] // 4, units[4] // 4]
        KTc = np.concatenate(
            [qkvT_b[b][NH * Dh + hh * Dh : NH * Dh + (hh + 1) * Dh, :] for hh in kvh], 0)
        Vc = asc(np.concatenate(
            [qkvT_b[b][NH * Dh + NKV * Dh + hh * Dh : NH * Dh + NKV * Dh + (hh + 1) * Dh, :]
             for hh in kvh], 0).T)
        scales_bc = np.zeros((128, 16), np.float32)
        for i, uu in enumerate(units):
            scales_bc[:, 2 * i] = 1.0 / (slopes[uu] * np.sqrt(Dh))
            scales_bc[:, 2 * i + 1] = slopes[uu]
        in_att.append({"QTs": QTs, "KTc": asc(KTc), "Vc": Vc, "SDrep": SDrep,
                       "scales_bc": scales_bc})
    r = _run(build_l1att, in_att, ["OUTN"])
    OUTN_b = [np.zeros((D, S), BF) for _ in range(B)]
    for c in range(NCORES):
        b, s_ = c // 4, c % 4
        OUTN_b[b][s_ * 8 * 128 : (s_ * 8 + 8) * 128, :] = r[c]["OUTN"]
    in_wo = []
    w_o = np.asarray(inputs["L1_attn_w_o"], np.float32)
    for c in range(NCORES):
        b = c // 4
        cols = []
        for (st, n) in core_rows_slices(c):
            cols.append(OUTN_b[b][:, st - b * S : st - b * S + n])
        in_wo.append({"OT": asc(np.concatenate(cols, 1)), "wo": w_o,
                      "xres": x2c[c]})
    r = _run(build_l1wo, in_wo, ["xout"])
    x4c = [r[c]["xout"] for c in range(NCORES)]

    _, xh4T = run_ln(x4c)
    gw1 = np.asarray(inputs["L1_ffn_gw1"], np.float32)
    com = {
        "gw1T": asc(gw1.T),
        "gb1": asc(np.asarray(inputs["L1_ffn_gb1"], np.float32)[None, :]),
        "gamma_bc": asc(np.broadcast_to(
            np.asarray(inputs["L1_ffn_ln_gamma"], np.float32)[None, :], (128, 32))),
        "beta_bc": asc(np.broadcast_to(
            np.asarray(inputs["L1_ffn_ln_beta"], np.float32)[None, :], (128, 32))),
        "gw2T": asc(np.asarray(inputs["L1_ffn_gw2"], np.float32).T),
        "gb2": asc(np.asarray(inputs["L1_ffn_gb2"], np.float32)[None, :]),
    }
    r = _run(build_l1ffa, [dict(com, xhT=xh4T[c]) for c in range(NCORES)],
             ["oh1", "oh2", "g1", "pc"])
    pcall = asc(np.concatenate([r[c]["pc"] for c in range(NCORES)], 0))
    lora_A = np.asarray(inputs["L1_ffn_lora_A"], np.float32)
    lora_B = np.asarray(inputs["L1_ffn_lora_B"], np.float32)
    lA_cat = asc(np.concatenate([lora_A[e] for e in range(8)], axis=1))
    lB_cat = asc(np.concatenate([lora_B[e] for e in range(8)], axis=0))
    wT = asc(np.asarray(inputs["L1_ffn_weight"], np.float32).T)
    in_b = [{"xhT": xh4T[c], "pcall": pcall, "oh1": r[c]["oh1"],
             "oh2": r[c]["oh2"], "g1": r[c]["g1"], "wT": wT, "lA": lA_cat,
             "lB": lB_cat, "xres": x4c[c]} for c in range(NCORES)]
    r = _run(build_l1ffb, in_b, ["xout"])
    out = np.zeros((B * S, D), np.float32)
    unshard_rows([r[c]["xout"] for c in range(NCORES)], out)
    return out.reshape(B, S, D)


def kernel(**inputs):
    asc = np.ascontiguousarray
    NH, Dh = 32, 128
    x = np.asarray(inputs["L0_attn_x"], np.float32).reshape(B * S, D)
    xc = [shard_rows(x, c) for c in range(NCORES)]
    slopes = _alibi_slopes(NH)
    SDrep = _sd_rep()
    Wv_u = np.asarray(inputs["L0_attn_Wv_u"], np.float32)
    W = {
        "WqdT": asc(np.asarray(inputs["L0_attn_Wq_d"], np.float32).T),
        "Wqk": np.asarray(inputs["L0_attn_W_qk"], np.float32),
        "bqd": asc(np.asarray(inputs["L0_attn_bq_d"], np.float32)[:, None]),
        "bqk": asc(np.asarray(inputs["L0_attn_b_qk"], np.float32)[None, :]),
        "Wkvd": np.asarray(inputs["L0_attn_Wkv_d"], np.float32),
        "bkv": asc(np.asarray(inputs["L0_attn_bkv_d"], np.float32)[None, :]),
        "Wo": np.asarray(inputs["L0_attn_Wo"], np.float32),
        "WvuT": asc(np.concatenate(
            [Wv_u[:, h * 128 : (h + 1) * 128].T for h in range(32)], axis=1)),
        "bvu": asc(np.asarray(inputs["L0_attn_bv_u"], np.float32).reshape(32, 128).T),
        "bo": asc(np.asarray(inputs["L0_attn_bo"], np.float32)[None, :]),
        "WstkT": asc(np.concatenate(
            [np.asarray(inputs["L0_ffn_W_res"], np.float32),
             np.asarray(inputs["L0_ffn_W_pre"], np.float32),
             np.asarray(inputs["L0_ffn_W_post"], np.float32)], 0).T),
        "bases_bc": asc(np.broadcast_to(np.concatenate(
            [np.asarray(inputs["L0_ffn_H_res_base"], np.float32).reshape(-1),
             np.asarray(inputs["L0_ffn_H_pre_base"], np.float32),
             np.asarray(inputs["L0_ffn_H_post_base"], np.float32)]
        ).astype(np.float32)[None, :], (128, 24))),
        "gate": np.asarray(inputs["L0_ffn_gate"], np.float32).reshape(1, 1),
        "Wbr": np.asarray(inputs["L0_ffn_W_branch"], np.float32),
        "wqkv": np.asarray(inputs["L1_attn_w_qkv"], np.float32),
        "SDrep": SDrep,
        "wo": np.asarray(inputs["L1_attn_w_o"], np.float32),
        "gw1T": asc(np.asarray(inputs["L1_ffn_gw1"], np.float32).T),
        "gb1": asc(np.asarray(inputs["L1_ffn_gb1"], np.float32)[None, :]),
        "gamma_bc": asc(np.broadcast_to(
            np.asarray(inputs["L1_ffn_ln_gamma"], np.float32)[None, :], (128, 32))),
        "beta_bc": asc(np.broadcast_to(
            np.asarray(inputs["L1_ffn_ln_beta"], np.float32)[None, :], (128, 32))),
        "gw2T": asc(np.asarray(inputs["L1_ffn_gw2"], np.float32).T),
        "gb2": asc(np.asarray(inputs["L1_ffn_gb2"], np.float32)[None, :]),
        "wT": asc(np.asarray(inputs["L1_ffn_weight"], np.float32).T),
        "lA": asc(np.concatenate(
            [np.asarray(inputs["L1_ffn_lora_A"], np.float32)[e] for e in range(8)],
            axis=1)),
        "lB": asc(np.concatenate(
            [np.asarray(inputs["L1_ffn_lora_B"], np.float32)[e] for e in range(8)],
            axis=0)),
    }
    scs = []
    for c in range(NCORES):
        sc = np.zeros((128, 16), np.float32)
        for u in range(8):
            g = u % 4
            sl = slopes[c * 4 + g]
            sc[:, 2 * u] = 1.0 / (sl * np.sqrt(Dh))
            sc[:, 2 * u + 1] = sl
        scs.append(sc)
    if FUSED:
        in_maps = [dict(W, x=xc[c], scales_bc=scs[c]) for c in range(NCORES)]
        nc = bass.Bass("TRN2", target_bir_lowering=False)
        build_fused(nc)
        nc.finalize()
        res = run_bass_kernel_spmd(nc, in_maps, core_ids=list(range(NCORES)))
        out = np.zeros((B * S, D), np.float32)
        unshard_rows([res.results[c]["xout"] for c in range(NCORES)], out)
        return out.reshape(B, S, D)

    r1 = _run(build_seg1, [dict(W, x=xc[c]) for c in range(NCORES)],
              ["qT", "ckv"])
    ckv_g = asc(np.concatenate([r1[c]["ckv"] for c in range(NCORES)], 0))
    r2 = _run(build_seg2,
              [dict(W, x=xc[c], qT=r1[c]["qT"], ckv_g=ckv_g)
               for c in range(NCORES)],
              ["qkvT", "x2o"])
    qkvTs = [r2[c]["qkvT"] for c in range(NCORES)]
    in3 = []
    for c in range(NCORES):
        qA2A = asc(np.concatenate(
            [qkvTs[r][c * 512 : (c + 1) * 512, :] for r in range(NCORES)], 0))
        kA2A = asc(np.concatenate(
            [qkvTs[r][4096 + c * 128 : 4096 + (c + 1) * 128, :] for r in range(NCORES)], 0))
        vA2A = asc(np.concatenate(
            [qkvTs[r][5120 + c * 128 : 5120 + (c + 1) * 128, :] for r in range(NCORES)], 0))
        in3.append(dict(W, qA2A=qA2A, kA2A=kA2A, vA2A=vA2A, scales_bc=scs[c]))
    r3 = _run(build_seg3, in3, ["outn"])
    in4 = []
    for c in range(NCORES):
        ot = asc(np.concatenate(
            [r3[r]["outn"][c * 512 : (c + 1) * 512, :] for r in range(NCORES)], 0))
        in4.append(dict(W, ot=ot, x2=r2[c]["x2o"]))
    r4 = _run(build_seg4, in4, ["x4o", "xh4T", "oh1", "oh2", "g1", "pc"])
    pcall = asc(np.concatenate([r4[c]["pc"] for c in range(NCORES)], 0))
    in5 = [dict(W, xh4T=r4[c]["xh4T"], pcall=pcall, oh1=r4[c]["oh1"],
                oh2=r4[c]["oh2"], g1=r4[c]["g1"], x4=r4[c]["x4o"])
           for c in range(NCORES)]
    r5 = _run(build_seg5, in5, ["xout"])
    out = np.zeros((B * S, D), np.float32)
    unshard_rows([r5[c]["xout"] for c in range(NCORES)], out)
    return out.reshape(B, S, D)
